# version 03
# baseline (speedup 1.0000x reference)
from contextlib import ExitStack
"""Raw-Bass TRN2 kernel for mk_recurrent_d2 (depth-2: multi-branch causal
attention + gated linear FFN, then Mamba2 SSD + soft-MoE FFN).

Layout convention: FEATURE-MAJOR on chip — activations live as
[features on partitions, tokens on free dim]; host pre-transposes inputs /
weights (pure layout) and post-transposes outputs.

Sharding: 8 cores; cores 0-3 own batch 0, cores 4-7 batch 1; within a batch
each core owns a contiguous 2048-token slice (launches A/C/E/G), attention
(launch B) is sharded by balanced (head, q-tile) slots, SSD (launch D) by
(batch, head).

v0 is a multi-launch chain with host resharding between launches (pure
numpy layout ops). Collectives / remote-DMA are broken in this container
(fail at NEFF load), so cross-core exchanges go through launch boundaries.
"""

import math
import numpy as np
import ml_dtypes
import concourse.bass as bass
import concourse.mybir as mybir
from concourse.bass_utils import run_bass_kernel_spmd

F32 = mybir.dt.float32
BF16 = mybir.dt.bfloat16
AF = mybir.ActivationFunctionType
ALU = mybir.AluOpType
AX = mybir.AxisListType.X
BF = ml_dtypes.bfloat16

B_, T_, C_ = 2, 8192, 1024
H, HD = 16, 64
NHYP0, NHYP1 = 6, 10
NCORE = 8
TPC = 2048
LN_EPS = 1e-5
NEG_BIG = -1e30
LN8 = math.log(8.0)
LNI8 = math.log(0.125)
CPOOL = {LN_EPS: 0, 1e-30: 1, 1e-24: 2, LN8: 3, LNI8: 4}

FORM_OFF = []
FORM_W = []
_o = 0
for _h in range(H):
    _w = 65 if NHYP0 <= _h < NHYP1 else 64
    FORM_OFF.append(_o)
    FORM_W.append(_w)
    _o += _w
FORM_ROWS = _o


def is_hyp(h):
    return NHYP0 <= h < NHYP1


def hyp_idx(h):
    return h - NHYP0


def rotary_tables():
    inv_freq = (1.0 / 10000.0 ** (np.arange(0, HD, 2, dtype=np.float32) / HD)
                ).astype(np.float32)
    t = np.arange(T_, dtype=np.float32)
    freqs = np.outer(t, inv_freq).astype(np.float32)
    return np.cos(freqs).T.copy(), np.sin(freqs).T.copy()


def pooled_select():
    m = np.zeros((C_, 64), dtype=np.float32)
    for d in range(C_):
        m[d, d % 64] = 1.0
    return m


class Sem:
    def __init__(self, handle):
        self.h = handle
        self.n = 0

    def inc(self, instr, k=1):
        instr.then_inc(self.h, k)
        self.n += k
        return self.n


def mk_identity(g, ident):
    g.memset(ident, 1.0)
    g.affine_select(out=ident, in_=ident, compare_op=ALU.is_equal, fill=0.0,
                    base=0, pattern=[[-1, ident.shape[1]]], channel_multiplier=1)


def barrier_maker(engines, sem):
    state = {"k": 0}

    def bar():
        state["k"] += 1
        tgt = state["k"] * len(engines)
        for e in engines:
            e.nop().then_inc(sem.h, 1)
        for e in engines:
            e.wait_ge(sem.h, tgt)
        sem.n = tgt
    return bar


A_STOP = 99


def build_A():
    nc = bass.Bass("TRN2", target_bir_lowering=False, debug=False)
    xT = nc.dram_tensor("xT", [C_, TPC], F32, kind="ExternalInput").ap()
    qkvwT = nc.dram_tensor("qkvwT", [C_, 3 * C_], F32, kind="ExternalInput").ap()
    ffnwT = nc.dram_tensor("ffnwT", [C_, C_], F32, kind="ExternalInput").ap()
    disha = nc.dram_tensor("disha", [64, C_], F32, kind="ExternalInput").ap()
    psel = nc.dram_tensor("psel", [C_, 64], F32, kind="ExternalInput").ap()
    trigC = nc.dram_tensor("trigC", [128, TPC], BF16, kind="ExternalInput").ap()
    trigS = nc.dram_tensor("trigS", [128, TPC], BF16, kind="ExternalInput").ap()
    curv = nc.dram_tensor("curv", [1, 4], F32, kind="ExternalInput").ap()

    kform = nc.dram_tensor("kform", [FORM_ROWS, TPC], BF16, kind="ExternalOutput").ap()
    qform = nc.dram_tensor("qform", [FORM_ROWS, TPC], BF16, kind="ExternalOutput").ap()
    vtok = nc.dram_tensor("vtok", [TPC, C_], BF16, kind="ExternalOutput").ap()
    qnrm = nc.dram_tensor("qnrm", [12, TPC], F32, kind="ExternalOutput").ap()
    knrm = nc.dram_tensor("knrm", [12, TPC], F32, kind="ExternalOutput").ap()
    g0 = nc.dram_tensor("g0", [C_, TPC], F32, kind="ExternalOutput").ap()

    NT = TPC // 512

    NT = TPC // 512

    with ExitStack() as _st, nc.Block() as blk:
        xn_ = _st.enter_context(nc.sbuf_tensor("xn", [128, 8 * TPC], BF16))
        qh_ = _st.enter_context(nc.sbuf_tensor("qhat", [128, 8 * TPC], BF16))
        kh_ = _st.enter_context(nc.sbuf_tensor("khat", [128, 8 * TPC], BF16))
        xstg = _st.enter_context(nc.sbuf_tensor("xstg", [128, 2 * TPC], F32))
        sqstg = _st.enter_context(nc.sbuf_tensor("sqstg", [128, 2 * 512], F32))
        w32 = _st.enter_context(nc.sbuf_tensor("w32", [128, 2 * 512], F32))
        wbf = _st.enter_context(nc.sbuf_tensor("wbf", [128, 8 * 512], BF16))
        stg = _st.enter_context(nc.sbuf_tensor("stg", [128, 2 * 512], F32))
        stgb = _st.enter_context(nc.sbuf_tensor("stgb", [128, 4 * 512], BF16))
        sqb = _st.enter_context(nc.sbuf_tensor("sqb", [128, TPC], BF16))
        trig_c = _st.enter_context(nc.sbuf_tensor("trig_c", [128, TPC], BF16))
        trig_s = _st.enter_context(nc.sbuf_tensor("trig_s", [128, TPC], BF16))
        qswb = _st.enter_context(nc.sbuf_tensor("qsw", [128, TPC], BF16))
        formstg = _st.enter_context(nc.sbuf_tensor("formstg", [128, TPC], BF16))
        pooledS = _st.enter_context(nc.sbuf_tensor("pooledS", [64, TPC], BF16))
        selA = _st.enter_context(nc.sbuf_tensor("selA", [1, 128], BF16))
        selB1 = _st.enter_context(nc.sbuf_tensor("selB1", [1, 128], BF16))
        ones128b = _st.enter_context(nc.sbuf_tensor("ones128b", [128, 1], BF16))
        curv_t = _st.enter_context(nc.sbuf_tensor("curv_t", [1, 4], F32))
        invc_t = _st.enter_context(nc.sbuf_tensor("invc_t", [1, 4], F32))
        onesc = _st.enter_context(nc.sbuf_tensor("onesc", [128, 1], F32))
        ones1 = _st.enter_context(nc.sbuf_tensor("ones1", [1, 128], F32))
        onesc_b = _st.enter_context(nc.sbuf_tensor("onesc_b", [64, 1], BF16))
        identb = _st.enter_context(nc.sbuf_tensor("identb", [128, 128], BF16))
        rows_ = _st.enter_context(nc.sbuf_tensor("rows", [1, 5 * TPC], F32))
        cols_ = _st.enter_context(nc.sbuf_tensor("cols", [128, 32], F32))
        cpool = _st.enter_context(nc.sbuf_tensor("cpool", [128, 8], F32))
        _s_ld = _st.enter_context(nc.semaphore("s_ld"))
        _s_tv = _st.enter_context(nc.semaphore("s_tv"))
        _s_vt = _st.enter_context(nc.semaphore("s_vt"))
        _s_sc = _st.enter_context(nc.semaphore("s_sc"))
        _s_gp = _st.enter_context(nc.semaphore("s_gp"))
        _s_out = _st.enter_context(nc.semaphore("s_out"))
        _s_bar = _st.enter_context(nc.semaphore("s_bar"))
        s_ld, s_tv, s_vt = Sem(_s_ld), Sem(_s_tv), Sem(_s_vt)
        s_sc, s_gp, s_out = Sem(_s_sc), Sem(_s_gp), Sem(_s_out)
        s_bar = Sem(_s_bar)
        ENGS = ("sync", "tensor", "vector", "scalar", "gpsimd")

        def sec(name, f):
            getattr(blk, name)(f)

        def bar():
            for e in ENGS:
                sec(e, lambda en: en.nop().then_inc(_s_bar, 1))
            s_bar.n += 5
            tgt = s_bar.n
            for e in ENGS:
                sec(e, lambda en, tgt=tgt: en.wait_ge(_s_bar, tgt))

        def _finish(nc):
            return nc

        def xn(d):
            return xn_[:, d * TPC:(d + 1) * TPC]

        def qh(d):
            return qh_[:, d * TPC:(d + 1) * TPC]

        def kh(d):
            return kh_[:, d * TPC:(d + 1) * TPC]

        def row(i):
            return rows_[:, i * TPC:(i + 1) * TPC]

        def col(i):
            return cols_[:, i:i + 1]

        def cbias(v, parts=1):
            return cpool[0:parts, CPOOL[v]:CPOOL[v] + 1]

        s_ch = Sem(_s_out)

        class _Rec:

            def __init__(self, e):
                self._e = e
                self.last = None

            def __getattr__(self, k):
                a = getattr(self._e, k)
                if callable(a):
                    def w(*args, **kw):
                        r = a(*args, **kw)
                        if hasattr(r, "then_inc"):
                            self.last = r
                        return r
                    return w
                return a

        def step(eng, fn):
            cur = s_ch.n

            def _f(e, fn=fn, cur=cur):
                e.wait_ge(_s_out, cur)
                rec = _Rec(e)
                fn(rec)
                (rec.last if rec.last is not None else e.nop()).then_inc(_s_out, 1)
            sec(eng, _f)
            s_ch.n += 1

        def dstep(eng, fn, n=1):
            cur = s_ch.n

            def _f(e, fn=fn, cur=cur):
                e.wait_ge(_s_out, cur)
                fn(e, lambda ins: ins.then_inc(_s_out, 16))
            sec(eng, _f)
            s_ch.n += 16 * n

        def _s0(s, mark):
            mark(s.dma_start(out=trig_c[:], in_=trigC[:]))
            mark(s.dma_start(out=trig_s[:], in_=trigS[:]))
            mark(s.dma_start(out=curv_t[:], in_=curv[:]))
        dstep("sync", _s0, 3)

        def _g0a(g):
            g.memset(identb[:], 1.0)
            g.memset(onesc[:], 1.0)
            g.memset(ones1[:], 1.0)
            g.memset(onesc_b[:], 1.0)
            g.memset(ones128b[:], 1.0)
            g.memset(selA[:, 0:64], 1.0)
            g.memset(selA[:, 64:128], 0.0)
            g.memset(selB1[:, 0:64], 0.0)
            g.memset(selB1[:, 64:128], 1.0)
            for _v, _i in CPOOL.items():
                g.memset(cpool[:, _i:_i + 1], _v)
        step("gpsimd", _g0a)

        def _g0b(g):
            g.affine_select(out=identb[:], in_=identb[:], compare_op=ALU.is_equal,
                            fill=0.0, base=0, pattern=[[-1, 128]],
                            channel_multiplier=1)
        step("gpsimd", _g0b)

        _cm_psa = nc.psum_tensor("ps_a", [1, TPC], F32)
        _cm_psb = nc.psum_tensor("ps_b", [1, TPC], F32)
        ps_a = _cm_psa.__enter__()
        ps_b = _cm_psb.__enter__()
        if True:
            for d in range(8):
                sl = xstg[:, (d % 2) * TPC:(d % 2 + 1) * TPC]

                def _ld(s, mark, d=d, sl=sl):
                    mark(s.dma_start(out=sl, in_=xT[d * 128:(d + 1) * 128, :]))
                dstep("sync", _ld, 1)

                def _sq(v, sl=sl):
                    v.tensor_mul(sqstg[:, 0:512], sl[:, 0:512], sl[:, 0:512])
                    v.tensor_mul(sqstg[:, 512:1024], sl[:, 512:1024], sl[:, 512:1024])
                step("vector", _sq)

                def _mm1(t, d=d, sl=sl):
                    for j in range(2):
                        t.matmul(ps_a[:, j * 512:(j + 1) * 512], onesc[:],
                                 sl[:, j * 512:(j + 1) * 512],
                                 start=(d == 0), stop=(d == 7))
                        t.matmul(ps_b[:, j * 512:(j + 1) * 512], onesc[:],
                                 sqstg[:, j * 512:(j + 1) * 512],
                                 start=(d == 0), stop=(d == 7))
                step("tensor", _mm1)

                def _sq2(v, sl=sl):
                    v.tensor_mul(sqstg[:, 0:512], sl[:, 1024:1536], sl[:, 1024:1536])
                    v.tensor_mul(sqstg[:, 512:1024], sl[:, 1536:2048], sl[:, 1536:2048])
                step("vector", _sq2)

                def _mm2(t, d=d, sl=sl):
                    for j in range(2):
                        t.matmul(ps_a[:, (2 + j) * 512:(3 + j) * 512], onesc[:],
                                 sl[:, (2 + j) * 512:(3 + j) * 512],
                                 start=(d == 0), stop=(d == 7))
                        t.matmul(ps_b[:, (2 + j) * 512:(3 + j) * 512], onesc[:],
                                 sqstg[:, j * 512:(j + 1) * 512],
                                 start=(d == 0), stop=(d == 7))
                step("tensor", _mm2)

            step("vector", lambda v: v.tensor_scalar_mul(row(0), ps_a[:], 1.0 / C_))
            step("vector", lambda v: v.tensor_mul(row(3), row(0), row(0)))
            step("vector", lambda v: v.tensor_scalar_mul(row(4), ps_b[:], 1.0 / C_))
            step("vector", lambda v: v.tensor_sub(row(4), row(4), row(3)))
            step("scalar", lambda sc: sc.activation(out=row(3), in_=row(4),
                                                    func=AF.Ln, bias=cbias(LN_EPS)))
            step("scalar", lambda sc: sc.activation(out=row(1), in_=row(3),
                                                    func=AF.Exp, scale=-0.5))
            step("vector", lambda v: v.scalar_tensor_tensor(
                out=row(2), in0=row(0), scalar=-1.0, in1=row(1),
                op0=ALU.mult, op1=ALU.mult))
        _cm_psb.__exit__(None, None, None)
        _cm_psa.__exit__(None, None, None)
        if A_STOP <= 1:
            return _finish(nc)

        _cm_pbc = nc.psum_tensor("pbc", [128, 4 * 512], F32)
        pbc = _cm_pbc.__enter__()
        if True:
            for j in range(NT):
                b = 2 * (j % 2)

                def _bc(t, j=j, b=b):
                    t.matmul(pbc[:, b * 512:(b + 1) * 512], ones1[:],
                             row(1)[:, j * 512:(j + 1) * 512], start=True, stop=True)
                    t.matmul(pbc[:, (b + 1) * 512:(b + 2) * 512], ones1[:],
                             row(2)[:, j * 512:(j + 1) * 512], start=True, stop=True)
                step("tensor", _bc)

                for d in range(8):
                    sl = xstg[:, (d % 2) * 512:(d % 2 + 1) * 512]

                    def _ld(s, mark, d=d, j=j, sl=sl):
                        mark(s.dma_start(out=sl, in_=xT[d * 128:(d + 1) * 128,
                                                        j * 512:(j + 1) * 512]))
                    dstep("sync", _ld, 1)

                    def _ap1(v, sl=sl, b=b):
                        v.tensor_mul(stg[:, 0:512], sl, pbc[:, b * 512:(b + 1) * 512])
                    step("vector", _ap1)

                    def _ap2(v, d=d, j=j, b=b):
                        v.tensor_add(xn(d)[:, j * 512:(j + 1) * 512], stg[:, 0:512],
                                     pbc[:, (b + 1) * 512:(b + 2) * 512])
                    step("vector", _ap2)
        _cm_pbc.__exit__(None, None, None)
        if A_STOP <= 2:
            return _finish(nc)

        _cm_pbig = nc.psum_tensor("pbig", [128, 4 * 512], F32)
        _cm_ptr = nc.psum_tensor("ptr", [128, 4 * 128], BF16)
        pbig = _cm_pbig.__enter__()
        ptr = _cm_ptr.__enter__()
        if True:
            for cb in range(6):
                for d in range(8):
                    def _ld1(s, mark, cb=cb, d=d):
                        mark(s.dma_start(
                            out=w32[:, (d % 2) * 512:(d % 2 + 1) * 512],
                            in_=qkvwT[d * 128:(d + 1) * 128,
                                      cb * 512:(cb + 1) * 512]))
                    dstep("sync", _ld1, 1)

                    def _cst(v, cb=cb, d=d):
                        v.tensor_copy(wbf[:, d * 512:(d + 1) * 512],
                                      w32[:, (d % 2) * 512:(d % 2 + 1) * 512])
                    step("vector", _cst)

                for c4 in range(4):
                    R0 = cb * 512 + c4 * 128
                    for j in range(NT):
                        slot = (c4 * NT + j) % 4
                        ps = pbig[:, slot * 512:(slot + 1) * 512]

                        def _mm(t, c4=c4, j=j, ps=ps):
                            for d in range(8):
                                t.matmul(ps, wbf[:, d * 512 + c4 * 128:
                                                 d * 512 + c4 * 128 + 128],
                                         xn(d)[:, j * 512:(j + 1) * 512],
                                         start=(d == 0), stop=(d == 7))
                        step("tensor", _mm)

                        if R0 < 2048:
                            dst = qh_ if R0 < 1024 else kh_
                            dq = (R0 % 1024) // 128

                            def _ev(v, dst=dst, dq=dq, j=j, ps=ps):
                                v.tensor_copy(
                                    dst[:, dq * TPC + j * 512:
                                        dq * TPC + (j + 1) * 512], ps)
                            step("vector", _ev)
                        else:
                            f0 = R0 - 2048

                            def _evv(v, ps=ps):
                                v.tensor_copy(stgb[:, 0:512], ps)
                            step("vector", _evv)

                            def _tr(t):
                                for i in range(4):
                                    t.transpose(ptr[:, i * 128:(i + 1) * 128],
                                                stgb[:, i * 128:(i + 1) * 128],
                                                identb[:])
                            step("tensor", _tr)

                            def _evt2(v):
                                for i in range(4):
                                    v.tensor_copy(
                                        stgb[:, 512 + i * 128:512 + (i + 1) * 128],
                                        ptr[:, i * 128:(i + 1) * 128])
                            step("vector", _evt2)

                            def _od(g, mark, f0=f0, j=j):
                                for i in range(4):
                                    mark(g.dma_start(
                                        out=vtok[j * 512 + i * 128:
                                                 j * 512 + (i + 1) * 128,
                                                 f0:f0 + 128],
                                        in_=stgb[:, 512 + i * 128:
                                                 512 + (i + 1) * 128]))
                            dstep("gpsimd", _od, 4)
        bar()
        if A_STOP <= 3:
            return _finish(nc)

        for which, buf in (("q", qh_), ("k", kh_)):
            for dq in range(8):
                ch = buf[:, dq * TPC:(dq + 1) * TPC]

                def _sw(s, mark, ch=ch):
                    mark(s.dma_start(out=qswb[0:32, :], in_=ch[32:64, :]))
                    mark(s.dma_start(out=qswb[32:64, :], in_=ch[0:32, :]))
                    mark(s.dma_start(out=qswb[64:96, :], in_=ch[96:128, :]))
                    mark(s.dma_start(out=qswb[96:128, :], in_=ch[64:96, :]))
                dstep("sync", _sw, 4)

                def _rp1(v, ch=ch):
                    v.tensor_mul(sqb[:], qswb[:], trig_s[:])
                    v.tensor_mul(stgb[:, 0:TPC], ch, trig_c[:])
                step("vector", _rp1)

                def _rp2(v, ch=ch):
                    v.tensor_add(ch, stgb[:, 0:TPC], sqb[:])
                step("vector", _rp2)
        bar()
        if A_STOP <= 4:
            return _finish(nc)


        step("scalar", lambda sc: sc.activation(out=invc_t[:], in_=curv_t[:],
                                                func=AF.Ln))
        step("scalar", lambda sc: sc.activation(out=invc_t[:], in_=invc_t[:],
                                                func=AF.Exp, scale=-1.0))

        _cm_ptr.__exit__(None, None, None)
        _cm_pbig.__exit__(None, None, None)
        _cm_psnr = nc.psum_tensor("psnr", [1, TPC], F32)
        _cm_pbcf = nc.psum_tensor("pbcf", [128, 512], F32)
        _cm_pcol = nc.psum_tensor("pcol", [128, 1], F32)
        psnr = _cm_psnr.__enter__()
        pbcf = _cm_pbcf.__enter__()
        pcol = _cm_pcol.__enter__()
        if True:
            for p in range(8):
                h0, h1 = 2 * p, 2 * p + 1
                qch = qh_[:, p * TPC:(p + 1) * TPC]
                kch = kh_[:, p * TPC:(p + 1) * TPC]
                for which, ch, r_even, r_odd in (
                        ("k", kch, 3, 4), ("q", qch, 0, 1)):
                    def _sq(v, ch=ch):
                        v.tensor_mul(sqb[:], ch, ch)
                    step("vector", _sq)
                    for m, rr in ((0, r_even), (1, r_odd)):
                        bpar = 64 * m

                        def _nm(t, bpar=bpar):
                            for j in range(NT):
                                t.matmul(psnr[:, j * 512:(j + 1) * 512],
                                         ones128b[bpar:bpar + 64, :],
                                         sqb[bpar:bpar + 64, j * 512:(j + 1) * 512],
                                         start=True, stop=True)
                        step("tensor", _nm)

                        def _ev(v, rr=rr):
                            v.tensor_copy(row(rr), psnr[:])
                        step("vector", _ev)

                if p < 3:
                    def _kf(v, kch=kch):
                        v.tensor_scalar_mul(formstg[:], kch, 0.125)
                    step("vector", _kf)

                    def _okf(g, mark, h0=h0, h1=h1):
                        mark(g.dma_start(out=kform[FORM_OFF[h0]:FORM_OFF[h0] + 64, :],
                                         in_=formstg[0:64, :]))
                        mark(g.dma_start(out=kform[FORM_OFF[h1]:FORM_OFF[h1] + 64, :],
                                         in_=formstg[64:128, :]))
                    dstep("gpsimd", _okf, 2)

                    def _oqf(g, mark, qch=qch, h0=h0, h1=h1):
                        mark(g.dma_start(out=qform[FORM_OFF[h0]:FORM_OFF[h0] + 64, :],
                                         in_=qch[0:64, :]))
                        mark(g.dma_start(out=qform[FORM_OFF[h1]:FORM_OFF[h1] + 64, :],
                                         in_=qch[64:128, :]))
                    dstep("gpsimd", _oqf, 2)

                    for rr, bias2, outt, oidx in (
                            (3, LNI8, knrm, h0), (4, LNI8, knrm, h1),
                            (0, 0.0, qnrm, h0), (1, 0.0, qnrm, h1)):
                        def _nra(sc, rr=rr):
                            sc.activation(out=row(2), in_=row(rr), func=AF.Ln,
                                          bias=cbias(1e-30))
                        step("scalar", _nra)

                        def _nrb(sc, bias2=bias2):
                            sc.activation(out=row(2), in_=row(2), func=AF.Exp,
                                          scale=0.5, bias=cbias(bias2) if bias2 else 0.0)
                        step("scalar", _nrb)

                        def _onr(g, mark, outt=outt, oidx=oidx):
                            mark(g.dma_start(out=outt[oidx:oidx + 1, :],
                                             in_=row(2)))
                        dstep("gpsimd", _onr, 1)

                elif p < 5:
                    i0 = h0 - 6

                    def _nga(v, i0=i0):
                        v.tensor_scalar_mul(stg[0:1, 0:2], curv_t[:, i0:i0 + 2], -1.0)
                    step("vector", _nga)
                    step("vector", lambda v: v.tensor_copy(stgb[0:1, 0:2],
                                                           stg[0:1, 0:2]))

                    def _bc(t, i0=i0):
                        t.matmul(pcol[:], selA[:], stgb[0:1, 0:1], start=True,
                                 stop=False)
                        t.matmul(pcol[:], selB1[:], stgb[0:1, 1:2], start=False,
                                 stop=True)
                    step("tensor", _bc)

                    def _evc(v):
                        v.tensor_copy(cols_[:, 0:1], pcol[:])
                    step("vector", _evc)

                    def _kf(v, kch=kch):
                        v.tensor_scalar(out=formstg[:], in0=kch,
                                        scalar1=cols_[:, 0:1], scalar2=None,
                                        op0=ALU.mult)
                    step("vector", _kf)

                    def _okf(g, mark, h0=h0, h1=h1):
                        mark(g.dma_start(out=kform[FORM_OFF[h0]:FORM_OFF[h0] + 64, :],
                                         in_=formstg[0:64, :]))
                        mark(g.dma_start(out=kform[FORM_OFF[h1]:FORM_OFF[h1] + 64, :],
                                         in_=formstg[64:128, :]))
                    dstep("gpsimd", _okf, 2)

                    def _oqf(g, mark, qch=qch, h0=h0, h1=h1):
                        mark(g.dma_start(out=qform[FORM_OFF[h0]:FORM_OFF[h0] + 64, :],
                                         in_=qch[0:64, :]))
                        mark(g.dma_start(out=qform[FORM_OFF[h1]:FORM_OFF[h1] + 64, :],
                                         in_=qch[64:128, :]))
                    dstep("gpsimd", _oqf, 2)

                    for rr, ci, outt, hh, scale_c in (
                            (3, i0, kform, h0, True), (4, i0 + 1, kform, h1, True),
                            (0, i0, qform, h0, False), (1, i0 + 1, qform, h1, False)):
                        def _tka(sc, rr=rr, ci=ci):
                            sc.activation(out=row(2), in_=row(rr), func=AF.Ln,
                                          bias=invc_t[:, ci:ci + 1])
                        step("scalar", _tka)
                        step("scalar", lambda sc: sc.activation(
                            out=row(2), in_=row(2), func=AF.Exp, scale=0.5))

                        if scale_c:
                            def _sc_c(v, ci=ci):
                                v.tensor_scalar(out=stgb[0:1, 0:TPC], in0=row(2),
                                                scalar1=curv_t[:, ci:ci + 1],
                                                scalar2=None, op0=ALU.mult)
                            step("vector", _sc_c)
                        else:
                            def _sc_c(v):
                                v.tensor_copy(stgb[0:1, 0:TPC], row(2))
                            step("vector", _sc_c)

                        def _otk(g, mark, outt=outt, hh=hh):
                            mark(g.dma_start(
                                out=outt[FORM_OFF[hh] + 64:FORM_OFF[hh] + 65, :],
                                in_=stgb[0:1, 0:TPC]))
                        dstep("gpsimd", _otk, 1)

                else:
                    for which, ch, r_even, r_odd, outt in (
                            ("k", kch, 3, 4, kform), ("q", qch, 0, 1, qform)):
                        bias2 = LN8 if which == "k" else 0.0

                        def _rna(sc, r_even=r_even):
                            sc.activation(out=row(2), in_=row(r_even), func=AF.Ln,
                                          bias=cbias(1e-24))
                        step("scalar", _rna)

                        def _rnb(sc, bias2=bias2):
                            sc.activation(out=row(2), in_=row(2), func=AF.Exp,
                                          scale=-0.5, bias=cbias(bias2) if bias2 else 0.0)
                        step("scalar", _rnb)

                        def _cv1(v):
                            v.tensor_copy(stgb[0:1, 0:TPC], row(2))
                        step("vector", _cv1)

                        def _rn2a(sc, r_odd=r_odd):
                            sc.activation(out=row(2), in_=row(r_odd), func=AF.Ln,
                                          bias=cbias(1e-24))
                        step("scalar", _rn2a)

                        def _rn2b(sc, bias2=bias2):
                            sc.activation(out=row(2), in_=row(2), func=AF.Exp,
                                          scale=-0.5, bias=cbias(bias2) if bias2 else 0.0)
                        step("scalar", _rn2b)

                        def _cv2(v):
                            v.tensor_copy(sqb[0:1, 0:TPC], row(2))
                        step("vector", _cv2)

                        for j in range(NT):
                            def _bc(t, j=j):
                                t.matmul(pbcf[:], selA[:],
                                         stgb[0:1, j * 512:(j + 1) * 512],
                                         start=True, stop=False)
                                t.matmul(pbcf[:], selB1[:],
                                         sqb[0:1, j * 512:(j + 1) * 512],
                                         start=False, stop=True)
                            step("tensor", _bc)

                            def _mlt(v, ch=ch, j=j):
                                v.tensor_mul(formstg[:, j * 512:(j + 1) * 512],
                                             ch[:, j * 512:(j + 1) * 512], pbcf[:])
                            step("vector", _mlt)

                        def _of(g, mark, outt=outt, h0=h0, h1=h1):
                            mark(g.dma_start(out=outt[FORM_OFF[h0]:FORM_OFF[h0] + 64, :],
                                             in_=formstg[0:64, :]))
                            mark(g.dma_start(out=outt[FORM_OFF[h1]:FORM_OFF[h1] + 64, :],
                                             in_=formstg[64:128, :]))
                        dstep("gpsimd", _of, 2)

                    def _c8(g, h0=h0, h1=h1):
                        g.memset(row(2), 8.0)
                    step("gpsimd", _c8)

                    def _oc8(g, mark, h0=h0, h1=h1):
                        mark(g.dma_start(out=knrm[h0 - 4:h0 - 3, :], in_=row(2)))
                        mark(g.dma_start(out=knrm[h1 - 4:h1 - 3, :], in_=row(2)))
                    dstep("gpsimd", _oc8, 2)

                    def _c1(g):
                        g.memset(row(2), 1.0)
                    step("gpsimd", _c1)

                    def _oc1(g, mark, h0=h0, h1=h1):
                        mark(g.dma_start(out=qnrm[h0 - 4:h0 - 3, :], in_=row(2)))
                        mark(g.dma_start(out=qnrm[h1 - 4:h1 - 3, :], in_=row(2)))
                    dstep("gpsimd", _oc1, 2)
        bar()
        if A_STOP <= 5:
            return _finish(nc)

        _cm_pcol.__exit__(None, None, None)
        _cm_pbcf.__exit__(None, None, None)
        _cm_psnr.__exit__(None, None, None)
        _cm_pbig2 = nc.psum_tensor("pbig2", [128, 4 * 512], F32)
        _cm_ppool = nc.psum_tensor("ppool", [64, 512], F32)
        pbig2 = _cm_pbig2.__enter__()
        ppool = _cm_ppool.__enter__()
        if True:
            for d in range(8):
                def _lp(s, mark, d=d):
                    mark(s.dma_start(out=w32[:, 0:64],
                                     in_=psel[d * 128:(d + 1) * 128, :]))
                dstep("sync", _lp, 1)

                def _cp(v, d=d):
                    v.tensor_copy(wbf[:, d * 512:d * 512 + 64], w32[:, 0:64])
                step("vector", _cp)

            for j in range(NT):
                def _pm(t, j=j):
                    for d in range(8):
                        t.matmul(ppool[:], wbf[:, d * 512:d * 512 + 64],
                                 xn(d)[:, j * 512:(j + 1) * 512],
                                 start=(d == 0), stop=(d == 7))
                step("tensor", _pm)

                def _pe(v, j=j):
                    v.tensor_copy(pooledS[:, j * 512:(j + 1) * 512], ppool[:])
                step("vector", _pe)

            def _ldd(s, mark):
                mark(s.dma_start(out=stg[0:64, 0:1024], in_=disha[:]))
            dstep("sync", _ldd, 1)

            def _cdd(v):
                v.tensor_copy(formstg[0:64, 0:1024], stg[0:64, 0:1024])
            step("vector", _cdd)

            for cb in range(2):
                for d in range(8):
                    def _lw(s, mark, cb=cb, d=d):
                        mark(s.dma_start(
                            out=w32[:, (d % 2) * 512:(d % 2 + 1) * 512],
                            in_=ffnwT[d * 128:(d + 1) * 128,
                                      cb * 512:(cb + 1) * 512]))
                    dstep("sync", _lw, 1)

                    def _cw(v, d=d):
                        v.tensor_copy(wbf[:, d * 512:(d + 1) * 512],
                                      w32[:, (d % 2) * 512:(d % 2 + 1) * 512])
                    step("vector", _cw)

                for c4 in range(4):
                    R0 = cb * 512 + c4 * 128
                    for j in range(NT):
                        slot = (c4 * NT + j) % 4
                        ps = pbig2[:, slot * 512:(slot + 1) * 512]

                        def _mm(t, c4=c4, j=j, ps=ps, R0=R0):
                            for d in range(8):
                                t.matmul(ps, wbf[:, d * 512 + c4 * 128:
                                                 d * 512 + c4 * 128 + 128],
                                         xn(d)[:, j * 512:(j + 1) * 512],
                                         start=(d == 0), stop=False)
                            t.matmul(ps, formstg[0:64, R0:R0 + 128],
                                     pooledS[:, j * 512:(j + 1) * 512],
                                     start=False, stop=True)
                        step("tensor", _mm)

                        def _sl(sc, ps=ps):
                            sc.activation(out=stg[:, 0:512], in_=ps, func=AF.Silu)
                        step("scalar", _sl)

                        def _og(g, mark, R0=R0, j=j):
                            mark(g.dma_start(out=g0[R0:R0 + 128,
                                                    j * 512:(j + 1) * 512],
                                             in_=stg[:, 0:512]))
                        dstep("gpsimd", _og, 1)

        fin = s_ch.n

        def _fin(g, fin=fin):
            g.wait_ge(_s_out, fin)
        sec("gpsimd", _fin)

    return nc


def host_prep_A(inputs):
    xT = np.ascontiguousarray(
        inputs["L0_attn_x"].reshape(B_ * T_, C_).T)
    qkvwT = np.ascontiguousarray(inputs["L0_attn_qkv_w"].T)
    ffnwT = np.ascontiguousarray(inputs["L0_ffn_weight"].T)
    disha = np.ascontiguousarray(inputs["L0_ffn_disha"])
    psel = pooled_select()
    cos32, sin32 = rotary_tables()
    tc = np.empty((128, T_), np.float32)
    ts = np.empty((128, T_), np.float32)
    for hh in range(2):
        b = hh * 64
        tc[b:b + 32] = cos32
        tc[b + 32:b + 64] = cos32
        ts[b:b + 32] = sin32
        ts[b + 32:b + 64] = -sin32
    curv = inputs["L0_attn_curvature"].reshape(1, 4).astype(np.float32)
    maps = []
    for c in range(NCORE):
        t0 = c * TPC
        tl = t0 % T_
        maps.append({
            "xT": np.ascontiguousarray(xT[:, t0:t0 + TPC]),
            "qkvwT": qkvwT, "ffnwT": ffnwT, "disha": disha, "psel": psel,
            "trigC": np.ascontiguousarray(tc[:, tl:tl + TPC]).astype(BF),
            "trigS": np.ascontiguousarray(ts[:, tl:tl + TPC]).astype(BF),
            "curv": curv,
        })
    return maps


PLAIN_HEADS = list(range(6)) + list(range(10, 16))
HYP_HEADS = list(range(6, 10))
NQT = T_ // 512


def build_B():
    nc = bass.Bass("TRN2", target_bir_lowering=False, debug=False)
    kp = nc.dram_tensor("kp", [3, 64, T_], BF16, kind="ExternalInput").ap()
    qp = nc.dram_tensor("qp", [3, 64, T_], BF16, kind="ExternalInput").ap()
    vp = nc.dram_tensor("vp", [3, 128, 64 * 65], BF16, kind="ExternalInput").ap()
    knr = nc.dram_tensor("knr", [3, T_], F32, kind="ExternalInput").ap()
    qnr = nc.dram_tensor("qnr", [3, T_], F32, kind="ExternalInput").ap()
    khy = nc.dram_tensor("khy", [65, T_], BF16, kind="ExternalInput").ap()
    qhy = nc.dram_tensor("qhy", [65, T_], BF16, kind="ExternalInput").ap()
    vhy = nc.dram_tensor("vhy", [128, 64 * 65], BF16, kind="ExternalInput").ap()
    hcur = nc.dram_tensor("hcur", [1, 1], F32, kind="ExternalInput").ap()
    yp = nc.dram_tensor("yp", [3, 64, T_], BF16, kind="ExternalOutput").ap()
    yh = nc.dram_tensor("yh", [64, T_], BF16, kind="ExternalOutput").ap()
    mscr = nc.dram_tensor("mscr", [128, 1], F32).ap()

    with ExitStack() as _st, nc.Block() as blk:
        ku = _st.enter_context(nc.sbuf_tensor("ku", [64, T_], BF16))
        qu = _st.enter_context(nc.sbuf_tensor("qu", [64, T_], BF16))
        vu = _st.enter_context(nc.sbuf_tensor("vu", [128, 64 * 65], BF16))
        khu = _st.enter_context(nc.sbuf_tensor("khu", [65, T_], BF16))
        qhu = _st.enter_context(nc.sbuf_tensor("qhu", [65, T_], BF16))
        qhn = _st.enter_context(nc.sbuf_tensor("qhn", [65, T_], BF16))
        knru = _st.enter_context(nc.sbuf_tensor("knru", [1, T_], F32))
        qnru = _st.enter_context(nc.sbuf_tensor("qnru", [1, T_], F32))
        scf = _st.enter_context(nc.sbuf_tensor("scf", [128, 2048], F32))
        scg = _st.enter_context(nc.sbuf_tensor("scg", [128, 2048], F32))
        sch = _st.enter_context(nc.sbuf_tensor("sch", [128, 2048], F32))
        pb2 = _st.enter_context(nc.sbuf_tensor("pb2", [128, 2 * 1024], BF16))
        pbw = _st.enter_context(nc.sbuf_tensor("pbw", [128, 2048], BF16))
        yf = _st.enter_context(nc.sbuf_tensor("yf", [64, 512], F32))
        yo = _st.enter_context(nc.sbuf_tensor("yo", [64, 512], BF16))
        mcol = _st.enter_context(nc.sbuf_tensor("mcol", [128, 8], F32))
        mrow = _st.enter_context(nc.sbuf_tensor("mrow", [1, 128], F32))
        rowf = _st.enter_context(nc.sbuf_tensor("rowf", [1, 512], F32))
        rowb = _st.enter_context(nc.sbuf_tensor("rowb", [1, 512], BF16))
        sca = _st.enter_context(nc.sbuf_tensor("sca", [1, 8], F32))
        acol = _st.enter_context(nc.sbuf_tensor("acol", [128, 1], F32))
        ones1 = _st.enter_context(nc.sbuf_tensor("ones1", [1, 128], BF16))
        cpool = _st.enter_context(nc.sbuf_tensor("cpool", [128, 8], F32))
        psS = _st.enter_context(nc.psum_tensor("psS", [128, 2048], F32))
        psO = _st.enter_context(nc.psum_tensor("psO", [65, 512], F32))
        psB = _st.enter_context(nc.psum_tensor("psB", [128, 512], F32))
        psC = _st.enter_context(nc.psum_tensor("psC", [128, 1], F32))
        _s_out = _st.enter_context(nc.semaphore("s_out"))
        _s_qk = _st.enter_context(nc.semaphore("s_qk"))
        _s_pr = _st.enter_context(nc.semaphore("s_pr"))
        _s_pv = _st.enter_context(nc.semaphore("s_pv"))

        CP = {LN_EPS: 0, -1.0: 1, 1e-6: 2}

        def cbias(v, parts=1):
            return cpool[0:parts, CP[v]:CP[v] + 1]

        def sec(name, f):
            getattr(blk, name)(f)

        s_ch = Sem(_s_out)
        qk_n = [0]
        pr_n = [0]
        pv_n = [0]

        class _Rec:
            def __init__(self, e):
                self._e = e
                self.last = None

            def __getattr__(self, k):
                a = getattr(self._e, k)
                if callable(a):
                    def w(*args, **kw):
                        r = a(*args, **kw)
                        if hasattr(r, "then_inc"):
                            self.last = r
                        return r
                    return w
                return a

        def step(eng, fn):
            cur = s_ch.n

            def _f(e, fn=fn, cur=cur):
                e.wait_ge(_s_out, cur)
                rec = _Rec(e)
                fn(rec)
                (rec.last if rec.last is not None else e.nop()).then_inc(_s_out, 1)
            sec(eng, _f)
            s_ch.n += 1

        def dstep(eng, fn, n=1):
            cur = s_ch.n

            def _f(e, fn=fn, cur=cur):
                e.wait_ge(_s_out, cur)
                fn(e, lambda ins: ins.then_inc(_s_out, 16))
            sec(eng, _f)
            s_ch.n += 16 * n

        FR = {}

        def _c0(g):
            g.memset(ones1[:], 1.0)
            for _v, _i in CP.items():
                g.memset(cpool[:, _i:_i + 1], _v)
            FR["f"] = g.alloc_register("fillneg")
            g.reg_mov(FR["f"], NEG_BIG)
        step("gpsimd", _c0)

        def _lc(s, mark):
            mark(s.dma_start(out=sca[:, 0:1], in_=hcur[:]))
        dstep("sync", _lc, 1)
        step("scalar", lambda sc: sc.activation(out=sca[:, 1:2], in_=sca[:, 0:1],
                                                func=AF.Ln))
        step("scalar", lambda sc: sc.activation(out=sca[:, 1:2], in_=sca[:, 1:2],
                                                func=AF.Exp, scale=-0.5))
        step("vector", lambda v: v.tensor_copy(rowb[:, 0:1], sca[:, 1:2]))
        step("tensor", lambda t: t.matmul(psC[:], ones1[:], rowb[:, 0:1],
                                          start=True, stop=True))
        step("vector", lambda v: v.tensor_copy(acol[:], psC[:]))

        for u in range(3):
            def _lu(s, mark, u=u):
                mark(s.dma_start(out=ku[:], in_=kp[u]))
                mark(s.dma_start(out=qu[:], in_=qp[u]))
                mark(s.dma_start(out=vu[:], in_=vp[u]))
                mark(s.dma_start(out=knru[:], in_=knr[u:u + 1, :]))
                mark(s.dma_start(out=qnru[:], in_=qnr[u:u + 1, :]))
            dstep("sync", _lu, 5)

            for t in range(NQT):
                q0 = t * 512
                bound = 4 * (t + 1)
                step("vector", lambda v, t=t: v.reduce_max(
                    sca[:, 2:3], qnru[:, t * 512:(t + 1) * 512], axis=AX))
                step("vector", lambda v, bound=bound: v.reduce_max(
                    sca[:, 3:4], knru[:, 0:bound * 128], axis=AX))
                step("vector", lambda v: v.tensor_mul(sca[:, 4:5], sca[:, 2:3],
                                                      sca[:, 3:4]))
                step("vector", lambda v: v.tensor_scalar_mul(sca[:, 5:6],
                                                             sca[:, 4:5], -1.0))
                step("vector", lambda v: v.tensor_copy(rowb[:, 0:1], sca[:, 5:6]))
                step("tensor", lambda t_: t_.matmul(psC[:], ones1[:], rowb[:, 0:1],
                                                    start=True, stop=True))
                step("vector", lambda v: v.tensor_copy(mcol[:, 0:1], psC[:]))
                ch_gate = s_ch.n

                NG = 2 * (t + 1)
                npipe = NG - 2

                def qk_group(g, q0=q0):
                    par = g % 2
                    qn0 = qk_n[0]
                    prn = pr_n[0]

                    def _f(t_, g=g, par=par, q0=q0, qn0=qn0):
                        t_.wait_ge(_s_out, ch_gate)
                        if g >= 2:
                            t_.wait_ge(_s_pr, prn)
                        for i in range(2):
                            kc = g * 2 + i
                            ins = t_.matmul(psS[:, par * 1024 + i * 512:
                                                par * 1024 + (i + 1) * 512],
                                            ku[:, kc * 128:(kc + 1) * 128],
                                            qu[:, q0:q0 + 512],
                                            start=True, stop=True)
                        ins.then_inc(_s_qk, 1)
                    sec("tensor", _f)
                    qk_n[0] += 1

                def exp_group(g):
                    par = g % 2
                    qn = qk_n[0]
                    pvn = pv_n[0]

                    def _f(sc, g=g, par=par, qn=qn, pvn=pvn):
                        sc.wait_ge(_s_qk, qn)
                        if g >= 2:
                            sc.wait_ge(_s_pv, pvn - 1)
                        sc.activation(out=pb2[:, par * 1024:(par + 1) * 1024],
                                      in_=psS[:, par * 1024:(par + 1) * 1024],
                                      func=AF.Exp, bias=mcol[:, 0:1]
                                      ).then_inc(_s_pr, 1)
                    sec("scalar", _f)
                    pr_n[0] += 1

                def pv_group(g, t=t, first=False):
                    par = g % 2
                    prn = pr_n[0]

                    def _f(t_, g=g, par=par, prn=prn, first=first):
                        t_.wait_ge(_s_pr, prn)
                        for i in range(2):
                            kc = g * 2 + i
                            ins = t_.matmul(psO[:],
                                            vu[:, kc * 65:(kc + 1) * 65],
                                            pb2[:, par * 1024 + i * 512:
                                                par * 1024 + (i + 1) * 512],
                                            start=(first and i == 0), stop=False,
                                            skip_group_check=True)
                        ins.then_inc(_s_pv, 1)
                    sec("tensor", _f)
                    pv_n[0] += 1

                if npipe > 0:
                    qk_group(0)
                    for g in range(npipe):
                        if g + 1 < npipe:
                            qk_group(g + 1)
                        exp_group(g)
                        pv_group(g, first=(g == 0))
                pvn_done = pv_n[0]
                prn_done = pr_n[0]

                def _sync_pipe(t_, pvn_done=pvn_done):
                    t_.wait_ge(_s_pv, pvn_done)
                step("tensor", _sync_pipe)

                for kc in range(bound - 4, bound):
                    i4 = kc - (bound - 4)

                    def _qk(t_, kc=kc, q0=q0, i4=i4):
                        t_.matmul(psS[:, i4 * 512:(i4 + 1) * 512],
                                  ku[:, kc * 128:(kc + 1) * 128],
                                  qu[:, q0:q0 + 512], start=True, stop=True)
                    step("tensor", _qk)
                step("vector", lambda v: v.tensor_copy(scf[:], psS[:]))

                def _msk(g, q0=q0, bound=bound):
                    for i4 in range(4):
                        kc = bound - 4 + i4
                        g.affine_select(out=scf[:, i4 * 512:(i4 + 1) * 512],
                                        in_=scf[:, i4 * 512:(i4 + 1) * 512],
                                        compare_op=ALU.is_ge, fill=FR["f"],
                                        base=q0 - kc * 128, pattern=[[1, 512]],
                                        channel_multiplier=-1)
                step("gpsimd", _msk)
                step("scalar", lambda sc: sc.activation(
                    out=pbw[:], in_=scf[:], func=AF.Exp, bias=mcol[:, 0:1]))

                def _pvd(t_, bound=bound, t=t):
                    for i4 in range(4):
                        kc = bound - 4 + i4
                        t_.matmul(psO[:], vu[:, kc * 65:(kc + 1) * 65],
                                  pbw[:, i4 * 512:(i4 + 1) * 512],
                                  start=(t == 0 and i4 == 0 and True and False)
                                  or (bound == 4 and i4 == 0),
                                  stop=(i4 == 3), skip_group_check=True)
                step("tensor", _pvd)

                step("scalar", lambda sc: sc.activation(out=rowf[:],
                                                        in_=psO[64:65, :],
                                                        func=AF.Ln))
                step("scalar", lambda sc: sc.activation(out=rowf[:], in_=rowf[:],
                                                        func=AF.Exp, scale=-1.0))
                step("vector", lambda v: v.tensor_copy(rowb[:], rowf[:]))
                step("tensor", lambda t_: t_.matmul(psB[:], ones1[:], rowb[:],
                                                    start=True, stop=True))
                step("vector", lambda v: v.tensor_copy(yf[:], psO[0:64, :]))
                step("vector", lambda v: v.tensor_mul(yo[:], yf[:], psB[0:64, :]))

                def _oy(g, mark, u=u, q0=q0):
                    mark(g.dma_start(out=yp[u, :, q0:q0 + 512], in_=yo[:]))
                dstep("gpsimd", _oy, 1)

        def _lh(s, mark):
            mark(s.dma_start(out=khu[:], in_=khy[:]))
            mark(s.dma_start(out=qhu[:], in_=qhy[:]))
            mark(s.dma_start(out=vu[:], in_=vhy[:]))
        dstep("sync", _lh, 3)
        step("vector", lambda v: v.tensor_scalar_mul(qhn[:], qhu[:], -1.0))

        for t in range(NQT):
            q0 = t * 512
            bound = 4 * (t + 1)
            for qc in range(4):
                first = True
                for st0 in range(0, t + 1, 2):
                    ns = min(2, t + 1 - st0)

                    def _umm(t_, qc=qc, st0=st0, ns=ns, q0=q0):
                        for k in range(ns):
                            st = st0 + k
                            t_.matmul(psS[:, k * 512:(k + 1) * 512],
                                      qhn[:, q0 + qc * 128:q0 + (qc + 1) * 128],
                                      khu[:, st * 512:(st + 1) * 512],
                                      start=True, stop=True)
                    step("tensor", _umm)
                    has_diag = (st0 + ns - 1 == t)
                    if has_diag:
                        k = ns - 1
                        step("vector", lambda v, k=k: v.tensor_copy(
                            scf[:, 0:512], psS[:, k * 512:(k + 1) * 512]))

                        def _mA(g, q0=q0, qc=qc, t=t):
                            g.affine_select(out=scf[:, 0:512], in_=scf[:, 0:512],
                                            compare_op=ALU.is_ge, fill=FR["f"],
                                            base=q0 + qc * 128 - t * 512,
                                            pattern=[[-1, 512]],
                                            channel_multiplier=1)
                        step("gpsimd", _mA)
                        if ns == 2:
                            step("vector", lambda v: v.reduce_max(
                                mcol[:, 4:5], psS[:, 0:512], axis=AX))
                            step("vector", lambda v: v.reduce_max(
                                mcol[:, 5:6], scf[:, 0:512], axis=AX))
                            step("vector", lambda v: v.tensor_max(
                                mcol[:, 4:5], mcol[:, 4:5], mcol[:, 5:6]))
                        else:
                            step("vector", lambda v: v.reduce_max(
                                mcol[:, 4:5], scf[:, 0:512], axis=AX))
                    else:
                        if ns == 2:
                            step("vector", lambda v: v.reduce_max(
                                mcol[:, 4:5], psS[:, 0:1024]
                                .rearrange("p (a b) -> p (a b)")
                                if False else psS[:, 0:1024], axis=AX))
                        else:
                            step("vector", lambda v: v.reduce_max(
                                mcol[:, 4:5], psS[:, 0:512], axis=AX))
                    if first:
                        step("vector", lambda v, qc=qc: v.tensor_copy(
                            mcol[:, qc:qc + 1], mcol[:, 4:5]))
                        first = False
                    else:
                        step("vector", lambda v, qc=qc: v.tensor_max(
                            mcol[:, qc:qc + 1], mcol[:, qc:qc + 1], mcol[:, 4:5]))
            step("vector", lambda v: v.reduce_max(mcol[:, 5:6], mcol[:, 0:4],
                                                  axis=AX))

            def _ob(g, mark):
                mark(g.dma_start(out=mscr[:], in_=mcol[:, 5:6]))
            dstep("gpsimd", _ob, 1)

            def _ib(g, mark):
                mark(g.dma_start(out=mrow[:],
                                 in_=mscr.rearrange("p one -> (one p)")[None, :]))
            dstep("gpsimd", _ib, 1)
            step("vector", lambda v: v.reduce_max(sca[:, 2:3], mrow[:], axis=AX))
            step("vector", lambda v: v.tensor_scalar(
                out=sca[:, 3:4], in0=sca[:, 2:3], scalar1=-1.0,
                scalar2=1.0 + 1e-7, op0=ALU.mult, op1=ALU.max))
            step("vector", lambda v: v.tensor_mul(sca[:, 4:5], sca[:, 3:4],
                                                  sca[:, 3:4]))
            step("scalar", lambda sc: sc.activation(out=sca[:, 4:5], in_=sca[:, 4:5],
                                                    func=AF.Ln, bias=cbias(-1.0)))
            step("scalar", lambda sc: sc.activation(out=sca[:, 4:5], in_=sca[:, 4:5],
                                                    func=AF.Exp, scale=0.5))
            step("vector", lambda v: v.tensor_add(sca[:, 5:6], sca[:, 3:4],
                                                  sca[:, 4:5]))
            step("scalar", lambda sc: sc.activation(out=sca[:, 5:6], in_=sca[:, 5:6],
                                                    func=AF.Ln))
            step("scalar", lambda sc: sc.activation(out=sca[:, 5:6], in_=sca[:, 5:6],
                                                    func=AF.Ln, bias=cbias(1e-6),
                                                    scale=sca[:, 1:2]))
            step("scalar", lambda sc: sc.activation(out=sca[:, 6:7], in_=sca[:, 5:6],
                                                    func=AF.Exp, scale=-1.0))
            step("vector", lambda v: v.tensor_scalar_mul(sca[:, 6:7], sca[:, 6:7],
                                                         -1.0))
            step("vector", lambda v: v.tensor_copy(rowb[:, 0:1], sca[:, 6:7]))
            step("tensor", lambda t_: t_.matmul(psC[:], ones1[:], rowb[:, 0:1],
                                                start=True, stop=True))
            step("vector", lambda v: v.tensor_copy(mcol[:, 6:7], psC[:]))

            for g4 in range(t + 1):
                kc0 = g4 * 4
                wid = 2048

                def _xm(t_, kc0=kc0, q0=q0):
                    for i in range(4):
                        kc = kc0 + i
                        t_.matmul(psS[:, i * 512:(i + 1) * 512],
                                  khu[:, kc * 128:(kc + 1) * 128],
                                  qhu[:, q0:q0 + 512], start=True, stop=True)
                step("tensor", _xm)
                step("vector", lambda v: v.tensor_scalar_max(scf[:], psS[:],
                                                             1.0 + 1e-7))
                step("vector", lambda v: v.tensor_mul(scg[:], scf[:], scf[:]))
                step("scalar", lambda sc: sc.activation(out=sch[:], in_=scg[:],
                                                        func=AF.Ln,
                                                        bias=cbias(-1.0, 128)))
                step("scalar", lambda sc: sc.activation(out=scg[:], in_=sch[:],
                                                        func=AF.Exp, scale=0.5))
                step("vector", lambda v: v.tensor_add(sch[:], scf[:], scg[:]))
                step("scalar", lambda sc: sc.activation(out=scg[:], in_=sch[:],
                                                        func=AF.Ln))
                step("scalar", lambda sc: sc.activation(out=sch[:], in_=scg[:],
                                                        func=AF.Ln,
                                                        bias=cbias(1e-6, 128),
                                                        scale=acol[:]))
                step("scalar", lambda sc: sc.activation(out=scg[:], in_=sch[:],
                                                        func=AF.Exp, scale=-1.0))
                if g4 == t:
                    def _mk2(g, q0=q0, kc0=kc0):
                        for i in range(4):
                            kc = kc0 + i
                            g.affine_select(out=scg[:, i * 512:(i + 1) * 512],
                                            in_=scg[:, i * 512:(i + 1) * 512],
                                            compare_op=ALU.is_ge, fill=FR["f"],
                                            base=q0 - kc * 128,
                                            pattern=[[1, 512]],
                                            channel_multiplier=-1)
                    step("gpsimd", _mk2)
                step("scalar", lambda sc: sc.activation(out=pbw[:], in_=scg[:],
                                                        func=AF.Exp,
                                                        bias=mcol[:, 6:7]))

                def _pv(t_, kc0=kc0, g4=g4, t=t):
                    for i in range(4):
                        kc = kc0 + i
                        t_.matmul(psO[:], vu[:, kc * 65:(kc + 1) * 65],
                                  pbw[:, i * 512:(i + 1) * 512],
                                  start=(kc == 0), stop=(g4 == t and i == 3),
                                  skip_group_check=True)
                step("tensor", _pv)

            step("scalar", lambda sc: sc.activation(out=rowf[:], in_=psO[64:65, :],
                                                    func=AF.Ln))
            step("scalar", lambda sc: sc.activation(out=rowf[:], in_=rowf[:],
                                                    func=AF.Exp, scale=-1.0))
            step("vector", lambda v: v.tensor_copy(rowb[:], rowf[:]))
            step("tensor", lambda t_: t_.matmul(psB[:], ones1[:], rowb[:],
                                                start=True, stop=True))
            step("vector", lambda v: v.tensor_copy(yf[:], psO[0:64, :]))
            step("vector", lambda v: v.tensor_mul(yo[:], yf[:], psB[0:64, :]))

            def _oy(g, mark, q0=q0):
                mark(g.dma_start(out=yh[:, q0:q0 + 512], in_=yo[:]))
            dstep("gpsimd", _oy, 1)

        fin = s_ch.n

        def _fin(g, fin=fin):
            g.wait_ge(_s_out, fin)
        sec("gpsimd", _fin)
    return nc


def host_prep_B(inputs, A_out):
    kform = {}
    qform = {}
    vtokf = {}
    knrm = {}
    qnrm = {}
    for b in range(2):
        cores = range(b * 4, b * 4 + 4)
        kform[b] = np.concatenate([A_out[c]["kform"] for c in cores], axis=1)
        qform[b] = np.concatenate([A_out[c]["qform"] for c in cores], axis=1)
        vtokf[b] = np.concatenate([A_out[c]["vtok"] for c in cores], axis=0)
        knrm[b] = np.concatenate([A_out[c]["knrm"] for c in cores], axis=1)
        qnrm[b] = np.concatenate([A_out[c]["qnrm"] for c in cores], axis=1)
    plain_units = [(b, h) for b in range(2) for h in PLAIN_HEADS]
    hyp_units = [(b, h) for b in range(2) for h in HYP_HEADS]
    PIDX = {h: i for i, h in enumerate(PLAIN_HEADS)}
    onescol = np.ones((T_, 1), dtype=BF)

    def vchunks(b, h):
        v = vtokf[b][:, h * 64:(h + 1) * 64]
        ve = np.concatenate([v, onescol], axis=1)
        return np.ascontiguousarray(
            ve.reshape(64, 128, 65).transpose(1, 0, 2).reshape(128, 64 * 65))

    maps = []
    for c in range(NCORE):
        us = plain_units[3 * c:3 * c + 3]
        hb, hh = hyp_units[c]
        m = {
            "kp": np.stack([kform[b][FORM_OFF[h]:FORM_OFF[h] + 64] for b, h in us]),
            "qp": np.stack([qform[b][FORM_OFF[h]:FORM_OFF[h] + 64] for b, h in us]),
            "vp": np.stack([vchunks(b, h) for b, h in us]),
            "knr": np.stack([knrm[b][PIDX[h]] for b, h in us]),
            "qnr": np.stack([qnrm[b][PIDX[h]] for b, h in us]),
            "khy": np.ascontiguousarray(kform[hb][FORM_OFF[hh]:FORM_OFF[hh] + 65]),
            "qhy": np.ascontiguousarray(qform[hb][FORM_OFF[hh]:FORM_OFF[hh] + 65]),
            "vhy": vchunks(hb, hh),
            "hcur": inputs["L0_attn_curvature"].reshape(4)[hh - 6].reshape(1, 1)
                    .astype(np.float32),
        }
        maps.append(m)
    return maps


def reassemble_B(B_out):
    y = np.zeros((2, H, 64, T_), dtype=BF)
    plain_units = [(b, h) for b in range(2) for h in PLAIN_HEADS]
    hyp_units = [(b, h) for b in range(2) for h in HYP_HEADS]
    for c in range(NCORE):
        for i, (b, h) in enumerate(plain_units[3 * c:3 * c + 3]):
            y[b, h] = B_out[c]["yp"][i]
        hb, hh = hyp_units[c]
        y[hb, hh] = B_out[c]["yh"]
    return y


WID = 2560
NTW = WID // 512
DIN = 2048
CONVD = 2304


def build_C():
    nc = bass.Bass("TRN2", target_bir_lowering=False, debug=False)
    yT = nc.dram_tensor("yT", [C_, WID], BF16, kind="ExternalInput").ap()
    g0T = nc.dram_tensor("g0T", [C_, WID], F32, kind="ExternalInput").ap()
    xT = nc.dram_tensor("xT", [C_, WID], F32, kind="ExternalInput").ap()
    projwT = nc.dram_tensor("projwT", [C_, C_], F32, kind="ExternalInput").ap()
    projb = nc.dram_tensor("projb", [C_, 1], F32, kind="ExternalInput").ap()
    winT = nc.dram_tensor("winT", [C_, 4384], F32, kind="ExternalInput").ap()
    convw = nc.dram_tensor("convw", [CONVD, 4], F32, kind="ExternalInput").ap()
    convb = nc.dram_tensor("convb", [CONVD, 1], F32, kind="ExternalInput").ap()
    dtb = nc.dram_tensor("dtb", [32, 1], F32, kind="ExternalInput").ap()
    alog = nc.dram_tensor("alog", [32, 1], F32, kind="ExternalInput").ap()

    zo = nc.dram_tensor("zo", [DIN, TPC], BF16, kind="ExternalOutput").ap()
    xbco = nc.dram_tensor("xbco", [CONVD, TPC], BF16, kind="ExternalOutput").ap()
    dtpo = nc.dram_tensor("dtpo", [32, TPC], F32, kind="ExternalOutput").ap()
    adto = nc.dram_tensor("adto", [32, TPC], F32, kind="ExternalOutput").ap()
    x1o = nc.dram_tensor("x1o", [C_, WID], F32, kind="ExternalOutput").ap()
    xn1o = nc.dram_tensor("xn1o", [C_, TPC], BF16, kind="ExternalOutput").ap()

    with ExitStack() as _st, nc.Block() as blk:
        ybuf = _st.enter_context(nc.sbuf_tensor("ybuf", [128, 8 * WID], BF16))
        xn1b = _st.enter_context(nc.sbuf_tensor("xn1b", [128, 8 * WID], BF16))
        xbcb = _st.enter_context(nc.sbuf_tensor("xbcb", [128, 18 * 515], BF16))
        dtr = _st.enter_context(nc.sbuf_tensor("dtr", [32, WID], F32))
        w32 = _st.enter_context(nc.sbuf_tensor("w32", [128, 8 * 512], F32))
        wbf = _st.enter_context(nc.sbuf_tensor("wbf", [128, 8 * 512], BF16))
        stg = _st.enter_context(nc.sbuf_tensor("stg", [128, 2 * WID + 3 * 512], F32))
        stgb = _st.enter_context(nc.sbuf_tensor("stgb", [128, TPC], BF16))
        rows_ = _st.enter_context(nc.sbuf_tensor("rows", [1, 4 * WID], F32))
        cols_ = _st.enter_context(nc.sbuf_tensor("cols", [128, 16], F32))
        cw4 = _st.enter_context(nc.sbuf_tensor("cw4", [128, 5], F32))
        ones1 = _st.enter_context(nc.sbuf_tensor("ones1", [1, 128], F32))
        onesc = _st.enter_context(nc.sbuf_tensor("onesc", [128, 1], F32))
        cpool = _st.enter_context(nc.sbuf_tensor("cpool", [128, 8], F32))
        _s_out = _st.enter_context(nc.semaphore("s_out"))

        CP = {LN_EPS: 0, 1.0: 1}

        def cbias(v, parts=1):
            return cpool[0:parts, CP[v]:CP[v] + 1]

        def sec(name, f):
            getattr(blk, name)(f)

        s_ch = Sem(_s_out)

        class _Rec:
            def __init__(self, e):
                self._e = e
                self.last = None

            def __getattr__(self, k):
                a = getattr(self._e, k)
                if callable(a):
                    def w(*args, **kw):
                        r = a(*args, **kw)
                        if hasattr(r, "then_inc"):
                            self.last = r
                        return r
                    return w
                return a

        def step(eng, fn):
            cur = s_ch.n

            def _f(e, fn=fn, cur=cur):
                e.wait_ge(_s_out, cur)
                rec = _Rec(e)
                fn(rec)
                (rec.last if rec.last is not None else e.nop()).then_inc(_s_out, 1)
            sec(eng, _f)
            s_ch.n += 1

        def dstep(eng, fn, n=1):
            cur = s_ch.n

            def _f(e, fn=fn, cur=cur):
                e.wait_ge(_s_out, cur)
                fn(e, lambda ins: ins.then_inc(_s_out, 16))
            sec(eng, _f)
            s_ch.n += 16 * n

        def row(i):
            return rows_[:, i * WID:(i + 1) * WID]

        def _c0(g):
            g.memset(ones1[:], 1.0)
            g.memset(onesc[:], 1.0)
            g.memset(rows_[:], 0.0)
            for _v, _i in CP.items():
                g.memset(cpool[:, _i:_i + 1], _v)
        step("gpsimd", _c0)

        def _ly(s, mark):
            for d in range(8):
                mark(s.dma_start(out=ybuf[:, d * WID:(d + 1) * WID],
                                 in_=yT[d * 128:(d + 1) * 128, :]))
        dstep("sync", _ly, 8)

        _cm_p1 = nc.psum_tensor("pmm", [128, 4 * 512], F32)
        _cm_p2 = nc.psum_tensor("pst", [1, 2 * 512], F32)
        pmm = _cm_p1.__enter__()
        pst = _cm_p2.__enter__()
        if True:
            for cb in range(2):
                def _lw(s, mark, cb=cb):
                    for d in range(8):
                        mark(s.dma_start(
                            out=w32[:, d * 512:(d + 1) * 512],
                            in_=projwT[d * 128:(d + 1) * 128,
                                       cb * 512:(cb + 1) * 512]))
                dstep("sync", _lw, 8)
                step("vector", lambda v: v.tensor_copy(wbf[:], w32[:]))
                for c4 in range(4):
                    R0 = cb * 512 + c4 * 128
                    G0 = 0
                    X0 = WID
                    S0 = 2 * WID

                    def _lb(s, mark, R0=R0):
                        mark(s.dma_start(out=cols_[:, 0:1],
                                         in_=projb[R0:R0 + 128, :]))
                        mark(s.dma_start(out=stg[:, G0:G0 + WID],
                                         in_=g0T[R0:R0 + 128, :]))
                        mark(s.dma_start(out=stg[:, X0:X0 + WID],
                                         in_=xT[R0:R0 + 128, :]))
                    dstep("sync", _lb, 3)
                    for j in range(NTW):
                        ps = pmm[:, (j % 4) * 512:(j % 4 + 1) * 512]
                        gj = stg[:, G0 + j * 512:G0 + (j + 1) * 512]
                        xj = stg[:, X0 + j * 512:X0 + (j + 1) * 512]

                        def _mm(t, c4=c4, j=j, ps=ps):
                            for d in range(8):
                                t.matmul(ps, wbf[:, d * 512 + c4 * 128:
                                                 d * 512 + c4 * 128 + 128],
                                         ybuf[:, d * WID + j * 512:
                                              d * WID + (j + 1) * 512],
                                         start=(d == 0), stop=(d == 7))
                        step("tensor", _mm)
                        step("vector", lambda v, ps=ps, gj=gj, xj=xj: (
                            v.scalar_tensor_tensor(out=stg[:, S0:S0 + 512],
                                                   in0=gj, scalar=cols_[:, 0:1],
                                                   in1=xj, op0=ALU.mult,
                                                   op1=ALU.add),
                            v.scalar_tensor_tensor(out=stg[:, S0 + 512:S0 + 1024],
                                                   in0=ps, scalar=0.0,
                                                   in1=gj, op0=ALU.bypass,
                                                   op1=ALU.mult))[-1])
                        step("vector", lambda v, xj=xj: v.tensor_add(
                            xj, stg[:, S0:S0 + 512], stg[:, S0 + 512:S0 + 1024]))
                        step("vector", lambda v, xj=xj: v.tensor_mul(
                            stg[:, S0:S0 + 512], xj, xj))

                        def _sm(t, j=j, xj=xj):
                            t.matmul(pst[:, 0:512], onesc[:], xj,
                                     start=True, stop=True)
                            t.matmul(pst[:, 512:1024], onesc[:],
                                     stg[:, S0:S0 + 512], start=True, stop=True)
                        step("tensor", _sm)
                        step("vector", lambda v, j=j: (
                            v.tensor_add(row(0)[:, j * 512:(j + 1) * 512],
                                         row(0)[:, j * 512:(j + 1) * 512],
                                         pst[:, 0:512]),
                            v.tensor_add(row(1)[:, j * 512:(j + 1) * 512],
                                         row(1)[:, j * 512:(j + 1) * 512],
                                         pst[:, 512:1024]))[-1])

                    def _ox1(g, mark, R0=R0, X0=X0):
                        mark(g.dma_start(out=x1o[R0:R0 + 128, :],
                                         in_=stg[:, X0:X0 + WID]))
                    dstep("gpsimd", _ox1, 1)

            step("vector", lambda v: v.tensor_scalar_mul(row(2), row(0), 1.0 / C_))
            step("vector", lambda v: v.tensor_mul(row(0), row(2), row(2)))
            step("vector", lambda v: v.tensor_scalar_mul(row(1), row(1), 1.0 / C_))
            step("vector", lambda v: v.tensor_sub(row(1), row(1), row(0)))
            step("scalar", lambda sc: sc.activation(out=row(1), in_=row(1),
                                                    func=AF.Ln, bias=cbias(LN_EPS)))
            step("scalar", lambda sc: sc.activation(out=row(3), in_=row(1),
                                                    func=AF.Exp, scale=-0.5))
            step("vector", lambda v: v.scalar_tensor_tensor(
                out=row(0), in0=row(2), scalar=-1.0, in1=row(3),
                op0=ALU.mult, op1=ALU.mult))
        _cm_p2.__exit__(None, None, None)
        _cm_p1.__exit__(None, None, None)

        _cm_p3 = nc.psum_tensor("pbc", [128, 2 * 512], F32)
        pbc = _cm_p3.__enter__()
        if True:
            for j in range(NTW):
                def _bc(t, j=j):
                    t.matmul(pbc[:, 0:512], ones1[:],
                             row(3)[:, j * 512:(j + 1) * 512], start=True, stop=True)
                    t.matmul(pbc[:, 512:1024], ones1[:],
                             row(0)[:, j * 512:(j + 1) * 512], start=True, stop=True)
                step("tensor", _bc)
                for d in range(8):
                    def _lx1(s, mark, d=d, j=j):
                        mark(s.dma_start(out=stg[:, 0:512],
                                         in_=x1o[d * 128:(d + 1) * 128,
                                                 j * 512:(j + 1) * 512]))
                    dstep("sync", _lx1, 1)
                    step("vector", lambda v: v.tensor_mul(stg[:, 512:1024],
                                                          stg[:, 0:512],
                                                          pbc[:, 0:512]))
                    step("vector", lambda v, d=d, j=j: v.tensor_add(
                        xn1b[:, d * WID + j * 512:d * WID + (j + 1) * 512],
                        stg[:, 512:1024], pbc[:, 512:1024]))
                    if j >= 1:
                        def _oxn(g, mark, d=d, j=j):
                            mark(g.dma_start(
                                out=xn1o[d * 128:(d + 1) * 128,
                                         (j - 1) * 512:j * 512],
                                in_=xn1b[:, d * WID + j * 512:
                                         d * WID + (j + 1) * 512]))
                        dstep("gpsimd", _oxn, 1)
        _cm_p3.__exit__(None, None, None)

        _cm_p4 = nc.psum_tensor("pmm2", [128, 4 * 512], F32)
        pmm2 = _cm_p4.__enter__()
        if True:
            CW = [128] * 34 + [32]
            ncb = 9
            for cb in range(ncb):
                c0 = cb * 512
                cwidth = min(512, 4384 - c0)
                nch = (cwidth + 127) // 128

                def _lw(s, mark, cb=cb, c0=c0, cwidth=cwidth):
                    for d in range(8):
                        mark(s.dma_start(
                            out=w32[:, d * 512:d * 512 + cwidth],
                            in_=winT[d * 128:(d + 1) * 128, c0:c0 + cwidth]))
                dstep("sync", _lw, 8)

                def _cw(v, cwidth=cwidth):
                    for d in range(8):
                        v.tensor_copy(wbf[:, d * 512:d * 512 + cwidth],
                                      w32[:, d * 512:d * 512 + cwidth])
                step("vector", _cw)

                for c4 in range(nch):
                    R0 = c0 + c4 * 128
                    chw = min(128, 4384 - R0)
                    if R0 >= 2048 and R0 < 4352:
                        ch = (R0 - 2048) // 128

                        def _lcw(s, mark, ch=ch):
                            mark(s.dma_start(out=cw4[:, 0:4],
                                             in_=convw[ch * 128:(ch + 1) * 128, :]))
                            mark(s.dma_start(out=cw4[:, 4:5],
                                             in_=convb[ch * 128:(ch + 1) * 128, :]))
                        dstep("sync", _lcw, 2)
                    for j in range(NTW):
                        ps = pmm2[0:chw, (j % 4) * 512:(j % 4 + 1) * 512]

                        def _mm(t, c4=c4, j=j, ps=ps, chw=chw):
                            for d in range(8):
                                t.matmul(ps, wbf[:, d * 512 + c4 * 128:
                                                 d * 512 + c4 * 128 + chw],
                                         xn1b[:, d * WID + j * 512:
                                              d * WID + (j + 1) * 512],
                                         start=(d == 0), stop=(d == 7))
                        step("tensor", _mm)

                        if R0 < 2048:
                            if j >= 1:
                                step("vector", lambda v, ps=ps, j=j: v.tensor_copy(
                                    stgb[:, (j - 1) * 512:j * 512], ps))
                                if j == NTW - 1:
                                    def _oz(g, mark, R0=R0):
                                        mark(g.dma_start(out=zo[R0:R0 + 128, :],
                                                         in_=stgb[:]))
                                    dstep("gpsimd", _oz, 1)
                        elif R0 < 4352:
                            ch = (R0 - 2048) // 128
                            slot = xbcb[:, ch * 515:(ch + 1) * 515]
                            step("vector", lambda v, ps=ps, slot=slot: v.tensor_copy(
                                slot[:, 3:515], ps))
                            if j >= 1:
                                step("vector", lambda v, slot=slot: v.tensor_scalar(
                                    out=stg[:, 0:512], in0=slot[:, 0:512],
                                    scalar1=cw4[:, 0:1], scalar2=None,
                                    op0=ALU.mult))
                                for jj in range(1, 4):
                                    def _cv(v, jj=jj, slot=slot):
                                        a, b = (0, 512) if jj % 2 == 1 else (512, 0)
                                        v.scalar_tensor_tensor(
                                            out=stg[:, b:b + 512],
                                            in0=slot[:, jj:jj + 512],
                                            scalar=cw4[:, jj:jj + 1],
                                            in1=stg[:, a:a + 512],
                                            op0=ALU.mult, op1=ALU.add)
                                    step("vector", _cv)
                                step("scalar", lambda sc: sc.activation(
                                    out=stgb[:, 0:512], in_=stg[:, 512:1024],
                                    func=AF.Silu, bias=cw4[:, 4:5]))

                                def _ox(g, mark, R0=R0, j=j):
                                    mark(g.dma_start(
                                        out=xbco[R0 - 2048:R0 - 2048 + 128,
                                                 (j - 1) * 512:j * 512],
                                        in_=stgb[:, 0:512]))
                                dstep("gpsimd", _ox, 1)
                            step("vector", lambda v, slot=slot: v.tensor_copy(
                                slot[:, 0:3], slot[:, 512:515]))
                        else:
                            step("vector", lambda v, ps=ps, j=j: v.tensor_copy(
                                dtr[:, j * 512:(j + 1) * 512], ps[0:32, :]))
        _cm_p4.__exit__(None, None, None)

        def _ldt(s, mark):
            mark(s.dma_start(out=cols_[0:32, 1:2], in_=dtb[:]))
            mark(s.dma_start(out=cols_[0:32, 2:3], in_=alog[:]))
        dstep("sync", _ldt, 2)
        step("scalar", lambda sc: sc.activation(out=cols_[0:32, 3:4],
                                                in_=cols_[0:32, 2:3], func=AF.Exp))
        OWN = slice(512, WID)
        step("vector", lambda v: v.tensor_scalar_add(dtr[:, OWN], dtr[:, OWN],
                                                     cols_[0:32, 1:2]))
        step("scalar", lambda sc: sc.activation(out=stg[0:32, 0:TPC],
                                                in_=dtr[:, OWN], func=AF.Abs))
        step("scalar", lambda sc: sc.activation(out=stg[0:32, 0:TPC],
                                                in_=stg[0:32, 0:TPC],
                                                func=AF.Exp, scale=-1.0))
        step("scalar", lambda sc: sc.activation(out=stg[0:32, 0:TPC],
                                                in_=stg[0:32, 0:TPC],
                                                func=AF.Ln, bias=cbias(1.0, 32)))
        step("vector", lambda v: v.tensor_scalar_max(dtr[:, OWN], dtr[:, OWN], 0.0))
        step("vector", lambda v: v.tensor_add(dtr[:, OWN], dtr[:, OWN],
                                              stg[0:32, 0:TPC]))

        def _odt(g, mark):
            mark(g.dma_start(out=dtpo[:], in_=dtr[:, OWN]))
        dstep("gpsimd", _odt, 1)
        step("vector", lambda v: v.tensor_scalar(
            out=dtr[:, OWN], in0=dtr[:, OWN], scalar1=cols_[0:32, 3:4],
            scalar2=-1.0, op0=ALU.mult, op1=ALU.mult))

        def _oad(g, mark):
            mark(g.dma_start(out=adto[:], in_=dtr[:, OWN]))
        dstep("gpsimd", _oad, 1)

        fin = s_ch.n

        def _fin(g, fin=fin):
            g.wait_ge(_s_out, fin)
        sec("gpsimd", _fin)
    return nc


def host_prep_C(inputs, A_out, y_full):
    x = inputs["L0_attn_x"]
    xTf = np.ascontiguousarray(x.reshape(B_ * T_, C_).T)
    maps = []
    for c in range(NCORE):
        b, t0 = c // 4, (c % 4) * TPC
        yb = y_full[b].reshape(C_, T_)
        def pad_slice(arr, dtype):
            out = np.zeros((C_, WID), dtype)
            lo = t0 - 512
            src_lo = max(lo, 0)
            out[:, src_lo - lo:] = arr[:, src_lo:t0 + TPC]
            return out
        g0_full = np.concatenate([A_out[cc]["g0"] for cc in range(b * 4, b * 4 + 4)],
                                 axis=1)
        m = {
            "yT": pad_slice(yb, BF),
            "g0T": pad_slice(g0_full, np.float32),
            "xT": pad_slice(xTf[:, b * T_:(b + 1) * T_], np.float32),
            "projwT": np.ascontiguousarray(inputs["L0_attn_proj_w"].T),
            "projb": inputs["L0_attn_proj_b"].reshape(C_, 1),
            "winT": np.ascontiguousarray(inputs["L1_attn_W_in"].T),
            "convw": inputs["L1_attn_conv_w"].reshape(CONVD, 4),
            "convb": inputs["L1_attn_conv_b"].reshape(CONVD, 1),
            "dtb": inputs["L1_attn_dt_bias"].reshape(32, 1),
            "alog": inputs["L1_attn_A_log"].reshape(32, 1),
        }
        maps.append(m)
    return maps


NCH = T_ // 64


def build_D():
    nc = bass.Bass("TRN2", target_bir_lowering=False, debug=False)
    xhD = nc.dram_tensor("xhD", [8, 64, T_], BF16, kind="ExternalInput").ap()
    BfD = nc.dram_tensor("BfD", [128, T_], BF16, kind="ExternalInput").ap()
    CfD = nc.dram_tensor("CfD", [128, T_], BF16, kind="ExternalInput").ap()
    dtpD = nc.dram_tensor("dtpD", [8, T_], F32, kind="ExternalInput").ap()
    adtD = nc.dram_tensor("adtD", [8, T_], F32, kind="ExternalInput").ap()
    dpD = nc.dram_tensor("dpD", [1, 8], F32, kind="ExternalInput").ap()
    yssd = nc.dram_tensor("yssd", [8, 64, T_], F32, kind="ExternalOutput").ap()
    bscr = nc.dram_tensor("bscr", [8, 128], F32).ap()
    acsT = nc.dram_tensor("acsT", [NCH, 512], F32).ap()

    with ExitStack() as _st, nc.Block() as blk:
        xh4 = _st.enter_context(nc.sbuf_tensor("xh4", [64, 4 * T_], BF16))
        Bf = _st.enter_context(nc.sbuf_tensor("Bf", [128, T_], BF16))
        Cf = _st.enter_context(nc.sbuf_tensor("Cf", [128, T_], BF16))
        acs8 = _st.enter_context(nc.sbuf_tensor("acs8", [8, T_], F32))
        dtp8b = _st.enter_context(nc.sbuf_tensor("dtp8b", [8, T_], BF16))
        dtd8b = _st.enter_context(nc.sbuf_tensor("dtd8b", [8, T_], BF16))
        coltab = _st.enter_context(nc.sbuf_tensor("coltab", [128, 8 * 128], F32))
        dcols = _st.enter_context(nc.sbuf_tensor("dcols", [64, 8], F32))
        srun = _st.enter_context(nc.sbuf_tensor("srun", [128, 2 * 8 * 64], F32))
        sbf = _st.enter_context(nc.sbuf_tensor("sbf", [128, 8 * 64], BF16))
        btok = _st.enter_context(nc.sbuf_tensor("btok", [64, 128], BF16))
        colsA = _st.enter_context(nc.sbuf_tensor("colsA", [64, 8], F32))
        colsB = _st.enter_context(nc.sbuf_tensor("colsB", [64, 16], F32))
        gbuf = _st.enter_context(nc.sbuf_tensor("gbuf", [64, 64], F32))
        xq4 = _st.enter_context(nc.sbuf_tensor("xq4", [64, 256], BF16))
        xw14 = _st.enter_context(nc.sbuf_tensor("xw14", [64, 256], BF16))
        xw24 = _st.enter_context(nc.sbuf_tensor("xw24", [64, 256], BF16))
        lsub4 = _st.enter_context(nc.sbuf_tensor("lsub4", [64, 256], F32))
        lexp4 = _st.enter_context(nc.sbuf_tensor("lexp4", [64, 256], F32))
        mte4 = _st.enter_context(nc.sbuf_tensor("mte4", [64, 256], BF16))
        cdt4 = _st.enter_context(nc.sbuf_tensor("cdt4", [128, 256], BF16))
        ystgw = _st.enter_context(nc.sbuf_tensor("ystgw", [64, 8 * 256], F32))
        erow = _st.enter_context(nc.sbuf_tensor("erow", [1, 2 * 512], F32))
        erow2 = _st.enter_context(nc.sbuf_tensor("erow2", [1, 256], F32))
        mrow = _st.enter_context(nc.sbuf_tensor("mrow", [1, 128], F32))
        identb = _st.enter_context(nc.sbuf_tensor("identb", [128, 128], BF16))
        identf = _st.enter_context(nc.sbuf_tensor("identf", [8, 8], F32))
        ones1 = _st.enter_context(nc.sbuf_tensor("ones1", [1, 128], F32))
        dp = _st.enter_context(nc.sbuf_tensor("dp", [1, 8], F32))
        psT1 = _st.enter_context(nc.psum_tensor("psT1", [64, 144], BF16))
        psT2 = _st.enter_context(nc.psum_tensor("psT2", [64, 8], F32))
        psG = _st.enter_context(nc.psum_tensor("psG", [64, 64], F32))
        psL = _st.enter_context(nc.psum_tensor("psL", [64, 512], F32))
        psCd = _st.enter_context(nc.psum_tensor("psCd", [128, 256], F32))
        psY = _st.enter_context(nc.psum_tensor("psY", [64, 256], F32))
        psSt = _st.enter_context(nc.psum_tensor("psSt", [128, 256], F32))
        psX = _st.enter_context(nc.psum_tensor("psX", [64, 256], BF16))
        _s_out = _st.enter_context(nc.semaphore("s_out"))
        _s_er = _st.enter_context(nc.semaphore("s_er"))

        def sec(name, f):
            getattr(blk, name)(f)

        s_ch = Sem(_s_out)

        class _Rec:
            def __init__(self, e):
                self._e = e
                self.last = None

            def __getattr__(self, k):
                a = getattr(self._e, k)
                if callable(a):
                    def w(*args, **kw):
                        r = a(*args, **kw)
                        if hasattr(r, "then_inc"):
                            self.last = r
                        return r
                    return w
                return a

        def step(eng, fn):
            cur = s_ch.n

            def _f(e, fn=fn, cur=cur):
                e.wait_ge(_s_out, cur)
                rec = _Rec(e)
                fn(rec)
                (rec.last if rec.last is not None else e.nop()).then_inc(_s_out, 1)
            sec(eng, _f)
            s_ch.n += 1

        def dstep(eng, fn, n=1):
            cur = s_ch.n

            def _f(e, fn=fn, cur=cur):
                e.wait_ge(_s_out, cur)
                fn(e, lambda ins: ins.then_inc(_s_out, 16))
            sec(eng, _f)
            s_ch.n += 16 * n

        FR = {}

        def _c0(g):
            g.memset(identb[:], 1.0)
            g.memset(identf[:], 1.0)
            g.memset(ones1[:], 1.0)
            g.memset(srun[:], 0.0)
            g.memset(sbf[:], 0.0)
            FR["f"] = g.alloc_register("fillneg")
            g.reg_mov(FR["f"], NEG_BIG)
        step("gpsimd", _c0)

        def _c1(g):
            g.affine_select(out=identb[:], in_=identb[:], compare_op=ALU.is_equal,
                            fill=0.0, base=0, pattern=[[-1, 128]],
                            channel_multiplier=1)
            g.affine_select(out=identf[:], in_=identf[:], compare_op=ALU.is_equal,
                            fill=0.0, base=0, pattern=[[-1, 8]],
                            channel_multiplier=1)
        step("gpsimd", _c1)

        stA = xh4[0:8, :].bitcast(F32)[:, 0:T_]
        stB = xh4[0:8, :].bitcast(F32)[:, T_:2 * T_]

        def _l0(s, mark):
            mark(s.dma_start(out=Bf[:], in_=BfD[:]))
            mark(s.dma_start(out=Cf[:], in_=CfD[:]))
            mark(s.dma_start(out=stA, in_=adtD[:]))
            mark(s.dma_start(out=dp[:], in_=dpD[:]))
        dstep("sync", _l0, 4)

        for c in range(NCH):
            def _scan(v, c=c):
                v.tensor_tensor_scan(out=acs8[:, c * 64:(c + 1) * 64],
                                     data0=stA[:, c * 64:(c + 1) * 64],
                                     data1=stA[:, c * 64:(c + 1) * 64],
                                     initial=0.0, op0=ALU.add, op1=ALU.bypass)
            step("vector", _scan)

        acs3 = acs8.rearrange("p (c l) -> p c l", l=64)

        def _ds1(v):
            v.tensor_sub(stB.rearrange("p (c l) -> p c l", l=64),
                         acs3[:, :, 63:64].broadcast_to((8, NCH, 64)), acs3)
        step("vector", _ds1)
        step("scalar", lambda sc: sc.activation(out=stB, in_=stB, func=AF.Exp))

        def _ldt(s, mark):
            mark(s.dma_start(out=stA, in_=dtpD[:]))
        dstep("sync", _ldt, 1)
        step("vector", lambda v: v.tensor_mul(dtd8b[:], stB, stA))
        step("vector", lambda v: v.tensor_copy(dtp8b[:], stA))

        def _oat(g, mark):
            mark(g.dma_start(out=acsT.rearrange("c (h l) -> h c l", h=8, l=64),
                             in_=acs8.rearrange("h (c l) -> h c l", l=64)))
        dstep("gpsimd", _oat, 1)

        for h in range(8):
            def _bo(g, mark, h=h):
                mark(g.dma_start(out=bscr[h:h + 1, :],
                                 in_=acs8[h:h + 1, 63:T_:64],
                                 allow_slow_non_contiguous=True))
            dstep("gpsimd", _bo, 1)

            def _bi(s, mark, h=h):
                mark(s.dma_start(out=mrow[:], in_=bscr[h:h + 1, :]))
            dstep("sync", _bi, 1)
            step("scalar", lambda sc: sc.activation(out=mrow[:], in_=mrow[:],
                                                    func=AF.Exp))
            for hf in range(2):
                step("tensor", lambda t, hf=hf: t.matmul(
                    psCd[:, 0:64], ones1[:], mrow[:, hf * 64:(hf + 1) * 64],
                    start=True, stop=True))
                step("vector", lambda v, h=h, hf=hf: v.tensor_copy(
                    coltab[:, h * 128 + hf * 64:h * 128 + (hf + 1) * 64],
                    psCd[:, 0:64]))

        for h in range(8):
            step("tensor", lambda t, h=h: t.matmul(psY[:, 0:1], ones1[0:1, 0:64],
                                                   dp[:, h:h + 1],
                                                   start=True, stop=True))
            step("vector", lambda v, h=h: v.tensor_copy(dcols[:, h:h + 1],
                                                        psY[:, 0:1]))

        for grp in range(2):
            def _lx(s, mark, grp=grp):
                for i in range(4):
                    mark(s.dma_start(out=xh4[:, i * T_:(i + 1) * T_],
                                     in_=xhD[grp * 4 + i]))
            dstep("sync", _lx, 4)

            def _er0(s, mark, grp=grp):
                mark(s.dma_start(out=erow[:, 0:512], in_=acsT[0:1, :]))
            dstep("sync", _er0, 1)
            ern = [127 * 16 * grp]

            for c in range(NCH):
                cs = slice(c * 64, (c + 1) * 64)
                par = c % 2
                epar = c % 2
                h0 = grp * 4
                sr_a = srun[:, (par * 8 + h0) * 64:(par * 8 + h0 + 4) * 64]
                sr_b = srun[:, ((1 - par) * 8 + h0) * 64:
                             ((1 - par) * 8 + h0 + 4) * 64]
                sb4 = sbf[:, h0 * 64:(h0 + 4) * 64]

                def _bi2(s, mark, c=c):
                    mark(s.dma_start(out=erow[:, 0:512], in_=acsT[c:c + 1, :]))
                dstep("sync", _bi2, 1)

                def _sh1(t, c=c, cs=cs):
                    t.transpose(psT1[:, 0:128], Bf[:, cs], identb[:])
                    t.transpose(psT2[:], acs8[:, cs], identf[:])
                    t.transpose(psT1[:, 128:136], dtp8b[:, cs], identb[0:8, 0:8])
                    t.transpose(psT1[:, 136:144], dtd8b[:, cs], identb[0:8, 0:8])
                    t.matmul(psG[:], Bf[:, cs], Cf[:, cs], start=True, stop=True,
                             skip_group_check=True)
                    for i in range(4):
                        t.transpose(psX[:, i * 64:(i + 1) * 64],
                                    xh4[:, i * T_ + c * 64:i * T_ + (c + 1) * 64],
                                    identb[0:64, 0:64])
                step("tensor", _sh1)

                def _sh2(v):
                    v.tensor_copy(btok[:], psT1[:, 0:128])
                    v.tensor_copy(colsA[:], psT2[:])
                    v.tensor_copy(colsB[:], psT1[:, 128:144])
                    v.tensor_copy(gbuf[:], psG[:])
                    v.tensor_copy(xq4[:], psX[:])
                step("vector", _sh2)

                def _psl(t):
                    t.matmul(psL[:], ones1[0:1, 0:64], erow[:, 0:512],
                             start=True, stop=True)
                step("tensor", _psl)

                def _xw(v, h0=h0):
                    for i in range(4):
                        v.tensor_scalar(out=xw14[:, i * 64:(i + 1) * 64],
                                        in0=xq4[:, i * 64:(i + 1) * 64],
                                        scalar1=colsB[:, h0 + i:h0 + i + 1],
                                        scalar2=None, op0=ALU.mult)
                        v.tensor_scalar(out=xw24[:, i * 64:(i + 1) * 64],
                                        in0=xq4[:, i * 64:(i + 1) * 64],
                                        scalar1=colsB[:, 8 + h0 + i:9 + h0 + i],
                                        scalar2=None, op0=ALU.mult)
                    for i in range(4):
                        h = h0 + i
                        v.tensor_scalar(out=lsub4[:, i * 64:(i + 1) * 64],
                                        in0=psL[:, h * 64:(h + 1) * 64],
                                        scalar1=colsA[:, h:h + 1], scalar2=None,
                                        op0=ALU.subtract)
                step("vector", _xw)

                def _lm(g):
                    for i in range(4):
                        g.affine_select(out=lsub4[:, i * 64:(i + 1) * 64],
                                        in_=lsub4[:, i * 64:(i + 1) * 64],
                                        compare_op=ALU.is_ge, fill=FR["f"],
                                        base=0, pattern=[[1, 64]],
                                        channel_multiplier=-1)
                step("gpsimd", _lm)

                def _le(sc, h0=h0):
                    sc.activation(out=lexp4[:], in_=lsub4[:], func=AF.Exp)
                    sc.activation(out=erow2[:], in_=erow[:, h0 * 64:(h0 + 4) * 64],
                                  func=AF.Exp)
                step("scalar", _le)

                def _bc(t):
                    for i in range(4):
                        t.matmul(psCd[:, i * 64:(i + 1) * 64], ones1[:],
                                 erow2[:, i * 64:(i + 1) * 64],
                                 start=True, stop=True)
                step("tensor", _bc)

                def _mm2(v):
                    v.tensor_mul(mte4.rearrange("p (i l) -> p i l", l=64),
                                 gbuf[:].unsqueeze(1).broadcast_to((64, 4, 64)),
                                 lexp4.rearrange("p (i l) -> p i l", l=64))
                    v.tensor_mul(cdt4.rearrange("p (i l) -> p i l", l=64),
                                 Cf[:, cs].unsqueeze(1).broadcast_to((128, 4, 64)),
                                 psCd[:].rearrange("p (i l) -> p i l", l=64))
                step("vector", _mm2)

                def _ymm(t):
                    for i in range(4):
                        t.matmul(psY[:, i * 64:(i + 1) * 64],
                                 xw14[:, i * 64:(i + 1) * 64],
                                 mte4[:, i * 64:(i + 1) * 64],
                                 start=True, stop=False, skip_group_check=True)
                        t.matmul(psSt[:, i * 64:(i + 1) * 64], btok[:],
                                 xw24[:, i * 64:(i + 1) * 64],
                                 start=True, stop=True, skip_group_check=True)
                        t.matmul(psY[:, i * 64:(i + 1) * 64],
                                 sb4[:, i * 64:(i + 1) * 64],
                                 cdt4[:, i * 64:(i + 1) * 64],
                                 start=False, stop=True, skip_group_check=True)
                step("tensor", _ymm)

                def _ev(v, c=c, h0=h0):
                    co = (c % 8) * 256
                    for i in range(4):
                        v.scalar_tensor_tensor(
                            out=ystgw[:, co + i * 64:co + (i + 1) * 64],
                            in0=xh4[:, i * T_ + c * 64:i * T_ + (c + 1) * 64],
                            scalar=dcols[:, h0 + i:h0 + i + 1],
                            in1=psY[:, i * 64:(i + 1) * 64],
                            op0=ALU.mult, op1=ALU.add)
                        v.scalar_tensor_tensor(
                            out=sr_b[:, i * 64:(i + 1) * 64],
                            in0=sr_a[:, i * 64:(i + 1) * 64],
                            scalar=coltab[:, (h0 + i) * 128 + c:
                                          (h0 + i) * 128 + c + 1],
                            in1=psSt[:, i * 64:(i + 1) * 64],
                            op0=ALU.mult, op1=ALU.add)
                step("vector", _ev)
                step("vector", lambda v: v.tensor_copy(sb4[:], sr_b))

                if c % 8 == 7:
                    def _oy(g, mark, c=c, h0=h0):
                        c0 = c - 7
                        for i in range(4):
                            mark(g.dma_start(
                                out=yssd[h0 + i, :, c0 * 64:(c0 + 8) * 64],
                                in_=ystgw.rearrange("p (c8 i l) -> p c8 i l",
                                                    c8=8, i=4)[:, :, i, :]))
                    dstep("gpsimd", _oy, 4)

        fin = s_ch.n

        def _fin(g, fin=fin):
            g.wait_ge(_s_out, fin)
        sec("gpsimd", _fin)
    return nc


def host_prep_D(inputs, C_out):
    maps = []
    for c in range(NCORE):
        b = c // 4
        h0 = 8 * (c % 4)
        cores = range(b * 4, b * 4 + 4)
        xbc = np.concatenate([C_out[cc]["xbco"] for cc in cores], axis=1)
        dtp = np.concatenate([C_out[cc]["dtpo"] for cc in cores], axis=1)
        adt = np.concatenate([C_out[cc]["adto"] for cc in cores], axis=1)
        m = {
            "xhD": np.stack([xbc[(h0 + i) * 64:(h0 + i + 1) * 64]
                             for i in range(8)]),
            "BfD": np.ascontiguousarray(xbc[2048:2176]),
            "CfD": np.ascontiguousarray(xbc[2176:2304]),
            "dtpD": np.ascontiguousarray(dtp[h0:h0 + 8]),
            "adtD": np.ascontiguousarray(adt[h0:h0 + 8]),
            "dpD": inputs["L1_attn_D_param"][h0:h0 + 8].reshape(1, 8)
                   .astype(np.float32),
        }
        maps.append(m)
    return maps


ES = 256


def build_E():
    nc = bass.Bass("TRN2", target_bir_lowering=False, debug=False)
    ysT = nc.dram_tensor("ysT", [DIN, TPC], F32, kind="ExternalInput").ap()
    zT = nc.dram_tensor("zT", [DIN, TPC], BF16, kind="ExternalInput").ap()
    xn1T = nc.dram_tensor("xn1T", [C_, TPC], BF16, kind="ExternalInput").ap()
    woutT = nc.dram_tensor("woutT", [DIN, C_], F32, kind="ExternalInput").ap()
    normw = nc.dram_tensor("normw", [DIN, 1], F32, kind="ExternalInput").ap()
    muT = nc.dram_tensor("muT", [C_, ES], F32, kind="ExternalInput").ap()
    lng = nc.dram_tensor("lng", [C_, 1], F32, kind="ExternalInput").ap()
    lnb = nc.dram_tensor("lnb", [C_, 1], F32, kind="ExternalInput").ap()
    scl = nc.dram_tensor("scl", [1, 1], F32, kind="ExternalInput").ap()

    s1o = nc.dram_tensor("s1o", [C_, TPC], F32, kind="ExternalOutput").ap()
    edo = nc.dram_tensor("edo", [ES, TPC], F32, kind="ExternalOutput").ap()
    dso = nc.dram_tensor("dso", [ES, 1], F32, kind="ExternalOutput").ap()
    sino = nc.dram_tensor("sino", [ES, C_], F32, kind="ExternalOutput").ap()
    cso = nc.dram_tensor("cso", [1, TPC], F32, kind="ExternalOutput").ap()

    NT = TPC // 512

    with ExitStack() as _st, nc.Block() as blk:
        ygb = _st.enter_context(nc.sbuf_tensor("ygb", [128, 16 * TPC], BF16))
        xnf = _st.enter_context(nc.sbuf_tensor("xnf", [128, 8 * TPC], BF16))
        edb = _st.enter_context(nc.sbuf_tensor("edb", [128, 2 * TPC], BF16))
        edf = _st.enter_context(nc.sbuf_tensor("edf", [128, 2 * TPC], F32))
        mub = _st.enter_context(nc.sbuf_tensor("mub", [128, 8 * ES], BF16))
        w32 = _st.enter_context(nc.sbuf_tensor("w32", [128, 4 * 512], F32))
        wbf = _st.enter_context(nc.sbuf_tensor("wbf", [128, 16 * 512], BF16))
        stg = _st.enter_context(nc.sbuf_tensor("stg", [128, TPC + 1024], F32))
        zstg = _st.enter_context(nc.sbuf_tensor("zstg", [128, TPC], BF16))
        stgb = _st.enter_context(nc.sbuf_tensor("stgb", [128, 2 * 512], BF16))
        rows_ = _st.enter_context(nc.sbuf_tensor("rows", [1, 4 * TPC], F32))
        rowb = _st.enter_context(nc.sbuf_tensor("rowb", [1, TPC], BF16))
        cols_ = _st.enter_context(nc.sbuf_tensor("cols", [128, 8], F32))
        ones1 = _st.enter_context(nc.sbuf_tensor("ones1", [1, 128], F32))
        ones1b = _st.enter_context(nc.sbuf_tensor("ones1b", [1, 128], BF16))
        onescb = _st.enter_context(nc.sbuf_tensor("onescb", [128, 1], BF16))
        onesc = _st.enter_context(nc.sbuf_tensor("onesc", [128, 1], F32))
        identb = _st.enter_context(nc.sbuf_tensor("identb", [128, 128], BF16))
        cpool = _st.enter_context(nc.sbuf_tensor("cpool", [128, 8], F32))
        sca = _st.enter_context(nc.sbuf_tensor("sca", [1, 8], F32))
        psTb = _st.enter_context(nc.psum_tensor("psTb", [128, 128], BF16))
        _s_out = _st.enter_context(nc.semaphore("s_out"))
        xntok = ygb[:, 0:16 * C_]
        edtok = ygb[:, 16 * C_:16 * C_ + 16 * ES]

        CP = {LN_EPS: 0, 1e-24: 1, 1e-30: 2}

        def cbias(v, parts=1):
            return cpool[0:parts, CP[v]:CP[v] + 1]

        def sec(name, f):
            getattr(blk, name)(f)

        s_ch = Sem(_s_out)

        class _Rec:
            def __init__(self, e):
                self._e = e
                self.last = None

            def __getattr__(self, k):
                a = getattr(self._e, k)
                if callable(a):
                    def w(*args, **kw):
                        r = a(*args, **kw)
                        if hasattr(r, "then_inc"):
                            self.last = r
                        return r
                    return w
                return a

        def step(eng, fn):
            cur = s_ch.n

            def _f(e, fn=fn, cur=cur):
                e.wait_ge(_s_out, cur)
                rec = _Rec(e)
                fn(rec)
                (rec.last if rec.last is not None else e.nop()).then_inc(_s_out, 1)
            sec(eng, _f)
            s_ch.n += 1

        def dstep(eng, fn, n=1):
            cur = s_ch.n

            def _f(e, fn=fn, cur=cur):
                e.wait_ge(_s_out, cur)
                fn(e, lambda ins: ins.then_inc(_s_out, 16))
            sec(eng, _f)
            s_ch.n += 16 * n

        def row(i):
            return rows_[:, i * TPC:(i + 1) * TPC]

        def _c0(g):
            g.memset(ones1[:], 1.0)
            g.memset(ones1b[:], 1.0)
            g.memset(onesc[:], 1.0)
            g.memset(onescb[:], 1.0)
            g.memset(rows_[:], 0.0)
            g.memset(identb[:], 1.0)
            for _v, _i in CP.items():
                g.memset(cpool[:, _i:_i + 1], _v)
        step("gpsimd", _c0)
        step("gpsimd", lambda g: g.affine_select(
            out=identb[:], in_=identb[:], compare_op=ALU.is_equal, fill=0.0,
            base=0, pattern=[[-1, 128]], channel_multiplier=1))

        _cm1 = nc.psum_tensor("pst", [1, 2 * 512], F32)
        pst = _cm1.__enter__()
        if True:
            for d in range(16):
                def _l(s, mark, d=d):
                    mark(s.dma_start(out=zstg[:], in_=zT[d * 128:(d + 1) * 128, :]))
                    mark(s.dma_start(out=stg[:, 0:TPC],
                                     in_=ysT[d * 128:(d + 1) * 128, :]))
                dstep("sync", _l, 2)
                for j in range(NT):
                    S0 = TPC
                    step("scalar", lambda sc, j=j: sc.activation(
                        out=stg[:, S0:S0 + 512],
                        in_=zstg[:, j * 512:(j + 1) * 512], func=AF.Silu))
                    step("vector", lambda v, d=d, j=j: v.tensor_mul(
                        ygb[:, d * TPC + j * 512:d * TPC + (j + 1) * 512],
                        stg[:, j * 512:(j + 1) * 512], stg[:, S0:S0 + 512]))
                    step("vector", lambda v, d=d, j=j: v.tensor_mul(
                        stg[:, S0 + 512:S0 + 1024],
                        ygb[:, d * TPC + j * 512:d * TPC + (j + 1) * 512],
                        ygb[:, d * TPC + j * 512:d * TPC + (j + 1) * 512]))
                    step("tensor", lambda t, j=j: t.matmul(
                        pst[0:1, 0:512], onesc[:], stg[:, S0 + 512:S0 + 1024],
                        start=True, stop=True))
                    step("vector", lambda v, j=j: v.tensor_add(
                        row(0)[:, j * 512:(j + 1) * 512],
                        row(0)[:, j * 512:(j + 1) * 512], pst[0:1, 0:512]))
            step("scalar", lambda sc: sc.activation(out=row(1), in_=row(0),
                                                    func=AF.Ln, bias=cbias(LN_EPS),
                                                    scale=1.0 / DIN))
            step("scalar", lambda sc: sc.activation(out=row(1), in_=row(1),
                                                    func=AF.Exp, scale=-0.5))
        _cm1.__exit__(None, None, None)

        _cm2 = nc.psum_tensor("pbc", [128, 512], F32)
        pbc2 = _cm2.__enter__()
        if True:
            def _lnw(s, mark):
                mark(s.dma_start(out=cols_[:, 0:1], in_=normw[0:128, :]))
            for j in range(NT):
                step("tensor", lambda t, j=j: t.matmul(
                    pbc2[:], ones1[:], row(1)[:, j * 512:(j + 1) * 512],
                    start=True, stop=True))
                for d in range(16):
                    def _lw(s, mark, d=d):
                        mark(s.dma_start(out=cols_[:, 0:1],
                                         in_=normw[d * 128:(d + 1) * 128, :]))
                    dstep("sync", _lw, 1)

                    def _sc2(v, d=d, j=j):
                        sl = ygb[:, d * TPC + j * 512:d * TPC + (j + 1) * 512]
                        v.tensor_mul(stg[:, 0:512], sl, pbc2[:])
                    step("vector", _sc2)
                    step("vector", lambda v, d=d, j=j: v.tensor_scalar(
                        out=ygb[:, d * TPC + j * 512:d * TPC + (j + 1) * 512],
                        in0=stg[:, 0:512], scalar1=cols_[:, 0:1], scalar2=None,
                        op0=ALU.mult))
        _cm2.__exit__(None, None, None)

        _cm3 = nc.psum_tensor("pmm", [128, 4 * 512], F32)
        pmm = _cm3.__enter__()
        if True:
            for cb in range(2):
                for d4 in range(4):
                    def _lw(s, mark, cb=cb, d4=d4):
                        for i in range(4):
                            d = d4 * 4 + i
                            mark(s.dma_start(
                                out=w32[:, i * 512:(i + 1) * 512],
                                in_=woutT[d * 128:(d + 1) * 128,
                                          cb * 512:(cb + 1) * 512]))
                    dstep("sync", _lw, 4)

                    def _cw(v, d4=d4):
                        for i in range(4):
                            d = d4 * 4 + i
                            v.tensor_copy(wbf[:, d * 512:(d + 1) * 512],
                                          w32[:, i * 512:(i + 1) * 512])
                    step("vector", _cw)
                for c4 in range(4):
                    R0 = cb * 512 + c4 * 128
                    for j in range(NT):
                        ps = pmm[:, (j % 4) * 512:(j % 4 + 1) * 512]

                        def _mm(t, c4=c4, j=j, ps=ps):
                            for d in range(16):
                                t.matmul(ps, wbf[:, d * 512 + c4 * 128:
                                                 d * 512 + c4 * 128 + 128],
                                         ygb[:, d * TPC + j * 512:
                                             d * TPC + (j + 1) * 512],
                                         start=(d == 0), stop=(d == 15))
                        step("tensor", _mm)
                        step("vector", lambda v, ps=ps: v.tensor_copy(
                            stg[:, 0:512], ps))

                        def _os(g, mark, R0=R0, j=j):
                            mark(g.dma_start(out=s1o[R0:R0 + 128,
                                                     j * 512:(j + 1) * 512],
                                             in_=stg[:, 0:512]))
                        dstep("gpsimd", _os, 1)
        _cm3.__exit__(None, None, None)
        _cm4 = nc.psum_tensor("pm4", [128, 2 * 512], F32)
        pm4 = _cm4.__enter__()
        if True:
            def _lx(s, mark):
                for d in range(8):
                    mark(s.dma_start(out=xnf[:, d * TPC:(d + 1) * TPC],
                                     in_=xn1T[d * 128:(d + 1) * 128, :]))
            dstep("sync", _lx, 8)
            for d in range(8):
                for j in range(NT):
                    def _sq(v, d=d, j=j):
                        sl = xnf[:, d * TPC + j * 512:d * TPC + (j + 1) * 512]
                        v.tensor_mul(stgb[:, 0:512], sl, sl)
                    step("vector", _sq)

                    def _mm(t, d=d, j=j):
                        sl = xnf[:, d * TPC + j * 512:d * TPC + (j + 1) * 512]
                        t.matmul(pm4[0:1, 0:512], onescb[:], sl,
                                 start=True, stop=True)
                        t.matmul(pm4[0:1, 512:1024], onescb[:], stgb[:, 0:512],
                                 start=True, stop=True)
                    step("tensor", _mm)
                    step("vector", lambda v, j=j: v.tensor_add(
                        row(0)[:, j * 512:(j + 1) * 512],
                        row(0)[:, j * 512:(j + 1) * 512], pm4[0:1, 0:512]))
                    step("vector", lambda v, j=j: v.tensor_add(
                        row(1)[:, j * 512:(j + 1) * 512],
                        row(1)[:, j * 512:(j + 1) * 512], pm4[0:1, 512:1024]))
            step("vector", lambda v: v.tensor_scalar_mul(row(2), row(0), 1.0 / C_))
            step("vector", lambda v: v.tensor_mul(row(0), row(2), row(2)))
            step("vector", lambda v: v.tensor_scalar_mul(row(1), row(1), 1.0 / C_))
            step("vector", lambda v: v.tensor_sub(row(1), row(1), row(0)))
            step("scalar", lambda sc: sc.activation(out=row(1), in_=row(1),
                                                    func=AF.Ln, bias=cbias(LN_EPS)))
            step("scalar", lambda sc: sc.activation(out=row(1), in_=row(1),
                                                    func=AF.Exp, scale=-0.5))
            step("vector", lambda v: v.scalar_tensor_tensor(
                out=row(3), in0=row(2), scalar=-1.0, in1=row(1),
                op0=ALU.mult, op1=ALU.mult))
            for j in range(NT):
                def _bc(t, j=j):
                    t.matmul(pm4[:, 0:512], ones1[:],
                             row(1)[:, j * 512:(j + 1) * 512], start=True, stop=True)
                    t.matmul(pm4[:, 512:1024], ones1[:],
                             row(3)[:, j * 512:(j + 1) * 512], start=True, stop=True)
                step("tensor", _bc)
                for d in range(8):
                    def _lgb(s, mark, d=d):
                        mark(s.dma_start(out=cols_[:, 1:2],
                                         in_=lng[d * 128:(d + 1) * 128, :]))
                        mark(s.dma_start(out=cols_[:, 2:3],
                                         in_=lnb[d * 128:(d + 1) * 128, :]))
                    dstep("sync", _lgb, 2)
                    sl = lambda d=d, j=j: xnf[:, d * TPC + j * 512:
                                              d * TPC + (j + 1) * 512]
                    step("vector", lambda v, d=d, j=j: v.tensor_mul(
                        stg[:, 0:512],
                        xnf[:, d * TPC + j * 512:d * TPC + (j + 1) * 512],
                        pm4[:, 0:512]))
                    step("vector", lambda v: v.tensor_add(stg[:, 512:1024],
                                                          stg[:, 0:512],
                                                          pm4[:, 512:1024]))
                    step("vector", lambda v, d=d, j=j: v.tensor_scalar(
                        out=xnf[:, d * TPC + j * 512:d * TPC + (j + 1) * 512],
                        in0=stg[:, 512:1024], scalar1=cols_[:, 1:2],
                        scalar2=cols_[:, 2:3], op0=ALU.mult, op1=ALU.add))
            step("gpsimd", lambda g: g.memset(row(0), 0.0))
            for d in range(8):
                for j in range(NT):
                    def _sq(v, d=d, j=j):
                        sl = xnf[:, d * TPC + j * 512:d * TPC + (j + 1) * 512]
                        v.tensor_mul(stgb[:, 0:512], sl, sl)
                    step("vector", _sq)
                    step("tensor", lambda t: t.matmul(pm4[0:1, 0:512], onescb[:],
                                                      stgb[:, 0:512],
                                                      start=True, stop=True))
                    step("vector", lambda v, j=j: v.tensor_add(
                        row(0)[:, j * 512:(j + 1) * 512],
                        row(0)[:, j * 512:(j + 1) * 512], pm4[0:1, 0:512]))
            def _ls(s, mark):
                mark(s.dma_start(out=sca[:, 0:1], in_=scl[:]))
            dstep("sync", _ls, 1)
            step("scalar", lambda sc: sc.activation(out=row(0), in_=row(0),
                                                    func=AF.Ln, bias=cbias(1e-24)))
            step("scalar", lambda sc: sc.activation(out=row(0), in_=row(0),
                                                    func=AF.Exp, scale=-0.5))
            step("vector", lambda v: v.tensor_scalar(
                out=row(0), in0=row(0), scalar1=sca[:, 0:1], scalar2=None,
                op0=ALU.mult))
            step("vector", lambda v: v.tensor_copy(rowb[:], row(0)))
            step("scalar", lambda sc: sc.activation(out=sca[:, 1:2],
                                                    in_=sca[:, 0:1], func=AF.Abs))
            step("vector", lambda v: v.tensor_scalar_mul(sca[:, 2:3], sca[:, 1:2],
                                                         -1.0))
            step("vector", lambda v: v.tensor_copy(stgb[0:1, 0:1], sca[:, 2:3]))
            step("tensor", lambda t: t.matmul(pm4[:, 0:128][:, 0:1], ones1b[:],
                                              stgb[0:1, 0:1], start=True, stop=True))
            step("vector", lambda v: v.tensor_copy(cols_[:, 3:4], pm4[:, 0:1]))

            step("gpsimd", lambda g: g.memset(stg[0:1, 0:256], 0.0))
            for d in range(8):
                def _lmu(s, mark, d=d):
                    mark(s.dma_start(out=stg[:, 512:768],
                                     in_=muT[d * 128:(d + 1) * 128, :]))
                dstep("sync", _lmu, 1)
                step("vector", lambda v: v.tensor_mul(stg[:, 768:1024],
                                                      stg[:, 512:768],
                                                      stg[:, 512:768]))
                step("tensor", lambda t: t.matmul(pm4[0:1, 0:256], onesc[:],
                                                  stg[:, 768:1024],
                                                  start=True, stop=True))
                step("vector", lambda v: v.tensor_add(stg[0:1, 0:256],
                                                      stg[0:1, 0:256],
                                                      pm4[0:1, 0:256]))
                step("vector", lambda v, d=d: v.tensor_copy(
                    mub[:, d * ES:(d + 1) * ES], stg[:, 512:768]))
            step("scalar", lambda sc: sc.activation(out=stg[0:1, 0:256],
                                                    in_=stg[0:1, 0:256],
                                                    func=AF.Ln, bias=cbias(1e-24)))
            step("scalar", lambda sc: sc.activation(out=stg[0:1, 0:256],
                                                    in_=stg[0:1, 0:256],
                                                    func=AF.Exp, scale=-0.5))
            step("vector", lambda v: v.tensor_copy(stgb[0:1, 0:256],
                                                   stg[0:1, 0:256]))
            step("tensor", lambda t: t.matmul(pm4[:, 0:256], ones1b[:],
                                              stgb[0:1, 0:256],
                                              start=True, stop=True))
            for d in range(8):
                step("vector", lambda v, d=d: v.tensor_mul(
                    mub[:, d * ES:(d + 1) * ES], mub[:, d * ES:(d + 1) * ES],
                    pm4[:, 0:256]))

            for ec in range(2):
                for j in range(NT):
                    def _mm(t, ec=ec, j=j):
                        for d in range(8):
                            t.matmul(pm4[:, 0:512],
                                     mub[:, d * ES + ec * 128:d * ES + ec * 128 + 128],
                                     xnf[:, d * TPC + j * 512:d * TPC + (j + 1) * 512],
                                     start=(d == 0), stop=(d == 7))
                    step("tensor", _mm)
                    step("tensor", lambda t, j=j: t.matmul(
                        pm4[:, 512:1024], ones1b[:],
                        rowb[:, j * 512:(j + 1) * 512], start=True, stop=True))
                    step("vector", lambda v: v.tensor_copy(stg[:, 1024:1536],
                                                           pm4[:, 0:512]))
                    step("vector", lambda v: v.tensor_mul(stg[:, 0:512],
                                                          stg[:, 1024:1536],
                                                          pm4[:, 512:1024]))
                    step("scalar", lambda sc: sc.activation(out=stg[:, 512:1024],
                                                            in_=stg[:, 0:512],
                                                            func=AF.Exp,
                                                            bias=cols_[:, 3:4]))
                    step("vector", lambda v, ec=ec, j=j: v.tensor_copy(
                        edf[:, ec * TPC + j * 512:ec * TPC + (j + 1) * 512],
                        stg[:, 512:1024]))
                    step("vector", lambda v, ec=ec, j=j: v.tensor_copy(
                        edb[:, ec * TPC + j * 512:ec * TPC + (j + 1) * 512],
                        stg[:, 512:1024]))

            for ec in range(2):
                def _oe(g, mark, ec=ec):
                    mark(g.dma_start(out=edo[ec * 128:(ec + 1) * 128, :],
                                     in_=edf[:, ec * TPC:(ec + 1) * TPC]))
                dstep("gpsimd", _oe, 1)
                step("vector", lambda v, ec=ec: v.reduce_sum(
                    cols_[:, 4:5], edf[:, ec * TPC:(ec + 1) * TPC], axis=AX))

                def _od(g, mark, ec=ec):
                    mark(g.dma_start(out=dso[ec * 128:(ec + 1) * 128, :],
                                     in_=cols_[:, 4:5]))
                dstep("gpsimd", _od, 1)
            for j in range(NT):
                def _cs(t, j=j):
                    t.matmul(pm4[0:1, 0:512], onescb[:],
                             edb[:, 0 * TPC + j * 512:0 * TPC + (j + 1) * 512],
                             start=True, stop=False)
                    t.matmul(pm4[0:1, 0:512], onescb[:],
                             edb[:, 1 * TPC + j * 512:1 * TPC + (j + 1) * 512],
                             start=False, stop=True)
                step("tensor", _cs)
                step("vector", lambda v, j=j: v.tensor_copy(
                    row(0)[:, j * 512:(j + 1) * 512], pm4[0:1, 0:512]))

            def _oc(g, mark):
                mark(g.dma_start(out=cso[:], in_=row(0)))
            dstep("gpsimd", _oc, 1)

            for ec in range(2):
                for tc in range(16):
                    def _tr(t, ec=ec, tc=tc):
                        t.transpose(psTb[:],
                                    edb[:, ec * TPC + tc * 128:
                                        ec * TPC + (tc + 1) * 128], identb[:])
                    step("tensor", _tr)
                    step("vector", lambda v, ec=ec, tc=tc: v.tensor_copy(
                        edtok[:, tc * ES + ec * 128:tc * ES + (ec + 1) * 128],
                        psTb[:]))
            for d in range(8):
                for tc in range(16):
                    def _tr(t, d=d, tc=tc):
                        t.transpose(psTb[:],
                                    xnf[:, d * TPC + tc * 128:
                                        d * TPC + (tc + 1) * 128], identb[:])
                    step("tensor", _tr)
                    step("vector", lambda v, d=d, tc=tc: v.tensor_copy(
                        xntok[:, tc * C_ + d * 128:tc * C_ + (d + 1) * 128],
                        psTb[:]))
            for ec in range(2):
                for dj in range(2):
                    def _mm(t, ec=ec, dj=dj):
                        for tc in range(16):
                            t.matmul(pm4[:, 0:512],
                                     edtok[:, tc * ES + ec * 128:
                                           tc * ES + (ec + 1) * 128],
                                     xntok[:, tc * C_ + dj * 512:
                                           tc * C_ + (dj + 1) * 512],
                                     start=(tc == 0), stop=(tc == 15))
                    step("tensor", _mm)
                    step("vector", lambda v: v.tensor_copy(stg[:, 0:512],
                                                           pm4[:, 0:512]))

                    def _osi(g, mark, ec=ec, dj=dj):
                        mark(g.dma_start(out=sino[ec * 128:(ec + 1) * 128,
                                                  dj * 512:(dj + 1) * 512],
                                         in_=stg[:, 0:512]))
                    dstep("gpsimd", _osi, 1)
        _cm4.__exit__(None, None, None)

        fin = s_ch.n

        def _fin(g, fin=fin):
            g.wait_ge(_s_out, fin)
        sec("gpsimd", _fin)
    return nc


def build_F():
    nc = bass.Bass("TRN2", target_bir_lowering=False, debug=False)
    sinp = nc.dram_tensor("sinp", [4, 64, C_], F32, kind="ExternalInput").ap()
    dsp = nc.dram_tensor("dsp", [4, 64, 1], F32, kind="ExternalInput").ap()
    w1e = nc.dram_tensor("w1e", [4, C_, 256], F32, kind="ExternalInput").ap()
    b1e = nc.dram_tensor("b1e", [4, 256, 1], F32, kind="ExternalInput").ap()
    w2e = nc.dram_tensor("w2e", [4, 256, C_], F32, kind="ExternalInput").ap()
    b2e = nc.dram_tensor("b2e", [4, C_, 1], F32, kind="ExternalInput").ap()
    souto = nc.dram_tensor("souto", [C_, 64], F32, kind="ExternalOutput").ap()

    with ExitStack() as _st, nc.Block() as blk:
        sin = _st.enter_context(nc.sbuf_tensor("sin", [128, 2 * C_], F32))
        gsc = _st.enter_context(nc.sbuf_tensor("gsc", [128, 4], F32))
        sintb = _st.enter_context(nc.sbuf_tensor("sintb", [128, 8 * 64], BF16))
        hten = _st.enter_context(nc.sbuf_tensor("hten", [128, 32], BF16))
        wstg = _st.enter_context(nc.sbuf_tensor("wstg", [128, 2 * 512], F32))
        wb = _st.enter_context(nc.sbuf_tensor("wb", [128, 16 * 256], BF16))
        stg = _st.enter_context(nc.sbuf_tensor("stg", [128, 512], F32))
        bcol = _st.enter_context(nc.sbuf_tensor("bcol", [128, 2], F32))
        identb = _st.enter_context(nc.sbuf_tensor("identb", [128, 128], BF16))
        psT = _st.enter_context(nc.psum_tensor("psT", [128, 128], BF16))
        psM = _st.enter_context(nc.psum_tensor("psM", [128, 2 * 64], F32))
        _s_out = _st.enter_context(nc.semaphore("s_out"))

        def sec(name, f):
            getattr(blk, name)(f)

        s_ch = Sem(_s_out)

        class _Rec:
            def __init__(self, e):
                self._e = e
                self.last = None

            def __getattr__(self, k):
                a = getattr(self._e, k)
                if callable(a):
                    def w(*args, **kw):
                        r = a(*args, **kw)
                        if hasattr(r, "then_inc"):
                            self.last = r
                        return r
                    return w
                return a

        def step(eng, fn):
            cur = s_ch.n

            def _f(e, fn=fn, cur=cur):
                e.wait_ge(_s_out, cur)
                rec = _Rec(e)
                fn(rec)
                (rec.last if rec.last is not None else e.nop()).then_inc(_s_out, 1)
            sec(eng, _f)
            s_ch.n += 1

        def dstep(eng, fn, n=1):
            cur = s_ch.n

            def _f(e, fn=fn, cur=cur):
                e.wait_ge(_s_out, cur)
                fn(e, lambda ins: ins.then_inc(_s_out, 16))
            sec(eng, _f)
            s_ch.n += 16 * n

        step("gpsimd", lambda g: g.memset(identb[:], 1.0))
        step("gpsimd", lambda g: g.affine_select(
            out=identb[:], in_=identb[:], compare_op=ALU.is_equal, fill=0.0,
            base=0, pattern=[[-1, 128]], channel_multiplier=1))

        for p in range(4):
            def _l(s, mark, p=p):
                mark(s.dma_start(out=wstg[0:64, 0:1024], in_=sinp[p]))
                mark(s.dma_start(out=bcol[0:64, 1:2], in_=dsp[p]))
            dstep("sync", _l, 2)
            if p == 0:
                step("vector", lambda v: v.tensor_copy(sin[0:64, 0:C_],
                                                       wstg[0:64, 0:1024]))
                step("vector", lambda v: v.tensor_copy(bcol[0:64, 0:1],
                                                       bcol[0:64, 1:2]))
            else:
                step("vector", lambda v: v.tensor_add(sin[0:64, 0:C_],
                                                      sin[0:64, 0:C_],
                                                      wstg[0:64, 0:1024]))
                step("vector", lambda v: v.tensor_add(bcol[0:64, 0:1],
                                                      bcol[0:64, 0:1],
                                                      bcol[0:64, 1:2]))
        step("scalar", lambda sc: sc.activation(out=bcol[0:64, 0:1],
                                                in_=bcol[0:64, 0:1], func=AF.Ln))
        step("scalar", lambda sc: sc.activation(out=bcol[0:64, 0:1],
                                                in_=bcol[0:64, 0:1],
                                                func=AF.Exp, scale=-1.0))
        step("vector", lambda v: v.tensor_scalar(
            out=sin[0:64, C_:2 * C_], in0=sin[0:64, 0:C_],
            scalar1=bcol[0:64, 0:1], scalar2=None, op0=ALU.mult))
        step("vector", lambda v: v.tensor_copy(wb[0:64, 0:C_],
                                               sin[0:64, C_:2 * C_]))
        for dc in range(8):
            step("tensor", lambda t, dc=dc: t.transpose(
                psT[:, 0:64], wb[0:64, dc * 128:(dc + 1) * 128],
                identb[0:64, 0:64]))
            step("vector", lambda v, dc=dc: v.tensor_copy(
                sintb[:, dc * 64:(dc + 1) * 64], psT[:, 0:64]))

        for e in range(4):
            for hc in range(2):
                def _lb1(s, mark, e=e, hc=hc):
                    mark(s.dma_start(out=bcol[:, 0:1],
                                     in_=b1e[e, hc * 128:(hc + 1) * 128, :]))
                dstep("sync", _lb1, 1)
                for d in range(8):
                    def _lw(s, mark, e=e, hc=hc, d=d):
                        mark(s.dma_start(out=wstg[:, 0:128],
                                         in_=w1e[e, d * 128:(d + 1) * 128,
                                                 hc * 128:(hc + 1) * 128]))
                    dstep("sync", _lw, 1)
                    step("vector", lambda v, d=d: v.tensor_copy(
                        wb[:, d * 128:(d + 1) * 128], wstg[:, 0:128]))

                def _mm(t, e=e, hc=hc):
                    for d in range(8):
                        t.matmul(psM[:, 0:16], wb[:, d * 128:(d + 1) * 128],
                                 sintb[:, d * 64 + e * 16:d * 64 + (e + 1) * 16],
                                 start=(d == 0), stop=(d == 7))
                step("tensor", _mm)
                step("scalar", lambda sc, hc=hc: sc.activation(
                    out=hten[:, hc * 16:(hc + 1) * 16], in_=psM[:, 0:16],
                    func=AF.Gelu, bias=bcol[:, 0:1]))
            for dc in range(8):
                def _lb2(s, mark, e=e, dc=dc):
                    mark(s.dma_start(out=bcol[:, 0:1],
                                     in_=b2e[e, dc * 128:(dc + 1) * 128, :]))
                dstep("sync", _lb2, 1)
                for hc in range(2):
                    def _lw(s, mark, e=e, dc=dc, hc=hc):
                        mark(s.dma_start(out=wstg[:, 0:128],
                                         in_=w2e[e, hc * 128:(hc + 1) * 128,
                                                 dc * 128:(dc + 1) * 128]))
                    dstep("sync", _lw, 1)
                    step("vector", lambda v, hc=hc: v.tensor_copy(
                        wb[:, hc * 128:(hc + 1) * 128], wstg[:, 0:128]))

                def _mm(t, e=e, dc=dc):
                    for hc in range(2):
                        t.matmul(psM[:, 64:80], wb[:, hc * 128:(hc + 1) * 128],
                                 hten[:, hc * 16:(hc + 1) * 16],
                                 start=(hc == 0), stop=(hc == 1))
                step("tensor", _mm)
                step("vector", lambda v: v.tensor_scalar_add(
                    stg[:, 0:16], psM[:, 64:80], bcol[:, 0:1]))

                def _oso(g, mark, e=e, dc=dc):
                    mark(g.dma_start(out=souto[dc * 128:(dc + 1) * 128,
                                               e * 16:(e + 1) * 16],
                                     in_=stg[:, 0:16]))
                dstep("gpsimd", _oso, 1)

        fin = s_ch.n

        def _fin(g, fin=fin):
            g.wait_ge(_s_out, fin)
        sec("gpsimd", _fin)
    return nc


def build_G():
    nc = bass.Bass("TRN2", target_bir_lowering=False, debug=False)
    soutT = nc.dram_tensor("soutT", [ES, C_], F32, kind="ExternalInput").ap()
    edi = nc.dram_tensor("edi", [ES, TPC], F32, kind="ExternalInput").ap()
    csi = nc.dram_tensor("csi", [1, TPC], F32, kind="ExternalInput").ap()
    s1i = nc.dram_tensor("s1i", [C_, TPC], F32, kind="ExternalInput").ap()
    x1i = nc.dram_tensor("x1i", [C_, TPC], F32, kind="ExternalInput").ap()
    outo = nc.dram_tensor("outo", [C_, TPC], F32, kind="ExternalOutput").ap()
    NT = TPC // 512

    with ExitStack() as _st, nc.Block() as blk:
        edb = _st.enter_context(nc.sbuf_tensor("edb", [128, 2 * TPC], BF16))
        sob = _st.enter_context(nc.sbuf_tensor("sob", [128, 2 * C_], BF16))
        rcs = _st.enter_context(nc.sbuf_tensor("rcs", [1, TPC], F32))
        rcb = _st.enter_context(nc.sbuf_tensor("rcb", [1, TPC], BF16))
        stg = _st.enter_context(nc.sbuf_tensor("stg", [128, 6 * 512], F32))
        ones1b = _st.enter_context(nc.sbuf_tensor("ones1b", [1, 128], BF16))
        psM = _st.enter_context(nc.psum_tensor("psM", [128, 2 * 512], F32))
        _s_out = _st.enter_context(nc.semaphore("s_out"))

        def sec(name, f):
            getattr(blk, name)(f)

        s_ch = Sem(_s_out)

        class _Rec:
            def __init__(self, e):
                self._e = e
                self.last = None

            def __getattr__(self, k):
                a = getattr(self._e, k)
                if callable(a):
                    def w(*args, **kw):
                        r = a(*args, **kw)
                        if hasattr(r, "then_inc"):
                            self.last = r
                        return r
                    return w
                return a

        def step(eng, fn):
            cur = s_ch.n

            def _f(e, fn=fn, cur=cur):
                e.wait_ge(_s_out, cur)
                rec = _Rec(e)
                fn(rec)
                (rec.last if rec.last is not None else e.nop()).then_inc(_s_out, 1)
            sec(eng, _f)
            s_ch.n += 1

        def dstep(eng, fn, n=1):
            cur = s_ch.n

            def _f(e, fn=fn, cur=cur):
                e.wait_ge(_s_out, cur)
                fn(e, lambda ins: ins.then_inc(_s_out, 16))
            sec(eng, _f)
            s_ch.n += 16 * n

        step("gpsimd", lambda g: g.memset(ones1b[:], 1.0))

        def _l0(s, mark):
            for ec in range(2):
                mark(s.dma_start(out=stg[:, 0:TPC],
                                 in_=edi[ec * 128:(ec + 1) * 128, :]))
            mark(s.dma_start(out=rcs[:], in_=csi[:]))
        for ec in range(2):
            def _le(s, mark, ec=ec):
                mark(s.dma_start(out=stg[:, 0:TPC],
                                 in_=edi[ec * 128:(ec + 1) * 128, :]))
            dstep("sync", _le, 1)
            step("vector", lambda v, ec=ec: v.tensor_copy(
                edb[:, ec * TPC:(ec + 1) * TPC], stg[:, 0:TPC]))
        for ec in range(2):
            def _ls(s, mark, ec=ec):
                mark(s.dma_start(out=stg[:, 0:C_],
                                 in_=soutT[ec * 128:(ec + 1) * 128, :]))
            dstep("sync", _ls, 1)
            step("vector", lambda v, ec=ec: v.tensor_copy(
                sob[:, ec * C_:(ec + 1) * C_], stg[:, 0:C_]))

        def _lc(s, mark):
            mark(s.dma_start(out=rcs[:], in_=csi[:]))
        dstep("sync", _lc, 1)
        step("scalar", lambda sc: sc.activation(out=rcs[:], in_=rcs[:], func=AF.Ln))
        step("scalar", lambda sc: sc.activation(out=rcs[:], in_=rcs[:],
                                                func=AF.Exp, scale=-1.0))
        step("vector", lambda v: v.tensor_copy(rcb[:], rcs[:]))

        for dc in range(8):
            for j in range(NT):
                def _mm(t, dc=dc, j=j):
                    for ec in range(2):
                        t.matmul(psM[:, 0:512],
                                 sob[:, ec * C_ + dc * 128:ec * C_ + (dc + 1) * 128],
                                 edb[:, ec * TPC + j * 512:ec * TPC + (j + 1) * 512],
                                 start=(ec == 0), stop=(ec == 1))
                    t.matmul(psM[:, 512:1024], ones1b[:],
                             rcb[:, j * 512:(j + 1) * 512], start=True, stop=True,
                             skip_group_check=True)
                step("tensor", _mm)

                def _lsx(s, mark, dc=dc, j=j):
                    mark(s.dma_start(out=stg[:, 512:1024],
                                     in_=s1i[dc * 128:(dc + 1) * 128,
                                             j * 512:(j + 1) * 512]))
                    mark(s.dma_start(out=stg[:, 1024:1536],
                                     in_=x1i[dc * 128:(dc + 1) * 128,
                                             j * 512:(j + 1) * 512]))
                dstep("sync", _lsx, 2)
                step("vector", lambda v: v.tensor_copy(stg[:, 2048:2560],
                                                       psM[:, 0:512]))
                step("vector", lambda v: v.tensor_mul(stg[:, 0:512],
                                                      stg[:, 2048:2560],
                                                      psM[:, 512:1024]))
                step("scalar", lambda sc: sc.activation(out=stg[:, 1536:2048],
                                                        in_=stg[:, 0:512],
                                                        func=AF.Silu))
                step("vector", lambda v: v.tensor_mul(stg[:, 0:512],
                                                      stg[:, 512:1024],
                                                      stg[:, 1536:2048]))
                step("vector", lambda v: v.tensor_add(stg[:, 512:1024],
                                                      stg[:, 1024:1536],
                                                      stg[:, 0:512]))

                def _oo(g, mark, dc=dc, j=j):
                    mark(g.dma_start(out=outo[dc * 128:(dc + 1) * 128,
                                              j * 512:(j + 1) * 512],
                                     in_=stg[:, 512:1024]))
                dstep("gpsimd", _oo, 1)

        fin = s_ch.n

        def _fin(g, fin=fin):
            g.wait_ge(_s_out, fin)
        sec("gpsimd", _fin)
    return nc


def host_prep_E(inputs, C_out, D_out):
    maps = []
    for c in range(NCORE):
        b, t0 = c // 4, (c % 4) * TPC
        cores = range(b * 4, b * 4 + 4)
        ys = np.concatenate([D_out[b * 4 + i]["yssd"].reshape(512, T_)
                             for i in range(4)], axis=0)
        m = {
            "ysT": np.ascontiguousarray(ys[:, t0:t0 + TPC]),
            "zT": C_out[c]["zo"],
            "xn1T": C_out[c]["xn1o"],
            "woutT": np.ascontiguousarray(inputs["L1_attn_W_out"].T),
            "normw": inputs["L1_attn_norm_w"].reshape(DIN, 1),
            "muT": np.ascontiguousarray(inputs["L1_ffn_mu"].reshape(C_, ES)),
            "lng": inputs["L1_ffn_ln_gamma"].reshape(C_, 1),
            "lnb": inputs["L1_ffn_ln_beta"].reshape(C_, 1),
            "scl": inputs["L1_ffn_scale"].reshape(1, 1),
        }
        maps.append(m)
    return maps


def host_prep_F(inputs, E_out):
    w1 = inputs["L1_ffn_w1"]
    b1 = inputs["L1_ffn_b1"]
    w2 = inputs["L1_ffn_w2"]
    b2 = inputs["L1_ffn_b2"]
    maps = []
    for c in range(NCORE):
        b, e0 = c // 4, 4 * (c % 4)
        cores = range(b * 4, b * 4 + 4)
        r0 = e0 * 16
        m = {
            "sinp": np.stack([E_out[cc]["sino"][r0:r0 + 64] for cc in cores]),
            "dsp": np.stack([E_out[cc]["dso"][r0:r0 + 64] for cc in cores]),
            "w1e": np.ascontiguousarray(w1[e0:e0 + 4]),
            "b1e": np.ascontiguousarray(b1[e0:e0 + 4].reshape(4, 256, 1)),
            "w2e": np.ascontiguousarray(w2[e0:e0 + 4]),
            "b2e": np.ascontiguousarray(b2[e0:e0 + 4].reshape(4, C_, 1)),
        }
        maps.append(m)
    return maps


def host_prep_G(F_out, E_out, C_out):
    sout = {}
    for b in range(2):
        s = np.concatenate([F_out[b * 4 + i]["souto"] for i in range(4)],
                           axis=1)
        sout[b] = np.ascontiguousarray(s.T)
    maps = []
    for c in range(NCORE):
        b = c // 4
        m = {
            "soutT": sout[b],
            "edi": E_out[c]["edo"],
            "csi": E_out[c]["cso"],
            "s1i": E_out[c]["s1o"],
            "x1i": np.ascontiguousarray(C_out[c]["x1o"][:, 512:]),
        }
        maps.append(m)
    return maps


def kernel(**inputs):
    core_ids = list(range(NCORE))
    nA = build_A()
    A_out = run_bass_kernel_spmd(nA, host_prep_A(inputs), core_ids=core_ids).results
    nB = build_B()
    B_out = run_bass_kernel_spmd(nB, host_prep_B(inputs, A_out),
                                 core_ids=core_ids).results
    y = reassemble_B(B_out)
    nC = build_C()
    C_out = run_bass_kernel_spmd(nC, host_prep_C(inputs, A_out, y),
                                 core_ids=core_ids).results
    nD = build_D()
    D_out = run_bass_kernel_spmd(nD, host_prep_D(inputs, C_out),
                                 core_ids=core_ids).results
    nE = build_E()
    E_out = run_bass_kernel_spmd(nE, host_prep_E(inputs, C_out, D_out),
                                 core_ids=core_ids).results
    nF = build_F()
    F_out = run_bass_kernel_spmd(nF, host_prep_F(inputs, E_out),
                                 core_ids=core_ids).results
    nG = build_G()
    G_out = run_bass_kernel_spmd(nG, host_prep_G(F_out, E_out, C_out),
                                 core_ids=core_ids).results
    out = np.empty((B_, T_, C_), np.float32)
    for c in range(NCORE):
        b, t0 = c // 4, (c % 4) * TPC
        out[b, t0:t0 + TPC] = G_out[c]["outo"].T
    return out


# version 04
# speedup vs baseline: 1.2171x; 1.2171x over previous
from contextlib import ExitStack
"""Raw-Bass TRN2 kernel for mk_recurrent_d2 (depth-2: multi-branch causal
attention + gated linear FFN, then Mamba2 SSD + soft-MoE FFN).

Layout convention: FEATURE-MAJOR on chip — activations live as
[features on partitions, tokens on free dim]; host pre-transposes inputs /
weights (pure layout) and post-transposes outputs.

Sharding: 8 cores; cores 0-3 own batch 0, cores 4-7 batch 1; within a batch
each core owns a contiguous 2048-token slice (launches A/C/E/G), attention
(launch B) is sharded by balanced (head, q-tile) slots, SSD (launch D) by
(batch, head).

v0 is a multi-launch chain with host resharding between launches (pure
numpy layout ops). Collectives / remote-DMA are broken in this container
(fail at NEFF load), so cross-core exchanges go through launch boundaries.
"""

import math
import numpy as np
import ml_dtypes
import concourse.bass as bass
import concourse.mybir as mybir
from concourse.bass_utils import run_bass_kernel_spmd

F32 = mybir.dt.float32
BF16 = mybir.dt.bfloat16
AF = mybir.ActivationFunctionType
ALU = mybir.AluOpType
AX = mybir.AxisListType.X
BF = ml_dtypes.bfloat16

B_, T_, C_ = 2, 8192, 1024
H, HD = 16, 64
NHYP0, NHYP1 = 6, 10
NCORE = 8
TPC = 2048
LN_EPS = 1e-5
NEG_BIG = -1e30
LN8 = math.log(8.0)
LNI8 = math.log(0.125)
CPOOL = {LN_EPS: 0, 1e-30: 1, 1e-24: 2, LN8: 3, LNI8: 4}

FORM_OFF = []
FORM_W = []
_o = 0
for _h in range(H):
    _w = 65 if NHYP0 <= _h < NHYP1 else 64
    FORM_OFF.append(_o)
    FORM_W.append(_w)
    _o += _w
FORM_ROWS = _o


def is_hyp(h):
    return NHYP0 <= h < NHYP1


def hyp_idx(h):
    return h - NHYP0


def rotary_tables():
    inv_freq = (1.0 / 10000.0 ** (np.arange(0, HD, 2, dtype=np.float32) / HD)
                ).astype(np.float32)
    t = np.arange(T_, dtype=np.float32)
    freqs = np.outer(t, inv_freq).astype(np.float32)
    return np.cos(freqs).T.copy(), np.sin(freqs).T.copy()


def pooled_select():
    m = np.zeros((C_, 64), dtype=np.float32)
    for d in range(C_):
        m[d, d % 64] = 1.0
    return m


class Sem:
    def __init__(self, handle):
        self.h = handle
        self.n = 0

    def inc(self, instr, k=1):
        instr.then_inc(self.h, k)
        self.n += k
        return self.n


def mk_identity(g, ident):
    g.memset(ident, 1.0)
    g.affine_select(out=ident, in_=ident, compare_op=ALU.is_equal, fill=0.0,
                    base=0, pattern=[[-1, ident.shape[1]]], channel_multiplier=1)


def barrier_maker(engines, sem):
    state = {"k": 0}

    def bar():
        state["k"] += 1
        tgt = state["k"] * len(engines)
        for e in engines:
            e.nop().then_inc(sem.h, 1)
        for e in engines:
            e.wait_ge(sem.h, tgt)
        sem.n = tgt
    return bar


A_STOP = 99


def build_A():
    nc = bass.Bass("TRN2", target_bir_lowering=False, debug=False)
    xT = nc.dram_tensor("xT", [C_, TPC], F32, kind="ExternalInput").ap()
    qkvwT = nc.dram_tensor("qkvwT", [C_, 3 * C_], F32, kind="ExternalInput").ap()
    ffnwT = nc.dram_tensor("ffnwT", [C_, C_], F32, kind="ExternalInput").ap()
    disha = nc.dram_tensor("disha", [64, C_], F32, kind="ExternalInput").ap()
    psel = nc.dram_tensor("psel", [C_, 64], F32, kind="ExternalInput").ap()
    trigC = nc.dram_tensor("trigC", [128, TPC], BF16, kind="ExternalInput").ap()
    trigS = nc.dram_tensor("trigS", [128, TPC], BF16, kind="ExternalInput").ap()
    curv = nc.dram_tensor("curv", [1, 4], F32, kind="ExternalInput").ap()

    kform = nc.dram_tensor("kform", [FORM_ROWS, TPC], BF16, kind="ExternalOutput").ap()
    qform = nc.dram_tensor("qform", [FORM_ROWS, TPC], BF16, kind="ExternalOutput").ap()
    vtok = nc.dram_tensor("vtok", [TPC, C_], BF16, kind="ExternalOutput").ap()
    qnrm = nc.dram_tensor("qnrm", [12, TPC], F32, kind="ExternalOutput").ap()
    knrm = nc.dram_tensor("knrm", [12, TPC], F32, kind="ExternalOutput").ap()
    g0 = nc.dram_tensor("g0", [C_, TPC], F32, kind="ExternalOutput").ap()

    NT = TPC // 512

    NT = TPC // 512

    with ExitStack() as _st, nc.Block() as blk:
        xn_ = _st.enter_context(nc.sbuf_tensor("xn", [128, 8 * TPC], BF16))
        qh_ = _st.enter_context(nc.sbuf_tensor("qhat", [128, 8 * TPC], BF16))
        kh_ = _st.enter_context(nc.sbuf_tensor("khat", [128, 8 * TPC], BF16))
        wbf = _st.enter_context(nc.sbuf_tensor("wbf", [128, 8 * 512], BF16))
        stg = _st.enter_context(nc.sbuf_tensor("stg", [128, 2 * 512], F32))
        stgb = _st.enter_context(nc.sbuf_tensor("stgb", [128, 4 * 512], BF16))
        sqb = _st.enter_context(nc.sbuf_tensor("sqb", [128, TPC], BF16))
        trig_c = _st.enter_context(nc.sbuf_tensor("trig_c", [128, TPC], BF16))
        trig_s = _st.enter_context(nc.sbuf_tensor("trig_s", [128, TPC], BF16))
        qswb = _st.enter_context(nc.sbuf_tensor("qsw", [128, TPC], BF16))
        formstg = _st.enter_context(nc.sbuf_tensor("formstg", [128, TPC], BF16))
        pooledS = _st.enter_context(nc.sbuf_tensor("pooledS", [64, TPC], BF16))
        selA = _st.enter_context(nc.sbuf_tensor("selA", [1, 128], BF16))
        selB1 = _st.enter_context(nc.sbuf_tensor("selB1", [1, 128], BF16))
        ones128b = _st.enter_context(nc.sbuf_tensor("ones128b", [128, 1], BF16))
        curv_t = _st.enter_context(nc.sbuf_tensor("curv_t", [1, 4], F32))
        invc_t = _st.enter_context(nc.sbuf_tensor("invc_t", [1, 4], F32))
        onesc = _st.enter_context(nc.sbuf_tensor("onesc", [128, 1], F32))
        ones1 = _st.enter_context(nc.sbuf_tensor("ones1", [1, 128], F32))
        onesc_b = _st.enter_context(nc.sbuf_tensor("onesc_b", [64, 1], BF16))
        identb = _st.enter_context(nc.sbuf_tensor("identb", [128, 128], BF16))
        rows_ = _st.enter_context(nc.sbuf_tensor("rows", [1, 5 * TPC], F32))
        cols_ = _st.enter_context(nc.sbuf_tensor("cols", [128, 32], F32))
        cpool = _st.enter_context(nc.sbuf_tensor("cpool", [128, 8], F32))
        _s_ld = _st.enter_context(nc.semaphore("s_ld"))
        _s_tv = _st.enter_context(nc.semaphore("s_tv"))
        _s_vt = _st.enter_context(nc.semaphore("s_vt"))
        _s_sc = _st.enter_context(nc.semaphore("s_sc"))
        _s_gp = _st.enter_context(nc.semaphore("s_gp"))
        _s_out = _st.enter_context(nc.semaphore("s_out"))
        _s_bar = _st.enter_context(nc.semaphore("s_bar"))
        s_ld, s_tv, s_vt = Sem(_s_ld), Sem(_s_tv), Sem(_s_vt)
        s_sc, s_gp, s_out = Sem(_s_sc), Sem(_s_gp), Sem(_s_out)
        s_bar = Sem(_s_bar)
        ENGS = ("sync", "tensor", "vector", "scalar", "gpsimd")

        def sec(name, f):
            getattr(blk, name)(f)

        def bar():
            for e in ENGS:
                sec(e, lambda en: en.nop().then_inc(_s_bar, 1))
            s_bar.n += 5
            tgt = s_bar.n
            for e in ENGS:
                sec(e, lambda en, tgt=tgt: en.wait_ge(_s_bar, tgt))

        def _finish(nc):
            return nc

        def xn(d):
            return xn_[:, d * TPC:(d + 1) * TPC]

        def qh(d):
            return qh_[:, d * TPC:(d + 1) * TPC]

        def kh(d):
            return kh_[:, d * TPC:(d + 1) * TPC]

        def row(i):
            return rows_[:, i * TPC:(i + 1) * TPC]

        def col(i):
            return cols_[:, i:i + 1]

        def cbias(v, parts=1):
            return cpool[0:parts, CPOOL[v]:CPOOL[v] + 1]

        s_ch = Sem(_s_out)

        class _Rec:

            def __init__(self, e):
                self._e = e
                self.last = None

            def __getattr__(self, k):
                a = getattr(self._e, k)
                if callable(a):
                    def w(*args, **kw):
                        r = a(*args, **kw)
                        if hasattr(r, "then_inc"):
                            self.last = r
                        return r
                    return w
                return a

        def step(eng, fn):
            cur = s_ch.n

            def _f(e, fn=fn, cur=cur):
                e.wait_ge(_s_out, cur)
                rec = _Rec(e)
                fn(rec)
                (rec.last if rec.last is not None else e.nop()).then_inc(_s_out, 1)
            sec(eng, _f)
            s_ch.n += 1

        def dstep(eng, fn, n=1):
            cur = s_ch.n

            def _f(e, fn=fn, cur=cur):
                e.wait_ge(_s_out, cur)
                fn(e, lambda ins: ins.then_inc(_s_out, 16))
            sec(eng, _f)
            s_ch.n += 16 * n

        def _s0(s, mark):
            mark(s.dma_start(out=trig_c[:], in_=trigC[:]))
            mark(s.dma_start(out=trig_s[:], in_=trigS[:]))
            mark(s.dma_start(out=curv_t[:], in_=curv[:]))
        dstep("sync", _s0, 3)

        def _g0a(g):
            g.memset(identb[:], 1.0)
            g.memset(onesc[:], 1.0)
            g.memset(ones1[:], 1.0)
            g.memset(onesc_b[:], 1.0)
            g.memset(ones128b[:], 1.0)
            g.memset(selA[:, 0:64], 1.0)
            g.memset(selA[:, 64:128], 0.0)
            g.memset(selB1[:, 0:64], 0.0)
            g.memset(selB1[:, 64:128], 1.0)
            for _v, _i in CPOOL.items():
                g.memset(cpool[:, _i:_i + 1], _v)
        step("gpsimd", _g0a)

        def _g0b(g):
            g.affine_select(out=identb[:], in_=identb[:], compare_op=ALU.is_equal,
                            fill=0.0, base=0, pattern=[[-1, 128]],
                            channel_multiplier=1)
        step("gpsimd", _g0b)

        _cm_xstg = nc.sbuf_tensor("xstg", [128, 2 * TPC], F32)
        _cm_sqstg = nc.sbuf_tensor("sqstg", [128, 2 * 512], F32)
        xstg = _cm_xstg.__enter__()
        sqstg = _cm_sqstg.__enter__()
        _cm_psa = nc.psum_tensor("ps_a", [1, TPC], F32)
        _cm_psb = nc.psum_tensor("ps_b", [1, TPC], F32)
        ps_a = _cm_psa.__enter__()
        ps_b = _cm_psb.__enter__()
        if True:
            for d in range(8):
                sl = xstg[:, (d % 2) * TPC:(d % 2 + 1) * TPC]

                def _ld(s, mark, d=d, sl=sl):
                    mark(s.dma_start(out=sl, in_=xT[d * 128:(d + 1) * 128, :]))
                dstep("sync", _ld, 1)

                def _sq(v, sl=sl):
                    v.tensor_mul(sqstg[:, 0:512], sl[:, 0:512], sl[:, 0:512])
                    v.tensor_mul(sqstg[:, 512:1024], sl[:, 512:1024], sl[:, 512:1024])
                step("vector", _sq)

                def _mm1(t, d=d, sl=sl):
                    for j in range(2):
                        t.matmul(ps_a[:, j * 512:(j + 1) * 512], onesc[:],
                                 sl[:, j * 512:(j + 1) * 512],
                                 start=(d == 0), stop=(d == 7))
                        t.matmul(ps_b[:, j * 512:(j + 1) * 512], onesc[:],
                                 sqstg[:, j * 512:(j + 1) * 512],
                                 start=(d == 0), stop=(d == 7))
                step("tensor", _mm1)

                def _sq2(v, sl=sl):
                    v.tensor_mul(sqstg[:, 0:512], sl[:, 1024:1536], sl[:, 1024:1536])
                    v.tensor_mul(sqstg[:, 512:1024], sl[:, 1536:2048], sl[:, 1536:2048])
                step("vector", _sq2)

                def _mm2(t, d=d, sl=sl):
                    for j in range(2):
                        t.matmul(ps_a[:, (2 + j) * 512:(3 + j) * 512], onesc[:],
                                 sl[:, (2 + j) * 512:(3 + j) * 512],
                                 start=(d == 0), stop=(d == 7))
                        t.matmul(ps_b[:, (2 + j) * 512:(3 + j) * 512], onesc[:],
                                 sqstg[:, j * 512:(j + 1) * 512],
                                 start=(d == 0), stop=(d == 7))
                step("tensor", _mm2)

            step("vector", lambda v: v.tensor_scalar_mul(row(0), ps_a[:], 1.0 / C_))
            step("vector", lambda v: v.tensor_mul(row(3), row(0), row(0)))
            step("vector", lambda v: v.tensor_scalar_mul(row(4), ps_b[:], 1.0 / C_))
            step("vector", lambda v: v.tensor_sub(row(4), row(4), row(3)))
            step("scalar", lambda sc: sc.activation(out=row(3), in_=row(4),
                                                    func=AF.Ln, bias=cbias(LN_EPS)))
            step("scalar", lambda sc: sc.activation(out=row(1), in_=row(3),
                                                    func=AF.Exp, scale=-0.5))
            step("vector", lambda v: v.scalar_tensor_tensor(
                out=row(2), in0=row(0), scalar=-1.0, in1=row(1),
                op0=ALU.mult, op1=ALU.mult))
        _cm_psb.__exit__(None, None, None)
        _cm_psa.__exit__(None, None, None)
        if A_STOP <= 1:
            return _finish(nc)

        _cm_pbc = nc.psum_tensor("pbc", [128, 4 * 512], F32)
        pbc = _cm_pbc.__enter__()
        if True:
            for j in range(NT):
                b = 2 * (j % 2)

                def _bc(t, j=j, b=b):
                    t.matmul(pbc[:, b * 512:(b + 1) * 512], ones1[:],
                             row(1)[:, j * 512:(j + 1) * 512], start=True, stop=True)
                    t.matmul(pbc[:, (b + 1) * 512:(b + 2) * 512], ones1[:],
                             row(2)[:, j * 512:(j + 1) * 512], start=True, stop=True)
                step("tensor", _bc)

                for d in range(8):
                    sl = xstg[:, (d % 2) * 512:(d % 2 + 1) * 512]

                    def _ld(s, mark, d=d, j=j, sl=sl):
                        mark(s.dma_start(out=sl, in_=xT[d * 128:(d + 1) * 128,
                                                        j * 512:(j + 1) * 512]))
                    dstep("sync", _ld, 1)

                    def _ap1(v, sl=sl, b=b):
                        v.tensor_mul(stg[:, 0:512], sl, pbc[:, b * 512:(b + 1) * 512])
                    step("vector", _ap1)

                    def _ap2(v, d=d, j=j, b=b):
                        v.tensor_add(xn(d)[:, j * 512:(j + 1) * 512], stg[:, 0:512],
                                     pbc[:, (b + 1) * 512:(b + 2) * 512])
                    step("vector", _ap2)
        _cm_pbc.__exit__(None, None, None)
        _cm_sqstg.__exit__(None, None, None)
        _cm_xstg.__exit__(None, None, None)
        _cm_w32 = nc.sbuf_tensor("w32", [128, 8 * 512], F32)
        w32 = _cm_w32.__enter__()
        if A_STOP <= 2:
            return _finish(nc)

        _cm_pbig = nc.psum_tensor("pbig", [128, 4 * 512], F32)
        _cm_ptr = nc.psum_tensor("ptr", [128, 4 * 128], BF16)
        pbig = _cm_pbig.__enter__()
        ptr = _cm_ptr.__enter__()
        if True:
            for cb in range(6):
                def _ld1(s, mark, cb=cb):
                    for d in range(8):
                        mark(s.dma_start(
                            out=w32[:, d * 512:(d + 1) * 512],
                            in_=qkvwT[d * 128:(d + 1) * 128,
                                      cb * 512:(cb + 1) * 512]))
                dstep("sync", _ld1, 8)
                step("vector", lambda v: v.tensor_copy(wbf[:], w32[:]))

                for c4 in range(4):
                    R0 = cb * 512 + c4 * 128
                    for j in range(NT):
                        slot = (c4 * NT + j) % 4
                        ps = pbig[:, slot * 512:(slot + 1) * 512]

                        def _mm(t, c4=c4, j=j, ps=ps):
                            for d in range(8):
                                t.matmul(ps, wbf[:, d * 512 + c4 * 128:
                                                 d * 512 + c4 * 128 + 128],
                                         xn(d)[:, j * 512:(j + 1) * 512],
                                         start=(d == 0), stop=(d == 7))
                        step("tensor", _mm)

                        if R0 < 2048:
                            dst = qh_ if R0 < 1024 else kh_
                            dq = (R0 % 1024) // 128

                            def _ev(v, dst=dst, dq=dq, j=j, ps=ps):
                                v.tensor_copy(
                                    dst[:, dq * TPC + j * 512:
                                        dq * TPC + (j + 1) * 512], ps)
                            step("vector", _ev)
                        else:
                            f0 = R0 - 2048

                            def _evv(v, ps=ps):
                                v.tensor_copy(stgb[:, 0:512], ps)
                            step("vector", _evv)

                            def _tr(t):
                                for i in range(4):
                                    t.transpose(ptr[:, i * 128:(i + 1) * 128],
                                                stgb[:, i * 128:(i + 1) * 128],
                                                identb[:])
                            step("tensor", _tr)

                            def _evt2(v):
                                for i in range(4):
                                    v.tensor_copy(
                                        stgb[:, 512 + i * 128:512 + (i + 1) * 128],
                                        ptr[:, i * 128:(i + 1) * 128])
                            step("vector", _evt2)

                            def _od(g, mark, f0=f0, j=j):
                                for i in range(4):
                                    mark(g.dma_start(
                                        out=vtok[j * 512 + i * 128:
                                                 j * 512 + (i + 1) * 128,
                                                 f0:f0 + 128],
                                        in_=stgb[:, 512 + i * 128:
                                                 512 + (i + 1) * 128]))
                            dstep("gpsimd", _od, 4)
        bar()
        if A_STOP <= 3:
            return _finish(nc)

        for which, buf in (("q", qh_), ("k", kh_)):
            for dq in range(8):
                ch = buf[:, dq * TPC:(dq + 1) * TPC]

                def _sw(s, mark, ch=ch):
                    mark(s.dma_start(out=qswb[0:32, :], in_=ch[32:64, :]))
                    mark(s.dma_start(out=qswb[32:64, :], in_=ch[0:32, :]))
                    mark(s.dma_start(out=qswb[64:96, :], in_=ch[96:128, :]))
                    mark(s.dma_start(out=qswb[96:128, :], in_=ch[64:96, :]))
                dstep("sync", _sw, 4)

                def _rp1(v, ch=ch):
                    v.tensor_mul(sqb[:], qswb[:], trig_s[:])
                    v.tensor_mul(stgb[:, 0:TPC], ch, trig_c[:])
                step("vector", _rp1)

                def _rp2(v, ch=ch):
                    v.tensor_add(ch, stgb[:, 0:TPC], sqb[:])
                step("vector", _rp2)
        bar()
        if A_STOP <= 4:
            return _finish(nc)


        step("scalar", lambda sc: sc.activation(out=invc_t[:], in_=curv_t[:],
                                                func=AF.Ln))
        step("scalar", lambda sc: sc.activation(out=invc_t[:], in_=invc_t[:],
                                                func=AF.Exp, scale=-1.0))

        _cm_ptr.__exit__(None, None, None)
        _cm_pbig.__exit__(None, None, None)
        _cm_psnr = nc.psum_tensor("psnr", [1, TPC], F32)
        _cm_pbcf = nc.psum_tensor("pbcf", [128, 512], F32)
        _cm_pcol = nc.psum_tensor("pcol", [128, 1], F32)
        psnr = _cm_psnr.__enter__()
        pbcf = _cm_pbcf.__enter__()
        pcol = _cm_pcol.__enter__()
        if True:
            for p in range(8):
                h0, h1 = 2 * p, 2 * p + 1
                qch = qh_[:, p * TPC:(p + 1) * TPC]
                kch = kh_[:, p * TPC:(p + 1) * TPC]
                for which, ch, r_even, r_odd in (
                        ("k", kch, 3, 4), ("q", qch, 0, 1)):
                    def _sq(v, ch=ch):
                        v.tensor_mul(sqb[:], ch, ch)
                    step("vector", _sq)
                    for m, rr in ((0, r_even), (1, r_odd)):
                        bpar = 64 * m

                        def _nm(t, bpar=bpar):
                            for j in range(NT):
                                t.matmul(psnr[:, j * 512:(j + 1) * 512],
                                         ones128b[bpar:bpar + 64, :],
                                         sqb[bpar:bpar + 64, j * 512:(j + 1) * 512],
                                         start=True, stop=True)
                        step("tensor", _nm)

                        def _ev(v, rr=rr):
                            v.tensor_copy(row(rr), psnr[:])
                        step("vector", _ev)

                if p < 3:
                    def _kf(v, kch=kch):
                        v.tensor_scalar_mul(formstg[:], kch, 0.125)
                    step("vector", _kf)

                    def _okf(g, mark, h0=h0, h1=h1):
                        mark(g.dma_start(out=kform[FORM_OFF[h0]:FORM_OFF[h0] + 64, :],
                                         in_=formstg[0:64, :]))
                        mark(g.dma_start(out=kform[FORM_OFF[h1]:FORM_OFF[h1] + 64, :],
                                         in_=formstg[64:128, :]))
                    dstep("gpsimd", _okf, 2)

                    def _oqf(g, mark, qch=qch, h0=h0, h1=h1):
                        mark(g.dma_start(out=qform[FORM_OFF[h0]:FORM_OFF[h0] + 64, :],
                                         in_=qch[0:64, :]))
                        mark(g.dma_start(out=qform[FORM_OFF[h1]:FORM_OFF[h1] + 64, :],
                                         in_=qch[64:128, :]))
                    dstep("gpsimd", _oqf, 2)

                    for rr, bias2, outt, oidx in (
                            (3, LNI8, knrm, h0), (4, LNI8, knrm, h1),
                            (0, 0.0, qnrm, h0), (1, 0.0, qnrm, h1)):
                        def _nra(sc, rr=rr):
                            sc.activation(out=row(2), in_=row(rr), func=AF.Ln,
                                          bias=cbias(1e-30))
                        step("scalar", _nra)

                        def _nrb(sc, bias2=bias2):
                            sc.activation(out=row(2), in_=row(2), func=AF.Exp,
                                          scale=0.5, bias=cbias(bias2) if bias2 else 0.0)
                        step("scalar", _nrb)

                        def _onr(g, mark, outt=outt, oidx=oidx):
                            mark(g.dma_start(out=outt[oidx:oidx + 1, :],
                                             in_=row(2)))
                        dstep("gpsimd", _onr, 1)

                elif p < 5:
                    i0 = h0 - 6

                    def _nga(v, i0=i0):
                        v.tensor_scalar_mul(stg[0:1, 0:2], curv_t[:, i0:i0 + 2], -1.0)
                    step("vector", _nga)
                    step("vector", lambda v: v.tensor_copy(stgb[0:1, 0:2],
                                                           stg[0:1, 0:2]))

                    def _bc(t, i0=i0):
                        t.matmul(pcol[:], selA[:], stgb[0:1, 0:1], start=True,
                                 stop=False)
                        t.matmul(pcol[:], selB1[:], stgb[0:1, 1:2], start=False,
                                 stop=True)
                    step("tensor", _bc)

                    def _evc(v):
                        v.tensor_copy(cols_[:, 0:1], pcol[:])
                    step("vector", _evc)

                    def _kf(v, kch=kch):
                        v.tensor_scalar(out=formstg[:], in0=kch,
                                        scalar1=cols_[:, 0:1], scalar2=None,
                                        op0=ALU.mult)
                    step("vector", _kf)

                    def _okf(g, mark, h0=h0, h1=h1):
                        mark(g.dma_start(out=kform[FORM_OFF[h0]:FORM_OFF[h0] + 64, :],
                                         in_=formstg[0:64, :]))
                        mark(g.dma_start(out=kform[FORM_OFF[h1]:FORM_OFF[h1] + 64, :],
                                         in_=formstg[64:128, :]))
                    dstep("gpsimd", _okf, 2)

                    def _oqf(g, mark, qch=qch, h0=h0, h1=h1):
                        mark(g.dma_start(out=qform[FORM_OFF[h0]:FORM_OFF[h0] + 64, :],
                                         in_=qch[0:64, :]))
                        mark(g.dma_start(out=qform[FORM_OFF[h1]:FORM_OFF[h1] + 64, :],
                                         in_=qch[64:128, :]))
                    dstep("gpsimd", _oqf, 2)

                    for rr, ci, outt, hh, scale_c in (
                            (3, i0, kform, h0, True), (4, i0 + 1, kform, h1, True),
                            (0, i0, qform, h0, False), (1, i0 + 1, qform, h1, False)):
                        def _tka(sc, rr=rr, ci=ci):
                            sc.activation(out=row(2), in_=row(rr), func=AF.Ln,
                                          bias=invc_t[:, ci:ci + 1])
                        step("scalar", _tka)
                        step("scalar", lambda sc: sc.activation(
                            out=row(2), in_=row(2), func=AF.Exp, scale=0.5))

                        if scale_c:
                            def _sc_c(v, ci=ci):
                                v.tensor_scalar(out=stgb[0:1, 0:TPC], in0=row(2),
                                                scalar1=curv_t[:, ci:ci + 1],
                                                scalar2=None, op0=ALU.mult)
                            step("vector", _sc_c)
                        else:
                            def _sc_c(v):
                                v.tensor_copy(stgb[0:1, 0:TPC], row(2))
                            step("vector", _sc_c)

                        def _otk(g, mark, outt=outt, hh=hh):
                            mark(g.dma_start(
                                out=outt[FORM_OFF[hh] + 64:FORM_OFF[hh] + 65, :],
                                in_=stgb[0:1, 0:TPC]))
                        dstep("gpsimd", _otk, 1)

                else:
                    for which, ch, r_even, r_odd, outt in (
                            ("k", kch, 3, 4, kform), ("q", qch, 0, 1, qform)):
                        bias2 = LN8 if which == "k" else 0.0

                        def _rna(sc, r_even=r_even):
                            sc.activation(out=row(2), in_=row(r_even), func=AF.Ln,
                                          bias=cbias(1e-24))
                        step("scalar", _rna)

                        def _rnb(sc, bias2=bias2):
                            sc.activation(out=row(2), in_=row(2), func=AF.Exp,
                                          scale=-0.5, bias=cbias(bias2) if bias2 else 0.0)
                        step("scalar", _rnb)

                        def _cv1(v):
                            v.tensor_copy(stgb[0:1, 0:TPC], row(2))
                        step("vector", _cv1)

                        def _rn2a(sc, r_odd=r_odd):
                            sc.activation(out=row(2), in_=row(r_odd), func=AF.Ln,
                                          bias=cbias(1e-24))
                        step("scalar", _rn2a)

                        def _rn2b(sc, bias2=bias2):
                            sc.activation(out=row(2), in_=row(2), func=AF.Exp,
                                          scale=-0.5, bias=cbias(bias2) if bias2 else 0.0)
                        step("scalar", _rn2b)

                        def _cv2(v):
                            v.tensor_copy(sqb[0:1, 0:TPC], row(2))
                        step("vector", _cv2)

                        for j in range(NT):
                            def _bc(t, j=j):
                                t.matmul(pbcf[:], selA[:],
                                         stgb[0:1, j * 512:(j + 1) * 512],
                                         start=True, stop=False)
                                t.matmul(pbcf[:], selB1[:],
                                         sqb[0:1, j * 512:(j + 1) * 512],
                                         start=False, stop=True)
                            step("tensor", _bc)

                            def _mlt(v, ch=ch, j=j):
                                v.tensor_mul(formstg[:, j * 512:(j + 1) * 512],
                                             ch[:, j * 512:(j + 1) * 512], pbcf[:])
                            step("vector", _mlt)

                        def _of(g, mark, outt=outt, h0=h0, h1=h1):
                            mark(g.dma_start(out=outt[FORM_OFF[h0]:FORM_OFF[h0] + 64, :],
                                             in_=formstg[0:64, :]))
                            mark(g.dma_start(out=outt[FORM_OFF[h1]:FORM_OFF[h1] + 64, :],
                                             in_=formstg[64:128, :]))
                        dstep("gpsimd", _of, 2)

                    def _c8(g, h0=h0, h1=h1):
                        g.memset(row(2), 8.0)
                    step("gpsimd", _c8)

                    def _oc8(g, mark, h0=h0, h1=h1):
                        mark(g.dma_start(out=knrm[h0 - 4:h0 - 3, :], in_=row(2)))
                        mark(g.dma_start(out=knrm[h1 - 4:h1 - 3, :], in_=row(2)))
                    dstep("gpsimd", _oc8, 2)

                    def _c1(g):
                        g.memset(row(2), 1.0)
                    step("gpsimd", _c1)

                    def _oc1(g, mark, h0=h0, h1=h1):
                        mark(g.dma_start(out=qnrm[h0 - 4:h0 - 3, :], in_=row(2)))
                        mark(g.dma_start(out=qnrm[h1 - 4:h1 - 3, :], in_=row(2)))
                    dstep("gpsimd", _oc1, 2)
        bar()
        if A_STOP <= 5:
            return _finish(nc)

        _cm_pcol.__exit__(None, None, None)
        _cm_pbcf.__exit__(None, None, None)
        _cm_psnr.__exit__(None, None, None)
        _cm_pbig2 = nc.psum_tensor("pbig2", [128, 4 * 512], F32)
        _cm_ppool = nc.psum_tensor("ppool", [64, 512], F32)
        pbig2 = _cm_pbig2.__enter__()
        ppool = _cm_ppool.__enter__()
        if True:
            def _lp(s, mark):
                for d in range(8):
                    mark(s.dma_start(out=w32[:, d * 64:(d + 1) * 64],
                                     in_=psel[d * 128:(d + 1) * 128, :]))
            dstep("sync", _lp, 8)

            def _cp(v):
                for d in range(8):
                    v.tensor_copy(wbf[:, d * 512:d * 512 + 64],
                                  w32[:, d * 64:(d + 1) * 64])
            step("vector", _cp)

            for j in range(NT):
                def _pm(t, j=j):
                    for d in range(8):
                        t.matmul(ppool[:], wbf[:, d * 512:d * 512 + 64],
                                 xn(d)[:, j * 512:(j + 1) * 512],
                                 start=(d == 0), stop=(d == 7))
                step("tensor", _pm)

                def _pe(v, j=j):
                    v.tensor_copy(pooledS[:, j * 512:(j + 1) * 512], ppool[:])
                step("vector", _pe)

            def _ldd(s, mark):
                mark(s.dma_start(out=stg[0:64, 0:1024], in_=disha[:]))
            dstep("sync", _ldd, 1)

            def _cdd(v):
                v.tensor_copy(formstg[0:64, 0:1024], stg[0:64, 0:1024])
            step("vector", _cdd)

            for cb in range(2):
                def _lw(s, mark, cb=cb):
                    for d in range(8):
                        mark(s.dma_start(
                            out=w32[:, d * 512:(d + 1) * 512],
                            in_=ffnwT[d * 128:(d + 1) * 128,
                                      cb * 512:(cb + 1) * 512]))
                dstep("sync", _lw, 8)
                step("vector", lambda v: v.tensor_copy(wbf[:], w32[:]))

                for c4 in range(4):
                    R0 = cb * 512 + c4 * 128
                    for j in range(NT):
                        slot = (c4 * NT + j) % 4
                        ps = pbig2[:, slot * 512:(slot + 1) * 512]

                        def _mm(t, c4=c4, j=j, ps=ps, R0=R0):
                            for d in range(8):
                                t.matmul(ps, wbf[:, d * 512 + c4 * 128:
                                                 d * 512 + c4 * 128 + 128],
                                         xn(d)[:, j * 512:(j + 1) * 512],
                                         start=(d == 0), stop=False)
                            t.matmul(ps, formstg[0:64, R0:R0 + 128],
                                     pooledS[:, j * 512:(j + 1) * 512],
                                     start=False, stop=True)
                        step("tensor", _mm)

                        def _sl(sc, ps=ps):
                            sc.activation(out=stg[:, 0:512], in_=ps, func=AF.Silu)
                        step("scalar", _sl)

                        def _og(g, mark, R0=R0, j=j):
                            mark(g.dma_start(out=g0[R0:R0 + 128,
                                                    j * 512:(j + 1) * 512],
                                             in_=stg[:, 0:512]))
                        dstep("gpsimd", _og, 1)

        fin = s_ch.n

        def _fin(g, fin=fin):
            g.wait_ge(_s_out, fin)
        sec("gpsimd", _fin)
        _cm_w32.__exit__(None, None, None)

    return nc


def host_prep_A(inputs):
    xT = np.ascontiguousarray(
        inputs["L0_attn_x"].reshape(B_ * T_, C_).T)
    qkvwT = np.ascontiguousarray(inputs["L0_attn_qkv_w"].T)
    ffnwT = np.ascontiguousarray(inputs["L0_ffn_weight"].T)
    disha = np.ascontiguousarray(inputs["L0_ffn_disha"])
    psel = pooled_select()
    cos32, sin32 = rotary_tables()
    tc = np.empty((128, T_), np.float32)
    ts = np.empty((128, T_), np.float32)
    for hh in range(2):
        b = hh * 64
        tc[b:b + 32] = cos32
        tc[b + 32:b + 64] = cos32
        ts[b:b + 32] = sin32
        ts[b + 32:b + 64] = -sin32
    curv = inputs["L0_attn_curvature"].reshape(1, 4).astype(np.float32)
    maps = []
    for c in range(NCORE):
        t0 = c * TPC
        tl = t0 % T_
        maps.append({
            "xT": np.ascontiguousarray(xT[:, t0:t0 + TPC]),
            "qkvwT": qkvwT, "ffnwT": ffnwT, "disha": disha, "psel": psel,
            "trigC": np.ascontiguousarray(tc[:, tl:tl + TPC]).astype(BF),
            "trigS": np.ascontiguousarray(ts[:, tl:tl + TPC]).astype(BF),
            "curv": curv,
        })
    return maps


PLAIN_HEADS = list(range(6)) + list(range(10, 16))
HYP_HEADS = list(range(6, 10))
NQT = T_ // 512


def build_B():
    nc = bass.Bass("TRN2", target_bir_lowering=False, debug=False)
    kp = nc.dram_tensor("kp", [3, 64, T_], BF16, kind="ExternalInput").ap()
    qp = nc.dram_tensor("qp", [3, 64, T_], BF16, kind="ExternalInput").ap()
    vp = nc.dram_tensor("vp", [3, 128, 64 * 65], BF16, kind="ExternalInput").ap()
    knr = nc.dram_tensor("knr", [3, T_], F32, kind="ExternalInput").ap()
    qnr = nc.dram_tensor("qnr", [3, T_], F32, kind="ExternalInput").ap()
    khy = nc.dram_tensor("khy", [65, T_], BF16, kind="ExternalInput").ap()
    qhy = nc.dram_tensor("qhy", [65, T_], BF16, kind="ExternalInput").ap()
    vhy = nc.dram_tensor("vhy", [128, 64 * 65], BF16, kind="ExternalInput").ap()
    hcur = nc.dram_tensor("hcur", [1, 1], F32, kind="ExternalInput").ap()
    yp = nc.dram_tensor("yp", [3, 64, T_], BF16, kind="ExternalOutput").ap()
    yh = nc.dram_tensor("yh", [64, T_], BF16, kind="ExternalOutput").ap()
    mscr = nc.dram_tensor("mscr", [128, 1], F32).ap()

    with ExitStack() as _st, nc.Block() as blk:
        ku = _st.enter_context(nc.sbuf_tensor("ku", [64, T_], BF16))
        qu = _st.enter_context(nc.sbuf_tensor("qu", [64, T_], BF16))
        vu = _st.enter_context(nc.sbuf_tensor("vu", [128, 64 * 65], BF16))
        khu = _st.enter_context(nc.sbuf_tensor("khu", [65, T_], BF16))
        qhu = _st.enter_context(nc.sbuf_tensor("qhu", [65, T_], BF16))
        knru = _st.enter_context(nc.sbuf_tensor("knru", [1, T_], F32))
        qnru = _st.enter_context(nc.sbuf_tensor("qnru", [1, T_], F32))
        scf = _st.enter_context(nc.sbuf_tensor("scf", [128, 4096], F32))
        scg = _st.enter_context(nc.sbuf_tensor("scg", [128, 4096], F32))
        sch = _st.enter_context(nc.sbuf_tensor("sch", [128, 4096], F32))
        pb2 = _st.enter_context(nc.sbuf_tensor("pb2", [128, 2 * 1024], BF16))
        pbw = _st.enter_context(nc.sbuf_tensor("pbw", [128, 4096], BF16))
        yf = _st.enter_context(nc.sbuf_tensor("yf", [64, 512], F32))
        yo = _st.enter_context(nc.sbuf_tensor("yo", [64, 512], BF16))
        mcol = _st.enter_context(nc.sbuf_tensor("mcol", [128, 8], F32))
        mrow = _st.enter_context(nc.sbuf_tensor("mrow", [1, 128], F32))
        rowf = _st.enter_context(nc.sbuf_tensor("rowf", [1, 512], F32))
        rowb = _st.enter_context(nc.sbuf_tensor("rowb", [1, 512], BF16))
        sca = _st.enter_context(nc.sbuf_tensor("sca", [1, 8], F32))
        acol = _st.enter_context(nc.sbuf_tensor("acol", [128, 1], F32))
        ones1 = _st.enter_context(nc.sbuf_tensor("ones1", [1, 128], BF16))
        cpool = _st.enter_context(nc.sbuf_tensor("cpool", [128, 8], F32))
        psS = _st.enter_context(nc.psum_tensor("psS", [128, 2048], F32))
        psO = _st.enter_context(nc.psum_tensor("psO", [65, 512], F32))
        psB = _st.enter_context(nc.psum_tensor("psB", [128, 512], F32))
        psC = _st.enter_context(nc.psum_tensor("psC", [128, 1], F32))
        _s_out = _st.enter_context(nc.semaphore("s_out"))
        _s_qk = _st.enter_context(nc.semaphore("s_qk"))
        _s_pr = _st.enter_context(nc.semaphore("s_pr"))
        _s_pv = _st.enter_context(nc.semaphore("s_pv"))

        CP = {LN_EPS: 0, -1.0: 1, 1e-6: 2}

        def cbias(v, parts=1):
            return cpool[0:parts, CP[v]:CP[v] + 1]

        def sec(name, f):
            getattr(blk, name)(f)

        s_ch = Sem(_s_out)
        qk_n = [0]
        pr_n = [0]
        pv_n = [0]

        class _Rec:
            def __init__(self, e):
                self._e = e
                self.last = None

            def __getattr__(self, k):
                a = getattr(self._e, k)
                if callable(a):
                    def w(*args, **kw):
                        r = a(*args, **kw)
                        if hasattr(r, "then_inc"):
                            self.last = r
                        return r
                    return w
                return a

        def step(eng, fn):
            cur = s_ch.n

            def _f(e, fn=fn, cur=cur):
                e.wait_ge(_s_out, cur)
                rec = _Rec(e)
                fn(rec)
                (rec.last if rec.last is not None else e.nop()).then_inc(_s_out, 1)
            sec(eng, _f)
            s_ch.n += 1

        def dstep(eng, fn, n=1):
            cur = s_ch.n

            def _f(e, fn=fn, cur=cur):
                e.wait_ge(_s_out, cur)
                fn(e, lambda ins: ins.then_inc(_s_out, 16))
            sec(eng, _f)
            s_ch.n += 16 * n

        FR = {}

        def _c0(g):
            g.memset(ones1[:], 1.0)
            for _v, _i in CP.items():
                g.memset(cpool[:, _i:_i + 1], _v)
            FR["f"] = g.alloc_register("fillneg")
            g.reg_mov(FR["f"], NEG_BIG)
            FR["p"] = g.alloc_register("fillpos")
            g.reg_mov(FR["p"], -NEG_BIG)
        step("gpsimd", _c0)

        def _lc(s, mark):
            mark(s.dma_start(out=sca[:, 0:1], in_=hcur[:]))
        dstep("sync", _lc, 1)
        step("scalar", lambda sc: sc.activation(out=sca[:, 1:2], in_=sca[:, 0:1],
                                                func=AF.Ln))
        step("scalar", lambda sc: sc.activation(out=sca[:, 1:2], in_=sca[:, 1:2],
                                                func=AF.Exp, scale=-0.5))
        step("vector", lambda v: v.tensor_copy(rowb[:, 0:1], sca[:, 1:2]))
        step("tensor", lambda t: t.matmul(psC[:], ones1[:], rowb[:, 0:1],
                                          start=True, stop=True))
        step("vector", lambda v: v.tensor_copy(acol[:], psC[:]))

        for u in range(3):
            def _lu(s, mark, u=u):
                mark(s.dma_start(out=ku[:], in_=kp[u]))
                mark(s.dma_start(out=qu[:], in_=qp[u]))
                mark(s.dma_start(out=vu[:], in_=vp[u]))
                mark(s.dma_start(out=knru[:], in_=knr[u:u + 1, :]))
                mark(s.dma_start(out=qnru[:], in_=qnr[u:u + 1, :]))
            dstep("sync", _lu, 5)

            for t in range(NQT):
                q0 = t * 512
                bound = 4 * (t + 1)
                step("vector", lambda v, t=t: v.reduce_max(
                    sca[:, 2:3], qnru[:, t * 512:(t + 1) * 512], axis=AX))
                step("vector", lambda v, bound=bound: v.reduce_max(
                    sca[:, 3:4], knru[:, 0:bound * 128], axis=AX))
                step("vector", lambda v: v.tensor_mul(sca[:, 4:5], sca[:, 2:3],
                                                      sca[:, 3:4]))
                step("vector", lambda v: v.tensor_scalar_mul(sca[:, 5:6],
                                                             sca[:, 4:5], -1.0))
                step("vector", lambda v: v.tensor_copy(rowb[:, 0:1], sca[:, 5:6]))
                step("tensor", lambda t_: t_.matmul(psC[:], ones1[:], rowb[:, 0:1],
                                                    start=True, stop=True))
                step("vector", lambda v: v.tensor_copy(mcol[:, 0:1], psC[:]))
                ch_gate = s_ch.n

                NG = 2 * (t + 1)
                npipe = NG - 2

                def qk_group(g, q0=q0):
                    par = g % 2
                    qn0 = qk_n[0]
                    prn = pr_n[0]

                    def _f(t_, g=g, par=par, q0=q0, qn0=qn0):
                        t_.wait_ge(_s_out, ch_gate)
                        if g >= 2:
                            t_.wait_ge(_s_pr, prn)
                        for i in range(2):
                            kc = g * 2 + i
                            ins = t_.matmul(psS[:, par * 1024 + i * 512:
                                                par * 1024 + (i + 1) * 512],
                                            ku[:, kc * 128:(kc + 1) * 128],
                                            qu[:, q0:q0 + 512],
                                            start=True, stop=True)
                        ins.then_inc(_s_qk, 1)
                    sec("tensor", _f)
                    qk_n[0] += 1

                def exp_group(g):
                    par = g % 2
                    qn = qk_n[0]
                    pvn = pv_n[0]

                    def _f(sc, g=g, par=par, qn=qn, pvn=pvn):
                        sc.wait_ge(_s_qk, qn)
                        if g >= 2:
                            sc.wait_ge(_s_pv, pvn - 1)
                        sc.activation(out=pb2[:, par * 1024:(par + 1) * 1024],
                                      in_=psS[:, par * 1024:(par + 1) * 1024],
                                      func=AF.Exp, bias=mcol[:, 0:1]
                                      ).then_inc(_s_pr, 1)
                    sec("scalar", _f)
                    pr_n[0] += 1

                def pv_group(g, t=t, first=False):
                    par = g % 2
                    prn = pr_n[0]

                    def _f(t_, g=g, par=par, prn=prn, first=first):
                        t_.wait_ge(_s_pr, prn)
                        for i in range(2):
                            kc = g * 2 + i
                            ins = t_.matmul(psO[:],
                                            vu[:, kc * 65:(kc + 1) * 65],
                                            pb2[:, par * 1024 + i * 512:
                                                par * 1024 + (i + 1) * 512],
                                            start=(first and i == 0), stop=False,
                                            skip_group_check=True)
                        ins.then_inc(_s_pv, 1)
                    sec("tensor", _f)
                    pv_n[0] += 1

                if npipe > 0:
                    qk_group(0)
                    for g in range(npipe):
                        if g + 1 < npipe:
                            qk_group(g + 1)
                        exp_group(g)
                        pv_group(g, first=(g == 0))
                pvn_done = pv_n[0]
                prn_done = pr_n[0]

                def _sync_pipe(t_, pvn_done=pvn_done):
                    t_.wait_ge(_s_pv, pvn_done)
                step("tensor", _sync_pipe)

                for kc in range(bound - 4, bound):
                    i4 = kc - (bound - 4)

                    def _qk(t_, kc=kc, q0=q0, i4=i4):
                        t_.matmul(psS[:, i4 * 512:(i4 + 1) * 512],
                                  ku[:, kc * 128:(kc + 1) * 128],
                                  qu[:, q0:q0 + 512], start=True, stop=True)
                    step("tensor", _qk)
                step("vector", lambda v: v.tensor_copy(scf[:, 0:2048], psS[:]))

                def _msk(g, q0=q0, bound=bound):
                    for i4 in range(4):
                        kc = bound - 4 + i4
                        g.affine_select(out=scf[:, i4 * 512:(i4 + 1) * 512],
                                        in_=scf[:, i4 * 512:(i4 + 1) * 512],
                                        compare_op=ALU.is_ge, fill=FR["f"],
                                        base=q0 - kc * 128, pattern=[[1, 512]],
                                        channel_multiplier=-1)
                step("gpsimd", _msk)
                step("scalar", lambda sc: sc.activation(
                    out=pbw[:, 0:2048], in_=scf[:, 0:2048], func=AF.Exp,
                    bias=mcol[:, 0:1]))

                def _pvd(t_, bound=bound, t=t):
                    for i4 in range(4):
                        kc = bound - 4 + i4
                        t_.matmul(psO[:], vu[:, kc * 65:(kc + 1) * 65],
                                  pbw[:, i4 * 512:(i4 + 1) * 512],
                                  start=(t == 0 and i4 == 0 and True and False)
                                  or (bound == 4 and i4 == 0),
                                  stop=(i4 == 3), skip_group_check=True)
                step("tensor", _pvd)

                step("scalar", lambda sc: sc.activation(out=rowf[:],
                                                        in_=psO[64:65, :],
                                                        func=AF.Ln))
                step("scalar", lambda sc: sc.activation(out=rowf[:], in_=rowf[:],
                                                        func=AF.Exp, scale=-1.0))
                step("vector", lambda v: v.tensor_copy(rowb[:], rowf[:]))
                step("tensor", lambda t_: t_.matmul(psB[:], ones1[:], rowb[:],
                                                    start=True, stop=True))
                step("vector", lambda v: v.tensor_copy(yf[:], psO[0:64, :]))
                step("vector", lambda v: v.tensor_mul(yo[:], yf[:], psB[0:64, :]))

                def _oy(g, mark, u=u, q0=q0):
                    mark(g.dma_start(out=yp[u, :, q0:q0 + 512], in_=yo[:]))
                dstep("gpsimd", _oy, 1)

        def _lh(s, mark):
            mark(s.dma_start(out=khu[:], in_=khy[:]))
            mark(s.dma_start(out=qhu[:], in_=qhy[:]))
            mark(s.dma_start(out=vu[:], in_=vhy[:]))
        dstep("sync", _lh, 3)

        for t in range(NQT):
            q0 = t * 512
            bound = 4 * (t + 1)
            for qc in range(4):
                first = True
                for st0 in range(0, t + 1, 2):
                    ns = min(2, t + 1 - st0)

                    def _umm(t_, qc=qc, st0=st0, ns=ns, q0=q0):
                        for k in range(ns):
                            st = st0 + k
                            t_.matmul(psS[:, k * 512:(k + 1) * 512],
                                      qhu[:, q0 + qc * 128:q0 + (qc + 1) * 128],
                                      khu[:, st * 512:(st + 1) * 512],
                                      start=True, stop=True)
                    step("tensor", _umm)
                    has_diag = (st0 + ns - 1 == t)
                    if has_diag:
                        k = ns - 1
                        step("vector", lambda v, k=k: v.tensor_copy(
                            scf[:, 0:512], psS[:, k * 512:(k + 1) * 512]))

                        def _mA(g, q0=q0, qc=qc, t=t):
                            g.affine_select(out=scf[:, 0:512], in_=scf[:, 0:512],
                                            compare_op=ALU.is_ge, fill=FR["p"],
                                            base=q0 + qc * 128 - t * 512,
                                            pattern=[[-1, 512]],
                                            channel_multiplier=1)
                        step("gpsimd", _mA)
                        if ns == 2:
                            step("vector", lambda v: v.tensor_reduce(
                                mcol[:, 4:5], psS[:, 0:512], AX, ALU.min))
                            step("vector", lambda v: v.tensor_reduce(
                                mcol[:, 5:6], scf[:, 0:512], AX, ALU.min))
                            step("vector", lambda v: v.tensor_tensor(
                                mcol[:, 4:5], mcol[:, 4:5], mcol[:, 5:6],
                                ALU.min))
                        else:
                            step("vector", lambda v: v.tensor_reduce(
                                mcol[:, 4:5], scf[:, 0:512], AX, ALU.min))
                    else:
                        if ns == 2:
                            step("vector", lambda v: v.tensor_reduce(
                                mcol[:, 4:5], psS[:, 0:1024], AX, ALU.min))
                        else:
                            step("vector", lambda v: v.tensor_reduce(
                                mcol[:, 4:5], psS[:, 0:512], AX, ALU.min))
                    if first:
                        step("vector", lambda v, qc=qc: v.tensor_copy(
                            mcol[:, qc:qc + 1], mcol[:, 4:5]))
                        first = False
                    else:
                        step("vector", lambda v, qc=qc: v.tensor_tensor(
                            mcol[:, qc:qc + 1], mcol[:, qc:qc + 1], mcol[:, 4:5], ALU.min))
            step("vector", lambda v: v.tensor_reduce(mcol[:, 5:6], mcol[:, 0:4],
                                                     AX, ALU.min))

            def _ob(g, mark):
                mark(g.dma_start(out=mscr[:], in_=mcol[:, 5:6]))
            dstep("gpsimd", _ob, 1)

            def _ib(g, mark):
                mark(g.dma_start(out=mrow[:],
                                 in_=mscr.rearrange("p one -> (one p)")[None, :]))
            dstep("gpsimd", _ib, 1)
            step("vector", lambda v: v.tensor_reduce(sca[:, 2:3], mrow[:], AX,
                                                     ALU.min))
            step("vector", lambda v: v.tensor_scalar_max(
                sca[:, 3:4], sca[:, 2:3], 1.0 + 1e-7))
            step("vector", lambda v: v.tensor_mul(sca[:, 4:5], sca[:, 3:4],
                                                  sca[:, 3:4]))
            step("scalar", lambda sc: sc.activation(out=sca[:, 4:5], in_=sca[:, 4:5],
                                                    func=AF.Ln, bias=cbias(-1.0)))
            step("scalar", lambda sc: sc.activation(out=sca[:, 4:5], in_=sca[:, 4:5],
                                                    func=AF.Exp, scale=0.5))
            step("vector", lambda v: v.tensor_add(sca[:, 5:6], sca[:, 3:4],
                                                  sca[:, 4:5]))
            step("scalar", lambda sc: sc.activation(out=sca[:, 5:6], in_=sca[:, 5:6],
                                                    func=AF.Ln))
            step("scalar", lambda sc: sc.activation(out=sca[:, 5:6], in_=sca[:, 5:6],
                                                    func=AF.Ln, bias=cbias(1e-6),
                                                    scale=sca[:, 1:2]))
            step("scalar", lambda sc: sc.activation(out=sca[:, 6:7], in_=sca[:, 5:6],
                                                    func=AF.Exp, scale=-1.0))
            step("vector", lambda v: v.tensor_scalar_mul(sca[:, 6:7], sca[:, 6:7],
                                                         -1.0))
            step("vector", lambda v: v.tensor_copy(rowb[:, 0:1], sca[:, 6:7]))
            step("tensor", lambda t_: t_.matmul(psC[:], ones1[:], rowb[:, 0:1],
                                                start=True, stop=True))
            step("vector", lambda v: v.tensor_copy(mcol[:, 6:7], psC[:]))

            ngr = t + 1
            g4 = 0
            while g4 < ngr:
                take = 2 if g4 + 1 < ngr else 1
                W = take * 2048
                for p in range(take):
                    kc0 = (g4 + p) * 4

                    def _xm(t_, kc0=kc0, q0=q0):
                        for i in range(4):
                            kc = kc0 + i
                            t_.matmul(psS[:, i * 512:(i + 1) * 512],
                                      khu[:, kc * 128:(kc + 1) * 128],
                                      qhu[:, q0:q0 + 512], start=True, stop=True)
                    step("tensor", _xm)
                    step("vector", lambda v, p=p: v.tensor_scalar_max(
                        scf[:, p * 2048:(p + 1) * 2048], psS[:], 1.0 + 1e-7))
                step("vector", lambda v, W=W: v.tensor_mul(
                    scg[:, 0:W], scf[:, 0:W], scf[:, 0:W]))
                step("scalar", lambda sc, W=W: sc.activation(
                    out=sch[:, 0:W], in_=scg[:, 0:W], func=AF.Ln,
                    bias=cbias(-1.0, 128)))
                step("scalar", lambda sc, W=W: sc.activation(
                    out=scg[:, 0:W], in_=sch[:, 0:W], func=AF.Exp, scale=0.5))
                step("vector", lambda v, W=W: v.tensor_add(
                    sch[:, 0:W], scf[:, 0:W], scg[:, 0:W]))
                step("scalar", lambda sc, W=W: sc.activation(
                    out=scg[:, 0:W], in_=sch[:, 0:W], func=AF.Ln))
                step("scalar", lambda sc, W=W: sc.activation(
                    out=sch[:, 0:W], in_=scg[:, 0:W], func=AF.Ln,
                    bias=cbias(1e-6, 128), scale=acol[:]))
                step("scalar", lambda sc, W=W: sc.activation(
                    out=scg[:, 0:W], in_=sch[:, 0:W], func=AF.Exp, scale=-1.0))
                if g4 + take - 1 == ngr - 1:
                    po = (take - 1) * 2048
                    kcd = (ngr - 1) * 4

                    def _mk2(g, q0=q0, kcd=kcd, po=po):
                        for i in range(4):
                            kc = kcd + i
                            g.affine_select(
                                out=scg[:, po + i * 512:po + (i + 1) * 512],
                                in_=scg[:, po + i * 512:po + (i + 1) * 512],
                                compare_op=ALU.is_ge, fill=FR["f"],
                                base=q0 - kc * 128, pattern=[[1, 512]],
                                channel_multiplier=-1)
                    step("gpsimd", _mk2)
                step("scalar", lambda sc, W=W: sc.activation(
                    out=pbw[:, 0:W], in_=scg[:, 0:W], func=AF.Exp,
                    bias=mcol[:, 6:7]))

                def _pv(t_, g4=g4, take=take, t=t):
                    for p in range(take):
                        for i in range(4):
                            kc = (g4 + p) * 4 + i
                            t_.matmul(psO[:], vu[:, kc * 65:(kc + 1) * 65],
                                      pbw[:, p * 2048 + i * 512:
                                          p * 2048 + (i + 1) * 512],
                                      start=(kc == 0),
                                      stop=(kc == 4 * ngr - 1),
                                      skip_group_check=True)
                step("tensor", _pv)
                g4 += take

            step("scalar", lambda sc: sc.activation(out=rowf[:], in_=psO[64:65, :],
                                                    func=AF.Ln))
            step("scalar", lambda sc: sc.activation(out=rowf[:], in_=rowf[:],
                                                    func=AF.Exp, scale=-1.0))
            step("vector", lambda v: v.tensor_copy(rowb[:], rowf[:]))
            step("tensor", lambda t_: t_.matmul(psB[:], ones1[:], rowb[:],
                                                start=True, stop=True))
            step("vector", lambda v: v.tensor_copy(yf[:], psO[0:64, :]))
            step("vector", lambda v: v.tensor_mul(yo[:], yf[:], psB[0:64, :]))

            def _oy(g, mark, q0=q0):
                mark(g.dma_start(out=yh[:, q0:q0 + 512], in_=yo[:]))
            dstep("gpsimd", _oy, 1)

        fin = s_ch.n

        def _fin(g, fin=fin):
            g.wait_ge(_s_out, fin)
        sec("gpsimd", _fin)
    return nc


def host_prep_B(inputs, A_out):
    kform = {}
    qform = {}
    vtokf = {}
    knrm = {}
    qnrm = {}
    for b in range(2):
        cores = range(b * 4, b * 4 + 4)
        kform[b] = np.concatenate([A_out[c]["kform"] for c in cores], axis=1)
        qform[b] = np.concatenate([A_out[c]["qform"] for c in cores], axis=1)
        vtokf[b] = np.concatenate([A_out[c]["vtok"] for c in cores], axis=0)
        knrm[b] = np.concatenate([A_out[c]["knrm"] for c in cores], axis=1)
        qnrm[b] = np.concatenate([A_out[c]["qnrm"] for c in cores], axis=1)
    plain_units = [(b, h) for b in range(2) for h in PLAIN_HEADS]
    hyp_units = [(b, h) for b in range(2) for h in HYP_HEADS]
    PIDX = {h: i for i, h in enumerate(PLAIN_HEADS)}
    onescol = np.ones((T_, 1), dtype=BF)

    def vchunks(b, h):
        v = vtokf[b][:, h * 64:(h + 1) * 64]
        ve = np.concatenate([v, onescol], axis=1)
        return np.ascontiguousarray(
            ve.reshape(64, 128, 65).transpose(1, 0, 2).reshape(128, 64 * 65))

    maps = []
    for c in range(NCORE):
        us = plain_units[3 * c:3 * c + 3]
        hb, hh = hyp_units[c]
        m = {
            "kp": np.stack([kform[b][FORM_OFF[h]:FORM_OFF[h] + 64] for b, h in us]),
            "qp": np.stack([qform[b][FORM_OFF[h]:FORM_OFF[h] + 64] for b, h in us]),
            "vp": np.stack([vchunks(b, h) for b, h in us]),
            "knr": np.stack([knrm[b][PIDX[h]] for b, h in us]),
            "qnr": np.stack([qnrm[b][PIDX[h]] for b, h in us]),
            "khy": np.ascontiguousarray(kform[hb][FORM_OFF[hh]:FORM_OFF[hh] + 65]),
            "qhy": np.ascontiguousarray(qform[hb][FORM_OFF[hh]:FORM_OFF[hh] + 65]),
            "vhy": vchunks(hb, hh),
            "hcur": inputs["L0_attn_curvature"].reshape(4)[hh - 6].reshape(1, 1)
                    .astype(np.float32),
        }
        maps.append(m)
    return maps


def reassemble_B(B_out):
    y = np.zeros((2, H, 64, T_), dtype=BF)
    plain_units = [(b, h) for b in range(2) for h in PLAIN_HEADS]
    hyp_units = [(b, h) for b in range(2) for h in HYP_HEADS]
    for c in range(NCORE):
        for i, (b, h) in enumerate(plain_units[3 * c:3 * c + 3]):
            y[b, h] = B_out[c]["yp"][i]
        hb, hh = hyp_units[c]
        y[hb, hh] = B_out[c]["yh"]
    return y


WID = 2560
NTW = WID // 512
DIN = 2048
CONVD = 2304


def build_C():
    nc = bass.Bass("TRN2", target_bir_lowering=False, debug=False)
    yT = nc.dram_tensor("yT", [C_, WID], BF16, kind="ExternalInput").ap()
    g0T = nc.dram_tensor("g0T", [C_, WID], F32, kind="ExternalInput").ap()
    xT = nc.dram_tensor("xT", [C_, WID], F32, kind="ExternalInput").ap()
    projwT = nc.dram_tensor("projwT", [C_, C_], F32, kind="ExternalInput").ap()
    projb = nc.dram_tensor("projb", [C_, 1], F32, kind="ExternalInput").ap()
    winT = nc.dram_tensor("winT", [C_, 4384], F32, kind="ExternalInput").ap()
    convw = nc.dram_tensor("convw", [CONVD, 4], F32, kind="ExternalInput").ap()
    convb = nc.dram_tensor("convb", [CONVD, 1], F32, kind="ExternalInput").ap()
    dtb = nc.dram_tensor("dtb", [32, 1], F32, kind="ExternalInput").ap()
    alog = nc.dram_tensor("alog", [32, 1], F32, kind="ExternalInput").ap()

    zo = nc.dram_tensor("zo", [DIN, TPC], BF16, kind="ExternalOutput").ap()
    xbco = nc.dram_tensor("xbco", [CONVD, TPC], BF16, kind="ExternalOutput").ap()
    dtpo = nc.dram_tensor("dtpo", [32, TPC], F32, kind="ExternalOutput").ap()
    adto = nc.dram_tensor("adto", [32, TPC], F32, kind="ExternalOutput").ap()
    x1o = nc.dram_tensor("x1o", [C_, WID], F32, kind="ExternalOutput").ap()
    xn1o = nc.dram_tensor("xn1o", [C_, TPC], BF16, kind="ExternalOutput").ap()

    with ExitStack() as _st, nc.Block() as blk:
        ybuf = _st.enter_context(nc.sbuf_tensor("ybuf", [128, 8 * WID], BF16))
        xn1b = _st.enter_context(nc.sbuf_tensor("xn1b", [128, 8 * WID], BF16))
        xbcb = _st.enter_context(nc.sbuf_tensor("xbcb", [128, 18 * 515], BF16))
        dtr = _st.enter_context(nc.sbuf_tensor("dtr", [32, WID], F32))
        w32 = _st.enter_context(nc.sbuf_tensor("w32", [128, 8 * 512], F32))
        wbf = _st.enter_context(nc.sbuf_tensor("wbf", [128, 8 * 512], BF16))
        stg = _st.enter_context(nc.sbuf_tensor("stg", [128, 2 * WID + 3 * 512], F32))
        stgb = _st.enter_context(nc.sbuf_tensor("stgb", [128, TPC], BF16))
        rows_ = _st.enter_context(nc.sbuf_tensor("rows", [1, 4 * WID], F32))
        cols_ = _st.enter_context(nc.sbuf_tensor("cols", [128, 16], F32))
        cw4 = _st.enter_context(nc.sbuf_tensor("cw4", [128, 5], F32))
        ones1 = _st.enter_context(nc.sbuf_tensor("ones1", [1, 128], F32))
        onesc = _st.enter_context(nc.sbuf_tensor("onesc", [128, 1], F32))
        cpool = _st.enter_context(nc.sbuf_tensor("cpool", [128, 8], F32))
        _s_out = _st.enter_context(nc.semaphore("s_out"))

        CP = {LN_EPS: 0, 1.0: 1}

        def cbias(v, parts=1):
            return cpool[0:parts, CP[v]:CP[v] + 1]

        def sec(name, f):
            getattr(blk, name)(f)

        s_ch = Sem(_s_out)

        class _Rec:
            def __init__(self, e):
                self._e = e
                self.last = None

            def __getattr__(self, k):
                a = getattr(self._e, k)
                if callable(a):
                    def w(*args, **kw):
                        r = a(*args, **kw)
                        if hasattr(r, "then_inc"):
                            self.last = r
                        return r
                    return w
                return a

        def step(eng, fn):
            cur = s_ch.n

            def _f(e, fn=fn, cur=cur):
                e.wait_ge(_s_out, cur)
                rec = _Rec(e)
                fn(rec)
                (rec.last if rec.last is not None else e.nop()).then_inc(_s_out, 1)
            sec(eng, _f)
            s_ch.n += 1

        def dstep(eng, fn, n=1):
            cur = s_ch.n

            def _f(e, fn=fn, cur=cur):
                e.wait_ge(_s_out, cur)
                fn(e, lambda ins: ins.then_inc(_s_out, 16))
            sec(eng, _f)
            s_ch.n += 16 * n

        def row(i):
            return rows_[:, i * WID:(i + 1) * WID]

        def _c0(g):
            g.memset(ones1[:], 1.0)
            g.memset(onesc[:], 1.0)
            g.memset(rows_[:], 0.0)
            for _v, _i in CP.items():
                g.memset(cpool[:, _i:_i + 1], _v)
        step("gpsimd", _c0)

        def _ly(s, mark):
            for d in range(8):
                mark(s.dma_start(out=ybuf[:, d * WID:(d + 1) * WID],
                                 in_=yT[d * 128:(d + 1) * 128, :]))
        dstep("sync", _ly, 8)

        _cm_p1 = nc.psum_tensor("pmm", [128, 4 * 512], F32)
        _cm_p2 = nc.psum_tensor("pst", [1, 2 * 512], F32)
        pmm = _cm_p1.__enter__()
        pst = _cm_p2.__enter__()
        if True:
            for cb in range(2):
                def _lw(s, mark, cb=cb):
                    for d in range(8):
                        mark(s.dma_start(
                            out=w32[:, d * 512:(d + 1) * 512],
                            in_=projwT[d * 128:(d + 1) * 128,
                                       cb * 512:(cb + 1) * 512]))
                dstep("sync", _lw, 8)
                step("vector", lambda v: v.tensor_copy(wbf[:], w32[:]))
                for c4 in range(4):
                    R0 = cb * 512 + c4 * 128
                    G0 = 0
                    X0 = WID
                    S0 = 2 * WID

                    def _lb(s, mark, R0=R0):
                        mark(s.dma_start(out=cols_[:, 0:1],
                                         in_=projb[R0:R0 + 128, :]))
                        mark(s.dma_start(out=stg[:, G0:G0 + WID],
                                         in_=g0T[R0:R0 + 128, :]))
                        mark(s.dma_start(out=stg[:, X0:X0 + WID],
                                         in_=xT[R0:R0 + 128, :]))
                    dstep("sync", _lb, 3)
                    for j in range(NTW):
                        ps = pmm[:, (j % 4) * 512:(j % 4 + 1) * 512]
                        gj = stg[:, G0 + j * 512:G0 + (j + 1) * 512]
                        xj = stg[:, X0 + j * 512:X0 + (j + 1) * 512]

                        def _mm(t, c4=c4, j=j, ps=ps):
                            for d in range(8):
                                t.matmul(ps, wbf[:, d * 512 + c4 * 128:
                                                 d * 512 + c4 * 128 + 128],
                                         ybuf[:, d * WID + j * 512:
                                              d * WID + (j + 1) * 512],
                                         start=(d == 0), stop=(d == 7))
                        step("tensor", _mm)
                        step("vector", lambda v, ps=ps, gj=gj, xj=xj: (
                            v.scalar_tensor_tensor(out=stg[:, S0:S0 + 512],
                                                   in0=gj, scalar=cols_[:, 0:1],
                                                   in1=xj, op0=ALU.mult,
                                                   op1=ALU.add),
                            v.scalar_tensor_tensor(out=stg[:, S0 + 512:S0 + 1024],
                                                   in0=ps, scalar=0.0,
                                                   in1=gj, op0=ALU.bypass,
                                                   op1=ALU.mult))[-1])
                        step("vector", lambda v, xj=xj: v.tensor_add(
                            xj, stg[:, S0:S0 + 512], stg[:, S0 + 512:S0 + 1024]))
                        step("vector", lambda v, xj=xj: v.tensor_mul(
                            stg[:, S0:S0 + 512], xj, xj))

                        def _sm(t, j=j, xj=xj):
                            t.matmul(pst[:, 0:512], onesc[:], xj,
                                     start=True, stop=True)
                            t.matmul(pst[:, 512:1024], onesc[:],
                                     stg[:, S0:S0 + 512], start=True, stop=True)
                        step("tensor", _sm)
                        step("vector", lambda v, j=j: (
                            v.tensor_add(row(0)[:, j * 512:(j + 1) * 512],
                                         row(0)[:, j * 512:(j + 1) * 512],
                                         pst[:, 0:512]),
                            v.tensor_add(row(1)[:, j * 512:(j + 1) * 512],
                                         row(1)[:, j * 512:(j + 1) * 512],
                                         pst[:, 512:1024]))[-1])

                    def _ox1(g, mark, R0=R0, X0=X0):
                        mark(g.dma_start(out=x1o[R0:R0 + 128, :],
                                         in_=stg[:, X0:X0 + WID]))
                    dstep("gpsimd", _ox1, 1)

            step("vector", lambda v: v.tensor_scalar_mul(row(2), row(0), 1.0 / C_))
            step("vector", lambda v: v.tensor_mul(row(0), row(2), row(2)))
            step("vector", lambda v: v.tensor_scalar_mul(row(1), row(1), 1.0 / C_))
            step("vector", lambda v: v.tensor_sub(row(1), row(1), row(0)))
            step("scalar", lambda sc: sc.activation(out=row(1), in_=row(1),
                                                    func=AF.Ln, bias=cbias(LN_EPS)))
            step("scalar", lambda sc: sc.activation(out=row(3), in_=row(1),
                                                    func=AF.Exp, scale=-0.5))
            step("vector", lambda v: v.scalar_tensor_tensor(
                out=row(0), in0=row(2), scalar=-1.0, in1=row(3),
                op0=ALU.mult, op1=ALU.mult))
        _cm_p2.__exit__(None, None, None)
        _cm_p1.__exit__(None, None, None)

        _cm_p3 = nc.psum_tensor("pbc", [128, 2 * 512], F32)
        pbc = _cm_p3.__enter__()
        if True:
            for j in range(NTW):
                def _bc(t, j=j):
                    t.matmul(pbc[:, 0:512], ones1[:],
                             row(3)[:, j * 512:(j + 1) * 512], start=True, stop=True)
                    t.matmul(pbc[:, 512:1024], ones1[:],
                             row(0)[:, j * 512:(j + 1) * 512], start=True, stop=True)
                step("tensor", _bc)
                for d in range(8):
                    def _lx1(s, mark, d=d, j=j):
                        mark(s.dma_start(out=stg[:, 0:512],
                                         in_=x1o[d * 128:(d + 1) * 128,
                                                 j * 512:(j + 1) * 512]))
                    dstep("sync", _lx1, 1)
                    step("vector", lambda v: v.tensor_mul(stg[:, 512:1024],
                                                          stg[:, 0:512],
                                                          pbc[:, 0:512]))
                    step("vector", lambda v, d=d, j=j: v.tensor_add(
                        xn1b[:, d * WID + j * 512:d * WID + (j + 1) * 512],
                        stg[:, 512:1024], pbc[:, 512:1024]))
                    if j >= 1:
                        def _oxn(g, mark, d=d, j=j):
                            mark(g.dma_start(
                                out=xn1o[d * 128:(d + 1) * 128,
                                         (j - 1) * 512:j * 512],
                                in_=xn1b[:, d * WID + j * 512:
                                         d * WID + (j + 1) * 512]))
                        dstep("gpsimd", _oxn, 1)
        _cm_p3.__exit__(None, None, None)

        _cm_p4 = nc.psum_tensor("pmm2", [128, 4 * 512], F32)
        pmm2 = _cm_p4.__enter__()
        if True:
            CW = [128] * 34 + [32]
            ncb = 9
            for cb in range(ncb):
                c0 = cb * 512
                cwidth = min(512, 4384 - c0)
                nch = (cwidth + 127) // 128

                def _lw(s, mark, cb=cb, c0=c0, cwidth=cwidth):
                    for d in range(8):
                        mark(s.dma_start(
                            out=w32[:, d * 512:d * 512 + cwidth],
                            in_=winT[d * 128:(d + 1) * 128, c0:c0 + cwidth]))
                dstep("sync", _lw, 8)

                def _cw(v, cwidth=cwidth):
                    for d in range(8):
                        v.tensor_copy(wbf[:, d * 512:d * 512 + cwidth],
                                      w32[:, d * 512:d * 512 + cwidth])
                step("vector", _cw)

                for c4 in range(nch):
                    R0 = c0 + c4 * 128
                    chw = min(128, 4384 - R0)
                    if R0 >= 2048 and R0 < 4352:
                        ch = (R0 - 2048) // 128

                        def _lcw(s, mark, ch=ch):
                            mark(s.dma_start(out=cw4[:, 0:4],
                                             in_=convw[ch * 128:(ch + 1) * 128, :]))
                            mark(s.dma_start(out=cw4[:, 4:5],
                                             in_=convb[ch * 128:(ch + 1) * 128, :]))
                        dstep("sync", _lcw, 2)
                    for j in range(NTW):
                        ps = pmm2[0:chw, (j % 4) * 512:(j % 4 + 1) * 512]

                        def _mm(t, c4=c4, j=j, ps=ps, chw=chw):
                            for d in range(8):
                                t.matmul(ps, wbf[:, d * 512 + c4 * 128:
                                                 d * 512 + c4 * 128 + chw],
                                         xn1b[:, d * WID + j * 512:
                                              d * WID + (j + 1) * 512],
                                         start=(d == 0), stop=(d == 7))
                        step("tensor", _mm)

                        if R0 < 2048:
                            if j >= 1:
                                step("vector", lambda v, ps=ps, j=j: v.tensor_copy(
                                    stgb[:, (j - 1) * 512:j * 512], ps))
                                if j == NTW - 1:
                                    def _oz(g, mark, R0=R0):
                                        mark(g.dma_start(out=zo[R0:R0 + 128, :],
                                                         in_=stgb[:]))
                                    dstep("gpsimd", _oz, 1)
                        elif R0 < 4352:
                            ch = (R0 - 2048) // 128
                            slot = xbcb[:, ch * 515:(ch + 1) * 515]
                            step("vector", lambda v, ps=ps, slot=slot: v.tensor_copy(
                                slot[:, 3:515], ps))
                            if j >= 1:
                                step("vector", lambda v, slot=slot: v.tensor_scalar(
                                    out=stg[:, 0:512], in0=slot[:, 0:512],
                                    scalar1=cw4[:, 0:1], scalar2=None,
                                    op0=ALU.mult))
                                for jj in range(1, 4):
                                    def _cv(v, jj=jj, slot=slot):
                                        a, b = (0, 512) if jj % 2 == 1 else (512, 0)
                                        v.scalar_tensor_tensor(
                                            out=stg[:, b:b + 512],
                                            in0=slot[:, jj:jj + 512],
                                            scalar=cw4[:, jj:jj + 1],
                                            in1=stg[:, a:a + 512],
                                            op0=ALU.mult, op1=ALU.add)
                                    step("vector", _cv)
                                step("scalar", lambda sc: sc.activation(
                                    out=stgb[:, 0:512], in_=stg[:, 512:1024],
                                    func=AF.Silu, bias=cw4[:, 4:5]))

                                def _ox(g, mark, R0=R0, j=j):
                                    mark(g.dma_start(
                                        out=xbco[R0 - 2048:R0 - 2048 + 128,
                                                 (j - 1) * 512:j * 512],
                                        in_=stgb[:, 0:512]))
                                dstep("gpsimd", _ox, 1)
                            step("vector", lambda v, slot=slot: v.tensor_copy(
                                slot[:, 0:3], slot[:, 512:515]))
                        else:
                            step("vector", lambda v, ps=ps, j=j: v.tensor_copy(
                                dtr[:, j * 512:(j + 1) * 512], ps[0:32, :]))
        _cm_p4.__exit__(None, None, None)

        def _ldt(s, mark):
            mark(s.dma_start(out=cols_[0:32, 1:2], in_=dtb[:]))
            mark(s.dma_start(out=cols_[0:32, 2:3], in_=alog[:]))
        dstep("sync", _ldt, 2)
        step("scalar", lambda sc: sc.activation(out=cols_[0:32, 3:4],
                                                in_=cols_[0:32, 2:3], func=AF.Exp))
        OWN = slice(512, WID)
        step("vector", lambda v: v.tensor_scalar_add(dtr[:, OWN], dtr[:, OWN],
                                                     cols_[0:32, 1:2]))
        step("scalar", lambda sc: sc.activation(out=stg[0:32, 0:TPC],
                                                in_=dtr[:, OWN], func=AF.Abs))
        step("scalar", lambda sc: sc.activation(out=stg[0:32, 0:TPC],
                                                in_=stg[0:32, 0:TPC],
                                                func=AF.Exp, scale=-1.0))
        step("scalar", lambda sc: sc.activation(out=stg[0:32, 0:TPC],
                                                in_=stg[0:32, 0:TPC],
                                                func=AF.Ln, bias=cbias(1.0, 32)))
        step("vector", lambda v: v.tensor_scalar_max(dtr[:, OWN], dtr[:, OWN], 0.0))
        step("vector", lambda v: v.tensor_add(dtr[:, OWN], dtr[:, OWN],
                                              stg[0:32, 0:TPC]))

        def _odt(g, mark):
            mark(g.dma_start(out=dtpo[:], in_=dtr[:, OWN]))
        dstep("gpsimd", _odt, 1)
        step("vector", lambda v: v.tensor_scalar(
            out=dtr[:, OWN], in0=dtr[:, OWN], scalar1=cols_[0:32, 3:4],
            scalar2=-1.0, op0=ALU.mult, op1=ALU.mult))

        def _oad(g, mark):
            mark(g.dma_start(out=adto[:], in_=dtr[:, OWN]))
        dstep("gpsimd", _oad, 1)

        fin = s_ch.n

        def _fin(g, fin=fin):
            g.wait_ge(_s_out, fin)
        sec("gpsimd", _fin)
    return nc


def host_prep_C(inputs, A_out, y_full):
    x = inputs["L0_attn_x"]
    xTf = np.ascontiguousarray(x.reshape(B_ * T_, C_).T)
    maps = []
    for c in range(NCORE):
        b, t0 = c // 4, (c % 4) * TPC
        yb = y_full[b].reshape(C_, T_)
        def pad_slice(arr, dtype):
            out = np.zeros((C_, WID), dtype)
            lo = t0 - 512
            src_lo = max(lo, 0)
            out[:, src_lo - lo:] = arr[:, src_lo:t0 + TPC]
            return out
        g0_full = np.concatenate([A_out[cc]["g0"] for cc in range(b * 4, b * 4 + 4)],
                                 axis=1)
        m = {
            "yT": pad_slice(yb, BF),
            "g0T": pad_slice(g0_full, np.float32),
            "xT": pad_slice(xTf[:, b * T_:(b + 1) * T_], np.float32),
            "projwT": np.ascontiguousarray(inputs["L0_attn_proj_w"].T),
            "projb": inputs["L0_attn_proj_b"].reshape(C_, 1),
            "winT": np.ascontiguousarray(inputs["L1_attn_W_in"].T),
            "convw": inputs["L1_attn_conv_w"].reshape(CONVD, 4),
            "convb": inputs["L1_attn_conv_b"].reshape(CONVD, 1),
            "dtb": inputs["L1_attn_dt_bias"].reshape(32, 1),
            "alog": inputs["L1_attn_A_log"].reshape(32, 1),
        }
        maps.append(m)
    return maps


NCH = T_ // 64


def build_D():
    nc = bass.Bass("TRN2", target_bir_lowering=False, debug=False)
    xhD = nc.dram_tensor("xhD", [8, 64, T_], BF16, kind="ExternalInput").ap()
    BfD = nc.dram_tensor("BfD", [128, T_], BF16, kind="ExternalInput").ap()
    CfD = nc.dram_tensor("CfD", [128, T_], BF16, kind="ExternalInput").ap()
    dtpD = nc.dram_tensor("dtpD", [8, T_], F32, kind="ExternalInput").ap()
    adtD = nc.dram_tensor("adtD", [8, T_], F32, kind="ExternalInput").ap()
    dpD = nc.dram_tensor("dpD", [1, 8], F32, kind="ExternalInput").ap()
    yssd = nc.dram_tensor("yssd", [8, 64, T_], F32, kind="ExternalOutput").ap()
    bscr = nc.dram_tensor("bscr", [8, 128], F32).ap()
    acsT = nc.dram_tensor("acsT", [NCH, 512], F32).ap()

    with ExitStack() as _st, nc.Block() as blk:
        xh4 = _st.enter_context(nc.sbuf_tensor("xh4", [64, 4 * T_], BF16))
        Bf = _st.enter_context(nc.sbuf_tensor("Bf", [128, T_], BF16))
        Cf = _st.enter_context(nc.sbuf_tensor("Cf", [128, T_], BF16))
        acs8 = _st.enter_context(nc.sbuf_tensor("acs8", [8, T_], F32))
        dtp8b = _st.enter_context(nc.sbuf_tensor("dtp8b", [8, T_], BF16))
        dtd8b = _st.enter_context(nc.sbuf_tensor("dtd8b", [8, T_], BF16))
        coltab = _st.enter_context(nc.sbuf_tensor("coltab", [128, 8 * 128], F32))
        dcols = _st.enter_context(nc.sbuf_tensor("dcols", [64, 8], F32))
        srun = _st.enter_context(nc.sbuf_tensor("srun", [128, 2 * 8 * 64], F32))
        sbf = _st.enter_context(nc.sbuf_tensor("sbf", [128, 8 * 64], BF16))
        btok = _st.enter_context(nc.sbuf_tensor("btok", [64, 128], BF16))
        colsA = _st.enter_context(nc.sbuf_tensor("colsA", [64, 8], F32))
        colsB = _st.enter_context(nc.sbuf_tensor("colsB", [64, 16], F32))
        gbuf = _st.enter_context(nc.sbuf_tensor("gbuf", [64, 64], F32))
        xq4 = _st.enter_context(nc.sbuf_tensor("xq4", [64, 256], BF16))
        xw14 = _st.enter_context(nc.sbuf_tensor("xw14", [64, 256], BF16))
        xw24 = _st.enter_context(nc.sbuf_tensor("xw24", [64, 256], BF16))
        lsub4 = _st.enter_context(nc.sbuf_tensor("lsub4", [64, 256], F32))
        lexp4 = _st.enter_context(nc.sbuf_tensor("lexp4", [64, 256], F32))
        mte4 = _st.enter_context(nc.sbuf_tensor("mte4", [64, 256], BF16))
        cdt4 = _st.enter_context(nc.sbuf_tensor("cdt4", [128, 256], BF16))
        ystgw = _st.enter_context(nc.sbuf_tensor("ystgw", [64, 8 * 256], F32))
        erow = _st.enter_context(nc.sbuf_tensor("erow", [1, 2 * 512], F32))
        erow2 = _st.enter_context(nc.sbuf_tensor("erow2", [1, 256], F32))
        mrow = _st.enter_context(nc.sbuf_tensor("mrow", [1, 128], F32))
        identb = _st.enter_context(nc.sbuf_tensor("identb", [128, 128], BF16))
        identf = _st.enter_context(nc.sbuf_tensor("identf", [8, 8], F32))
        ones1 = _st.enter_context(nc.sbuf_tensor("ones1", [1, 128], F32))
        dp = _st.enter_context(nc.sbuf_tensor("dp", [1, 8], F32))
        psT1 = _st.enter_context(nc.psum_tensor("psT1", [64, 144], BF16))
        psT2 = _st.enter_context(nc.psum_tensor("psT2", [64, 8], F32))
        psG = _st.enter_context(nc.psum_tensor("psG", [64, 64], F32))
        psL = _st.enter_context(nc.psum_tensor("psL", [64, 512], F32))
        psCd = _st.enter_context(nc.psum_tensor("psCd", [128, 256], F32))
        psY = _st.enter_context(nc.psum_tensor("psY", [64, 256], F32))
        psSt = _st.enter_context(nc.psum_tensor("psSt", [128, 256], F32))
        psX = _st.enter_context(nc.psum_tensor("psX", [64, 256], BF16))
        _s_out = _st.enter_context(nc.semaphore("s_out"))
        _s_er = _st.enter_context(nc.semaphore("s_er"))

        def sec(name, f):
            getattr(blk, name)(f)

        s_ch = Sem(_s_out)

        class _Rec:
            def __init__(self, e):
                self._e = e
                self.last = None

            def __getattr__(self, k):
                a = getattr(self._e, k)
                if callable(a):
                    def w(*args, **kw):
                        r = a(*args, **kw)
                        if hasattr(r, "then_inc"):
                            self.last = r
                        return r
                    return w
                return a

        def step(eng, fn):
            cur = s_ch.n

            def _f(e, fn=fn, cur=cur):
                e.wait_ge(_s_out, cur)
                rec = _Rec(e)
                fn(rec)
                (rec.last if rec.last is not None else e.nop()).then_inc(_s_out, 1)
            sec(eng, _f)
            s_ch.n += 1

        def dstep(eng, fn, n=1):
            cur = s_ch.n

            def _f(e, fn=fn, cur=cur):
                e.wait_ge(_s_out, cur)
                fn(e, lambda ins: ins.then_inc(_s_out, 16))
            sec(eng, _f)
            s_ch.n += 16 * n

        FR = {}

        def _c0(g):
            g.memset(identb[:], 1.0)
            g.memset(identf[:], 1.0)
            g.memset(ones1[:], 1.0)
            g.memset(srun[:], 0.0)
            g.memset(sbf[:], 0.0)
            FR["f"] = g.alloc_register("fillneg")
            g.reg_mov(FR["f"], NEG_BIG)
        step("gpsimd", _c0)

        def _c1(g):
            g.affine_select(out=identb[:], in_=identb[:], compare_op=ALU.is_equal,
                            fill=0.0, base=0, pattern=[[-1, 128]],
                            channel_multiplier=1)
            g.affine_select(out=identf[:], in_=identf[:], compare_op=ALU.is_equal,
                            fill=0.0, base=0, pattern=[[-1, 8]],
                            channel_multiplier=1)
        step("gpsimd", _c1)

        stA = xh4[0:8, :].bitcast(F32)[:, 0:T_]
        stB = xh4[0:8, :].bitcast(F32)[:, T_:2 * T_]

        def _l0(s, mark):
            mark(s.dma_start(out=Bf[:], in_=BfD[:]))
            mark(s.dma_start(out=Cf[:], in_=CfD[:]))
            mark(s.dma_start(out=stA, in_=adtD[:]))
            mark(s.dma_start(out=dp[:], in_=dpD[:]))
        dstep("sync", _l0, 4)

        for c in range(NCH):
            def _scan(v, c=c):
                v.tensor_tensor_scan(out=acs8[:, c * 64:(c + 1) * 64],
                                     data0=stA[:, c * 64:(c + 1) * 64],
                                     data1=stA[:, c * 64:(c + 1) * 64],
                                     initial=0.0, op0=ALU.add, op1=ALU.bypass)
            step("vector", _scan)

        acs3 = acs8.rearrange("p (c l) -> p c l", l=64)

        def _ds1(v):
            v.tensor_sub(stB.rearrange("p (c l) -> p c l", l=64),
                         acs3[:, :, 63:64].broadcast_to((8, NCH, 64)), acs3)
        step("vector", _ds1)
        step("scalar", lambda sc: sc.activation(out=stB, in_=stB, func=AF.Exp))

        def _ldt(s, mark):
            mark(s.dma_start(out=stA, in_=dtpD[:]))
        dstep("sync", _ldt, 1)
        step("vector", lambda v: v.tensor_mul(dtd8b[:], stB, stA))
        step("vector", lambda v: v.tensor_copy(dtp8b[:], stA))

        def _oat(g, mark):
            mark(g.dma_start(out=acsT.rearrange("c (h l) -> h c l", h=8, l=64),
                             in_=acs8.rearrange("h (c l) -> h c l", l=64)))
        dstep("gpsimd", _oat, 1)

        for h in range(8):
            def _bo(g, mark, h=h):
                mark(g.dma_start(out=bscr[h:h + 1, :],
                                 in_=acs8[h:h + 1, 63:T_:64],
                                 allow_slow_non_contiguous=True))
            dstep("gpsimd", _bo, 1)

            def _bi(s, mark, h=h):
                mark(s.dma_start(out=mrow[:], in_=bscr[h:h + 1, :]))
            dstep("sync", _bi, 1)
            step("scalar", lambda sc: sc.activation(out=mrow[:], in_=mrow[:],
                                                    func=AF.Exp))
            for hf in range(2):
                step("tensor", lambda t, hf=hf: t.matmul(
                    psCd[:, 0:64], ones1[:], mrow[:, hf * 64:(hf + 1) * 64],
                    start=True, stop=True))
                step("vector", lambda v, h=h, hf=hf: v.tensor_copy(
                    coltab[:, h * 128 + hf * 64:h * 128 + (hf + 1) * 64],
                    psCd[:, 0:64]))

        for h in range(8):
            step("tensor", lambda t, h=h: t.matmul(psY[:, 0:1], ones1[0:1, 0:64],
                                                   dp[:, h:h + 1],
                                                   start=True, stop=True))
            step("vector", lambda v, h=h: v.tensor_copy(dcols[:, h:h + 1],
                                                        psY[:, 0:1]))

        for grp in range(2):
            def _lx(s, mark, grp=grp):
                for i in range(4):
                    mark(s.dma_start(out=xh4[:, i * T_:(i + 1) * T_],
                                     in_=xhD[grp * 4 + i]))
            dstep("sync", _lx, 4)

            def _er0(s, mark, grp=grp):
                mark(s.dma_start(out=erow[:, 0:512], in_=acsT[0:1, :]))
            dstep("sync", _er0, 1)
            ern = [127 * 16 * grp]

            for c in range(NCH):
                cs = slice(c * 64, (c + 1) * 64)
                par = c % 2
                epar = c % 2
                h0 = grp * 4
                sr_a = srun[:, (par * 8 + h0) * 64:(par * 8 + h0 + 4) * 64]
                sr_b = srun[:, ((1 - par) * 8 + h0) * 64:
                             ((1 - par) * 8 + h0 + 4) * 64]
                sb4 = sbf[:, h0 * 64:(h0 + 4) * 64]

                def _bi2(s, mark, c=c):
                    mark(s.dma_start(out=erow[:, 0:512], in_=acsT[c:c + 1, :]))
                dstep("sync", _bi2, 1)

                def _sh1(t, c=c, cs=cs):
                    t.transpose(psT1[:, 0:128], Bf[:, cs], identb[:])
                    t.transpose(psT2[:], acs8[:, cs], identf[:])
                    t.transpose(psT1[:, 128:136], dtp8b[:, cs], identb[0:8, 0:8])
                    t.transpose(psT1[:, 136:144], dtd8b[:, cs], identb[0:8, 0:8])
                    t.matmul(psG[:], Bf[:, cs], Cf[:, cs], start=True, stop=True,
                             skip_group_check=True)
                    for i in range(4):
                        t.transpose(psX[:, i * 64:(i + 1) * 64],
                                    xh4[:, i * T_ + c * 64:i * T_ + (c + 1) * 64],
                                    identb[0:64, 0:64])
                step("tensor", _sh1)

                def _sh2(v):
                    v.tensor_copy(btok[:], psT1[:, 0:128])
                    v.tensor_copy(colsA[:], psT2[:])
                    v.tensor_copy(colsB[:], psT1[:, 128:144])
                    v.tensor_copy(gbuf[:], psG[:])
                    v.tensor_copy(xq4[:], psX[:])
                step("vector", _sh2)

                def _psl(t):
                    t.matmul(psL[:], ones1[0:1, 0:64], erow[:, 0:512],
                             start=True, stop=True)
                step("tensor", _psl)

                def _xw(v, h0=h0):
                    for i in range(4):
                        v.tensor_scalar(out=xw14[:, i * 64:(i + 1) * 64],
                                        in0=xq4[:, i * 64:(i + 1) * 64],
                                        scalar1=colsB[:, h0 + i:h0 + i + 1],
                                        scalar2=None, op0=ALU.mult)
                        v.tensor_scalar(out=xw24[:, i * 64:(i + 1) * 64],
                                        in0=xq4[:, i * 64:(i + 1) * 64],
                                        scalar1=colsB[:, 8 + h0 + i:9 + h0 + i],
                                        scalar2=None, op0=ALU.mult)
                    for i in range(4):
                        h = h0 + i
                        v.tensor_scalar(out=lsub4[:, i * 64:(i + 1) * 64],
                                        in0=psL[:, h * 64:(h + 1) * 64],
                                        scalar1=colsA[:, h:h + 1], scalar2=None,
                                        op0=ALU.subtract)
                step("vector", _xw)

                def _lm(g):
                    for i in range(4):
                        g.affine_select(out=lsub4[:, i * 64:(i + 1) * 64],
                                        in_=lsub4[:, i * 64:(i + 1) * 64],
                                        compare_op=ALU.is_ge, fill=FR["f"],
                                        base=0, pattern=[[1, 64]],
                                        channel_multiplier=-1)
                step("gpsimd", _lm)

                def _le(sc, h0=h0):
                    sc.activation(out=lexp4[:], in_=lsub4[:], func=AF.Exp)
                    sc.activation(out=erow2[:], in_=erow[:, h0 * 64:(h0 + 4) * 64],
                                  func=AF.Exp)
                step("scalar", _le)

                def _bc(t):
                    for i in range(4):
                        t.matmul(psCd[:, i * 64:(i + 1) * 64], ones1[:],
                                 erow2[:, i * 64:(i + 1) * 64],
                                 start=True, stop=True)
                step("tensor", _bc)

                def _mm2(v):
                    v.tensor_mul(mte4.rearrange("p (i l) -> p i l", l=64),
                                 gbuf[:].unsqueeze(1).broadcast_to((64, 4, 64)),
                                 lexp4.rearrange("p (i l) -> p i l", l=64))
                    v.tensor_mul(cdt4.rearrange("p (i l) -> p i l", l=64),
                                 Cf[:, cs].unsqueeze(1).broadcast_to((128, 4, 64)),
                                 psCd[:].rearrange("p (i l) -> p i l", l=64))
                step("vector", _mm2)

                def _ymm(t):
                    for i in range(4):
                        t.matmul(psY[:, i * 64:(i + 1) * 64],
                                 xw14[:, i * 64:(i + 1) * 64],
                                 mte4[:, i * 64:(i + 1) * 64],
                                 start=True, stop=False, skip_group_check=True)
                        t.matmul(psSt[:, i * 64:(i + 1) * 64], btok[:],
                                 xw24[:, i * 64:(i + 1) * 64],
                                 start=True, stop=True, skip_group_check=True)
                        t.matmul(psY[:, i * 64:(i + 1) * 64],
                                 sb4[:, i * 64:(i + 1) * 64],
                                 cdt4[:, i * 64:(i + 1) * 64],
                                 start=False, stop=True, skip_group_check=True)
                step("tensor", _ymm)

                def _ev(v, c=c, h0=h0):
                    co = (c % 8) * 256
                    for i in range(4):
                        v.scalar_tensor_tensor(
                            out=ystgw[:, co + i * 64:co + (i + 1) * 64],
                            in0=xh4[:, i * T_ + c * 64:i * T_ + (c + 1) * 64],
                            scalar=dcols[:, h0 + i:h0 + i + 1],
                            in1=psY[:, i * 64:(i + 1) * 64],
                            op0=ALU.mult, op1=ALU.add)
                        v.scalar_tensor_tensor(
                            out=sr_b[:, i * 64:(i + 1) * 64],
                            in0=sr_a[:, i * 64:(i + 1) * 64],
                            scalar=coltab[:, (h0 + i) * 128 + c:
                                          (h0 + i) * 128 + c + 1],
                            in1=psSt[:, i * 64:(i + 1) * 64],
                            op0=ALU.mult, op1=ALU.add)
                step("vector", _ev)
                step("vector", lambda v: v.tensor_copy(sb4[:], sr_b))

                if c % 8 == 7:
                    def _oy(g, mark, c=c, h0=h0):
                        c0 = c - 7
                        for i in range(4):
                            mark(g.dma_start(
                                out=yssd[h0 + i, :, c0 * 64:(c0 + 8) * 64],
                                in_=ystgw.rearrange("p (c8 i l) -> p c8 i l",
                                                    c8=8, i=4)[:, :, i, :]))
                    dstep("gpsimd", _oy, 4)

        fin = s_ch.n

        def _fin(g, fin=fin):
            g.wait_ge(_s_out, fin)
        sec("gpsimd", _fin)
    return nc


def host_prep_D(inputs, C_out):
    maps = []
    for c in range(NCORE):
        b = c // 4
        h0 = 8 * (c % 4)
        cores = range(b * 4, b * 4 + 4)
        xbc = np.concatenate([C_out[cc]["xbco"] for cc in cores], axis=1)
        dtp = np.concatenate([C_out[cc]["dtpo"] for cc in cores], axis=1)
        adt = np.concatenate([C_out[cc]["adto"] for cc in cores], axis=1)
        m = {
            "xhD": np.stack([xbc[(h0 + i) * 64:(h0 + i + 1) * 64]
                             for i in range(8)]),
            "BfD": np.ascontiguousarray(xbc[2048:2176]),
            "CfD": np.ascontiguousarray(xbc[2176:2304]),
            "dtpD": np.ascontiguousarray(dtp[h0:h0 + 8]),
            "adtD": np.ascontiguousarray(adt[h0:h0 + 8]),
            "dpD": inputs["L1_attn_D_param"][h0:h0 + 8].reshape(1, 8)
                   .astype(np.float32),
        }
        maps.append(m)
    return maps


ES = 256


def build_E():
    nc = bass.Bass("TRN2", target_bir_lowering=False, debug=False)
    ysT = nc.dram_tensor("ysT", [DIN, TPC], F32, kind="ExternalInput").ap()
    zT = nc.dram_tensor("zT", [DIN, TPC], BF16, kind="ExternalInput").ap()
    xn1T = nc.dram_tensor("xn1T", [C_, TPC], BF16, kind="ExternalInput").ap()
    woutT = nc.dram_tensor("woutT", [DIN, C_], F32, kind="ExternalInput").ap()
    normw = nc.dram_tensor("normw", [DIN, 1], F32, kind="ExternalInput").ap()
    muT = nc.dram_tensor("muT", [C_, ES], F32, kind="ExternalInput").ap()
    lng = nc.dram_tensor("lng", [C_, 1], F32, kind="ExternalInput").ap()
    lnb = nc.dram_tensor("lnb", [C_, 1], F32, kind="ExternalInput").ap()
    scl = nc.dram_tensor("scl", [1, 1], F32, kind="ExternalInput").ap()

    s1o = nc.dram_tensor("s1o", [C_, TPC], F32, kind="ExternalOutput").ap()
    edo = nc.dram_tensor("edo", [ES, TPC], F32, kind="ExternalOutput").ap()
    dso = nc.dram_tensor("dso", [ES, 1], F32, kind="ExternalOutput").ap()
    sino = nc.dram_tensor("sino", [ES, C_], F32, kind="ExternalOutput").ap()
    cso = nc.dram_tensor("cso", [1, TPC], F32, kind="ExternalOutput").ap()

    NT = TPC // 512

    with ExitStack() as _st, nc.Block() as blk:
        ygb = _st.enter_context(nc.sbuf_tensor("ygb", [128, 16 * TPC], BF16))
        xnf = _st.enter_context(nc.sbuf_tensor("xnf", [128, 8 * TPC], BF16))
        edb = _st.enter_context(nc.sbuf_tensor("edb", [128, 2 * TPC], BF16))
        edf = _st.enter_context(nc.sbuf_tensor("edf", [128, 2 * TPC], F32))
        mub = _st.enter_context(nc.sbuf_tensor("mub", [128, 8 * ES], BF16))
        w32 = _st.enter_context(nc.sbuf_tensor("w32", [128, 4 * 512], F32))
        wbf = _st.enter_context(nc.sbuf_tensor("wbf", [128, 16 * 512], BF16))
        stg = _st.enter_context(nc.sbuf_tensor("stg", [128, TPC + 1024], F32))
        zstg = _st.enter_context(nc.sbuf_tensor("zstg", [128, TPC], BF16))
        stgb = _st.enter_context(nc.sbuf_tensor("stgb", [128, 2 * 512], BF16))
        rows_ = _st.enter_context(nc.sbuf_tensor("rows", [1, 4 * TPC], F32))
        rowb = _st.enter_context(nc.sbuf_tensor("rowb", [1, TPC], BF16))
        cols_ = _st.enter_context(nc.sbuf_tensor("cols", [128, 8], F32))
        ones1 = _st.enter_context(nc.sbuf_tensor("ones1", [1, 128], F32))
        ones1b = _st.enter_context(nc.sbuf_tensor("ones1b", [1, 128], BF16))
        onescb = _st.enter_context(nc.sbuf_tensor("onescb", [128, 1], BF16))
        onesc = _st.enter_context(nc.sbuf_tensor("onesc", [128, 1], F32))
        identb = _st.enter_context(nc.sbuf_tensor("identb", [128, 128], BF16))
        cpool = _st.enter_context(nc.sbuf_tensor("cpool", [128, 8], F32))
        sca = _st.enter_context(nc.sbuf_tensor("sca", [1, 8], F32))
        psTb = _st.enter_context(nc.psum_tensor("psTb", [128, 128], BF16))
        _s_out = _st.enter_context(nc.semaphore("s_out"))
        xntok = ygb[:, 0:16 * C_]
        edtok = ygb[:, 16 * C_:16 * C_ + 16 * ES]

        CP = {LN_EPS: 0, 1e-24: 1, 1e-30: 2}

        def cbias(v, parts=1):
            return cpool[0:parts, CP[v]:CP[v] + 1]

        def sec(name, f):
            getattr(blk, name)(f)

        s_ch = Sem(_s_out)

        class _Rec:
            def __init__(self, e):
                self._e = e
                self.last = None

            def __getattr__(self, k):
                a = getattr(self._e, k)
                if callable(a):
                    def w(*args, **kw):
                        r = a(*args, **kw)
                        if hasattr(r, "then_inc"):
                            self.last = r
                        return r
                    return w
                return a

        def step(eng, fn):
            cur = s_ch.n

            def _f(e, fn=fn, cur=cur):
                e.wait_ge(_s_out, cur)
                rec = _Rec(e)
                fn(rec)
                (rec.last if rec.last is not None else e.nop()).then_inc(_s_out, 1)
            sec(eng, _f)
            s_ch.n += 1

        def dstep(eng, fn, n=1):
            cur = s_ch.n

            def _f(e, fn=fn, cur=cur):
                e.wait_ge(_s_out, cur)
                fn(e, lambda ins: ins.then_inc(_s_out, 16))
            sec(eng, _f)
            s_ch.n += 16 * n

        def row(i):
            return rows_[:, i * TPC:(i + 1) * TPC]

        def _c0(g):
            g.memset(ones1[:], 1.0)
            g.memset(ones1b[:], 1.0)
            g.memset(onesc[:], 1.0)
            g.memset(onescb[:], 1.0)
            g.memset(rows_[:], 0.0)
            g.memset(identb[:], 1.0)
            for _v, _i in CP.items():
                g.memset(cpool[:, _i:_i + 1], _v)
        step("gpsimd", _c0)
        step("gpsimd", lambda g: g.affine_select(
            out=identb[:], in_=identb[:], compare_op=ALU.is_equal, fill=0.0,
            base=0, pattern=[[-1, 128]], channel_multiplier=1))

        _cm1 = nc.psum_tensor("pst", [1, 2 * 512], F32)
        pst = _cm1.__enter__()
        if True:
            for d in range(16):
                def _l(s, mark, d=d):
                    mark(s.dma_start(out=zstg[:], in_=zT[d * 128:(d + 1) * 128, :]))
                    mark(s.dma_start(out=stg[:, 0:TPC],
                                     in_=ysT[d * 128:(d + 1) * 128, :]))
                dstep("sync", _l, 2)
                for j in range(NT):
                    S0 = TPC
                    step("scalar", lambda sc, j=j: sc.activation(
                        out=stg[:, S0:S0 + 512],
                        in_=zstg[:, j * 512:(j + 1) * 512], func=AF.Silu))
                    step("vector", lambda v, d=d, j=j: v.tensor_mul(
                        ygb[:, d * TPC + j * 512:d * TPC + (j + 1) * 512],
                        stg[:, j * 512:(j + 1) * 512], stg[:, S0:S0 + 512]))
                    step("vector", lambda v, d=d, j=j: v.tensor_mul(
                        stg[:, S0 + 512:S0 + 1024],
                        ygb[:, d * TPC + j * 512:d * TPC + (j + 1) * 512],
                        ygb[:, d * TPC + j * 512:d * TPC + (j + 1) * 512]))
                    step("tensor", lambda t, j=j: t.matmul(
                        pst[0:1, 0:512], onesc[:], stg[:, S0 + 512:S0 + 1024],
                        start=True, stop=True))
                    step("vector", lambda v, j=j: v.tensor_add(
                        row(0)[:, j * 512:(j + 1) * 512],
                        row(0)[:, j * 512:(j + 1) * 512], pst[0:1, 0:512]))
            step("scalar", lambda sc: sc.activation(out=row(1), in_=row(0),
                                                    func=AF.Ln, bias=cbias(LN_EPS),
                                                    scale=1.0 / DIN))
            step("scalar", lambda sc: sc.activation(out=row(1), in_=row(1),
                                                    func=AF.Exp, scale=-0.5))
        _cm1.__exit__(None, None, None)

        _cm2 = nc.psum_tensor("pbc", [128, 512], F32)
        pbc2 = _cm2.__enter__()
        if True:
            def _lnw(s, mark):
                mark(s.dma_start(out=cols_[:, 0:1], in_=normw[0:128, :]))
            for j in range(NT):
                step("tensor", lambda t, j=j: t.matmul(
                    pbc2[:], ones1[:], row(1)[:, j * 512:(j + 1) * 512],
                    start=True, stop=True))
                for d in range(16):
                    def _lw(s, mark, d=d):
                        mark(s.dma_start(out=cols_[:, 0:1],
                                         in_=normw[d * 128:(d + 1) * 128, :]))
                    dstep("sync", _lw, 1)

                    def _sc2(v, d=d, j=j):
                        sl = ygb[:, d * TPC + j * 512:d * TPC + (j + 1) * 512]
                        v.tensor_mul(stg[:, 0:512], sl, pbc2[:])
                    step("vector", _sc2)
                    step("vector", lambda v, d=d, j=j: v.tensor_scalar(
                        out=ygb[:, d * TPC + j * 512:d * TPC + (j + 1) * 512],
                        in0=stg[:, 0:512], scalar1=cols_[:, 0:1], scalar2=None,
                        op0=ALU.mult))
        _cm2.__exit__(None, None, None)

        _cm3 = nc.psum_tensor("pmm", [128, 4 * 512], F32)
        pmm = _cm3.__enter__()
        if True:
            for cb in range(2):
                for d4 in range(4):
                    def _lw(s, mark, cb=cb, d4=d4):
                        for i in range(4):
                            d = d4 * 4 + i
                            mark(s.dma_start(
                                out=w32[:, i * 512:(i + 1) * 512],
                                in_=woutT[d * 128:(d + 1) * 128,
                                          cb * 512:(cb + 1) * 512]))
                    dstep("sync", _lw, 4)

                    def _cw(v, d4=d4):
                        for i in range(4):
                            d = d4 * 4 + i
                            v.tensor_copy(wbf[:, d * 512:(d + 1) * 512],
                                          w32[:, i * 512:(i + 1) * 512])
                    step("vector", _cw)
                for c4 in range(4):
                    R0 = cb * 512 + c4 * 128
                    for j in range(NT):
                        ps = pmm[:, (j % 4) * 512:(j % 4 + 1) * 512]

                        def _mm(t, c4=c4, j=j, ps=ps):
                            for d in range(16):
                                t.matmul(ps, wbf[:, d * 512 + c4 * 128:
                                                 d * 512 + c4 * 128 + 128],
                                         ygb[:, d * TPC + j * 512:
                                             d * TPC + (j + 1) * 512],
                                         start=(d == 0), stop=(d == 15))
                        step("tensor", _mm)
                        step("vector", lambda v, ps=ps: v.tensor_copy(
                            stg[:, 0:512], ps))

                        def _os(g, mark, R0=R0, j=j):
                            mark(g.dma_start(out=s1o[R0:R0 + 128,
                                                     j * 512:(j + 1) * 512],
                                             in_=stg[:, 0:512]))
                        dstep("gpsimd", _os, 1)
        _cm3.__exit__(None, None, None)
        _cm4 = nc.psum_tensor("pm4", [128, 2 * 512], F32)
        pm4 = _cm4.__enter__()
        if True:
            def _lx(s, mark):
                for d in range(8):
                    mark(s.dma_start(out=xnf[:, d * TPC:(d + 1) * TPC],
                                     in_=xn1T[d * 128:(d + 1) * 128, :]))
            dstep("sync", _lx, 8)
            for d in range(8):
                for j in range(NT):
                    def _sq(v, d=d, j=j):
                        sl = xnf[:, d * TPC + j * 512:d * TPC + (j + 1) * 512]
                        v.tensor_mul(stgb[:, 0:512], sl, sl)
                    step("vector", _sq)

                    def _mm(t, d=d, j=j):
                        sl = xnf[:, d * TPC + j * 512:d * TPC + (j + 1) * 512]
                        t.matmul(pm4[0:1, 0:512], onescb[:], sl,
                                 start=True, stop=True)
                        t.matmul(pm4[0:1, 512:1024], onescb[:], stgb[:, 0:512],
                                 start=True, stop=True)
                    step("tensor", _mm)
                    step("vector", lambda v, j=j: v.tensor_add(
                        row(0)[:, j * 512:(j + 1) * 512],
                        row(0)[:, j * 512:(j + 1) * 512], pm4[0:1, 0:512]))
                    step("vector", lambda v, j=j: v.tensor_add(
                        row(1)[:, j * 512:(j + 1) * 512],
                        row(1)[:, j * 512:(j + 1) * 512], pm4[0:1, 512:1024]))
            step("vector", lambda v: v.tensor_scalar_mul(row(2), row(0), 1.0 / C_))
            step("vector", lambda v: v.tensor_mul(row(0), row(2), row(2)))
            step("vector", lambda v: v.tensor_scalar_mul(row(1), row(1), 1.0 / C_))
            step("vector", lambda v: v.tensor_sub(row(1), row(1), row(0)))
            step("scalar", lambda sc: sc.activation(out=row(1), in_=row(1),
                                                    func=AF.Ln, bias=cbias(LN_EPS)))
            step("scalar", lambda sc: sc.activation(out=row(1), in_=row(1),
                                                    func=AF.Exp, scale=-0.5))
            step("vector", lambda v: v.scalar_tensor_tensor(
                out=row(3), in0=row(2), scalar=-1.0, in1=row(1),
                op0=ALU.mult, op1=ALU.mult))
            for j in range(NT):
                def _bc(t, j=j):
                    t.matmul(pm4[:, 0:512], ones1[:],
                             row(1)[:, j * 512:(j + 1) * 512], start=True, stop=True)
                    t.matmul(pm4[:, 512:1024], ones1[:],
                             row(3)[:, j * 512:(j + 1) * 512], start=True, stop=True)
                step("tensor", _bc)
                for d in range(8):
                    def _lgb(s, mark, d=d):
                        mark(s.dma_start(out=cols_[:, 1:2],
                                         in_=lng[d * 128:(d + 1) * 128, :]))
                        mark(s.dma_start(out=cols_[:, 2:3],
                                         in_=lnb[d * 128:(d + 1) * 128, :]))
                    dstep("sync", _lgb, 2)
                    sl = lambda d=d, j=j: xnf[:, d * TPC + j * 512:
                                              d * TPC + (j + 1) * 512]
                    step("vector", lambda v, d=d, j=j: v.tensor_mul(
                        stg[:, 0:512],
                        xnf[:, d * TPC + j * 512:d * TPC + (j + 1) * 512],
                        pm4[:, 0:512]))
                    step("vector", lambda v: v.tensor_add(stg[:, 512:1024],
                                                          stg[:, 0:512],
                                                          pm4[:, 512:1024]))
                    step("vector", lambda v, d=d, j=j: v.tensor_scalar(
                        out=xnf[:, d * TPC + j * 512:d * TPC + (j + 1) * 512],
                        in0=stg[:, 512:1024], scalar1=cols_[:, 1:2],
                        scalar2=cols_[:, 2:3], op0=ALU.mult, op1=ALU.add))
            step("gpsimd", lambda g: g.memset(row(0), 0.0))
            for d in range(8):
                for j in range(NT):
                    def _sq(v, d=d, j=j):
                        sl = xnf[:, d * TPC + j * 512:d * TPC + (j + 1) * 512]
                        v.tensor_mul(stgb[:, 0:512], sl, sl)
                    step("vector", _sq)
                    step("tensor", lambda t: t.matmul(pm4[0:1, 0:512], onescb[:],
                                                      stgb[:, 0:512],
                                                      start=True, stop=True))
                    step("vector", lambda v, j=j: v.tensor_add(
                        row(0)[:, j * 512:(j + 1) * 512],
                        row(0)[:, j * 512:(j + 1) * 512], pm4[0:1, 0:512]))
            def _ls(s, mark):
                mark(s.dma_start(out=sca[:, 0:1], in_=scl[:]))
            dstep("sync", _ls, 1)
            step("scalar", lambda sc: sc.activation(out=row(0), in_=row(0),
                                                    func=AF.Ln, bias=cbias(1e-24)))
            step("scalar", lambda sc: sc.activation(out=row(0), in_=row(0),
                                                    func=AF.Exp, scale=-0.5))
            step("vector", lambda v: v.tensor_scalar(
                out=row(0), in0=row(0), scalar1=sca[:, 0:1], scalar2=None,
                op0=ALU.mult))
            step("vector", lambda v: v.tensor_copy(rowb[:], row(0)))
            step("scalar", lambda sc: sc.activation(out=sca[:, 1:2],
                                                    in_=sca[:, 0:1], func=AF.Abs))
            step("vector", lambda v: v.tensor_scalar_mul(sca[:, 2:3], sca[:, 1:2],
                                                         -1.0))
            step("vector", lambda v: v.tensor_copy(stgb[0:1, 0:1], sca[:, 2:3]))
            step("tensor", lambda t: t.matmul(pm4[:, 0:128][:, 0:1], ones1b[:],
                                              stgb[0:1, 0:1], start=True, stop=True))
            step("vector", lambda v: v.tensor_copy(cols_[:, 3:4], pm4[:, 0:1]))

            step("gpsimd", lambda g: g.memset(stg[0:1, 0:256], 0.0))
            for d in range(8):
                def _lmu(s, mark, d=d):
                    mark(s.dma_start(out=stg[:, 512:768],
                                     in_=muT[d * 128:(d + 1) * 128, :]))
                dstep("sync", _lmu, 1)
                step("vector", lambda v: v.tensor_mul(stg[:, 768:1024],
                                                      stg[:, 512:768],
                                                      stg[:, 512:768]))
                step("tensor", lambda t: t.matmul(pm4[0:1, 0:256], onesc[:],
                                                  stg[:, 768:1024],
                                                  start=True, stop=True))
                step("vector", lambda v: v.tensor_add(stg[0:1, 0:256],
                                                      stg[0:1, 0:256],
                                                      pm4[0:1, 0:256]))
                step("vector", lambda v, d=d: v.tensor_copy(
                    mub[:, d * ES:(d + 1) * ES], stg[:, 512:768]))
            step("scalar", lambda sc: sc.activation(out=stg[0:1, 0:256],
                                                    in_=stg[0:1, 0:256],
                                                    func=AF.Ln, bias=cbias(1e-24)))
            step("scalar", lambda sc: sc.activation(out=stg[0:1, 0:256],
                                                    in_=stg[0:1, 0:256],
                                                    func=AF.Exp, scale=-0.5))
            step("vector", lambda v: v.tensor_copy(stgb[0:1, 0:256],
                                                   stg[0:1, 0:256]))
            step("tensor", lambda t: t.matmul(pm4[:, 0:256], ones1b[:],
                                              stgb[0:1, 0:256],
                                              start=True, stop=True))
            for d in range(8):
                step("vector", lambda v, d=d: v.tensor_mul(
                    mub[:, d * ES:(d + 1) * ES], mub[:, d * ES:(d + 1) * ES],
                    pm4[:, 0:256]))

            for ec in range(2):
                for j in range(NT):
                    def _mm(t, ec=ec, j=j):
                        for d in range(8):
                            t.matmul(pm4[:, 0:512],
                                     mub[:, d * ES + ec * 128:d * ES + ec * 128 + 128],
                                     xnf[:, d * TPC + j * 512:d * TPC + (j + 1) * 512],
                                     start=(d == 0), stop=(d == 7))
                    step("tensor", _mm)
                    step("tensor", lambda t, j=j: t.matmul(
                        pm4[:, 512:1024], ones1b[:],
                        rowb[:, j * 512:(j + 1) * 512], start=True, stop=True))
                    step("vector", lambda v: v.tensor_copy(stg[:, 1024:1536],
                                                           pm4[:, 0:512]))
                    step("vector", lambda v: v.tensor_mul(stg[:, 0:512],
                                                          stg[:, 1024:1536],
                                                          pm4[:, 512:1024]))
                    step("scalar", lambda sc: sc.activation(out=stg[:, 512:1024],
                                                            in_=stg[:, 0:512],
                                                            func=AF.Exp,
                                                            bias=cols_[:, 3:4]))
                    step("vector", lambda v, ec=ec, j=j: v.tensor_copy(
                        edf[:, ec * TPC + j * 512:ec * TPC + (j + 1) * 512],
                        stg[:, 512:1024]))
                    step("vector", lambda v, ec=ec, j=j: v.tensor_copy(
                        edb[:, ec * TPC + j * 512:ec * TPC + (j + 1) * 512],
                        stg[:, 512:1024]))

            for ec in range(2):
                def _oe(g, mark, ec=ec):
                    mark(g.dma_start(out=edo[ec * 128:(ec + 1) * 128, :],
                                     in_=edf[:, ec * TPC:(ec + 1) * TPC]))
                dstep("gpsimd", _oe, 1)
                step("vector", lambda v, ec=ec: v.reduce_sum(
                    cols_[:, 4:5], edf[:, ec * TPC:(ec + 1) * TPC], axis=AX))

                def _od(g, mark, ec=ec):
                    mark(g.dma_start(out=dso[ec * 128:(ec + 1) * 128, :],
                                     in_=cols_[:, 4:5]))
                dstep("gpsimd", _od, 1)
            for j in range(NT):
                def _cs(t, j=j):
                    t.matmul(pm4[0:1, 0:512], onescb[:],
                             edb[:, 0 * TPC + j * 512:0 * TPC + (j + 1) * 512],
                             start=True, stop=False)
                    t.matmul(pm4[0:1, 0:512], onescb[:],
                             edb[:, 1 * TPC + j * 512:1 * TPC + (j + 1) * 512],
                             start=False, stop=True)
                step("tensor", _cs)
                step("vector", lambda v, j=j: v.tensor_copy(
                    row(0)[:, j * 512:(j + 1) * 512], pm4[0:1, 0:512]))

            def _oc(g, mark):
                mark(g.dma_start(out=cso[:], in_=row(0)))
            dstep("gpsimd", _oc, 1)

            for ec in range(2):
                for tc in range(16):
                    def _tr(t, ec=ec, tc=tc):
                        t.transpose(psTb[:],
                                    edb[:, ec * TPC + tc * 128:
                                        ec * TPC + (tc + 1) * 128], identb[:])
                    step("tensor", _tr)
                    step("vector", lambda v, ec=ec, tc=tc: v.tensor_copy(
                        edtok[:, tc * ES + ec * 128:tc * ES + (ec + 1) * 128],
                        psTb[:]))
            for d in range(8):
                for tc in range(16):
                    def _tr(t, d=d, tc=tc):
                        t.transpose(psTb[:],
                                    xnf[:, d * TPC + tc * 128:
                                        d * TPC + (tc + 1) * 128], identb[:])
                    step("tensor", _tr)
                    step("vector", lambda v, d=d, tc=tc: v.tensor_copy(
                        xntok[:, tc * C_ + d * 128:tc * C_ + (d + 1) * 128],
                        psTb[:]))
            for ec in range(2):
                for dj in range(2):
                    def _mm(t, ec=ec, dj=dj):
                        for tc in range(16):
                            t.matmul(pm4[:, 0:512],
                                     edtok[:, tc * ES + ec * 128:
                                           tc * ES + (ec + 1) * 128],
                                     xntok[:, tc * C_ + dj * 512:
                                           tc * C_ + (dj + 1) * 512],
                                     start=(tc == 0), stop=(tc == 15))
                    step("tensor", _mm)
                    step("vector", lambda v: v.tensor_copy(stg[:, 0:512],
                                                           pm4[:, 0:512]))

                    def _osi(g, mark, ec=ec, dj=dj):
                        mark(g.dma_start(out=sino[ec * 128:(ec + 1) * 128,
                                                  dj * 512:(dj + 1) * 512],
                                         in_=stg[:, 0:512]))
                    dstep("gpsimd", _osi, 1)
        _cm4.__exit__(None, None, None)

        fin = s_ch.n

        def _fin(g, fin=fin):
            g.wait_ge(_s_out, fin)
        sec("gpsimd", _fin)
    return nc


def build_F():
    nc = bass.Bass("TRN2", target_bir_lowering=False, debug=False)
    sinp = nc.dram_tensor("sinp", [4, 64, C_], F32, kind="ExternalInput").ap()
    dsp = nc.dram_tensor("dsp", [4, 64, 1], F32, kind="ExternalInput").ap()
    w1e = nc.dram_tensor("w1e", [4, C_, 256], F32, kind="ExternalInput").ap()
    b1e = nc.dram_tensor("b1e", [4, 256, 1], F32, kind="ExternalInput").ap()
    w2e = nc.dram_tensor("w2e", [4, 256, C_], F32, kind="ExternalInput").ap()
    b2e = nc.dram_tensor("b2e", [4, C_, 1], F32, kind="ExternalInput").ap()
    souto = nc.dram_tensor("souto", [C_, 64], F32, kind="ExternalOutput").ap()

    with ExitStack() as _st, nc.Block() as blk:
        sin = _st.enter_context(nc.sbuf_tensor("sin", [128, 2 * C_], F32))
        gsc = _st.enter_context(nc.sbuf_tensor("gsc", [128, 4], F32))
        sintb = _st.enter_context(nc.sbuf_tensor("sintb", [128, 8 * 64], BF16))
        hten = _st.enter_context(nc.sbuf_tensor("hten", [128, 32], BF16))
        wstg = _st.enter_context(nc.sbuf_tensor("wstg", [128, 2 * 512], F32))
        wb = _st.enter_context(nc.sbuf_tensor("wb", [128, 16 * 256], BF16))
        stg = _st.enter_context(nc.sbuf_tensor("stg", [128, 512], F32))
        bcol = _st.enter_context(nc.sbuf_tensor("bcol", [128, 2], F32))
        identb = _st.enter_context(nc.sbuf_tensor("identb", [128, 128], BF16))
        psT = _st.enter_context(nc.psum_tensor("psT", [128, 128], BF16))
        psM = _st.enter_context(nc.psum_tensor("psM", [128, 2 * 64], F32))
        _s_out = _st.enter_context(nc.semaphore("s_out"))

        def sec(name, f):
            getattr(blk, name)(f)

        s_ch = Sem(_s_out)

        class _Rec:
            def __init__(self, e):
                self._e = e
                self.last = None

            def __getattr__(self, k):
                a = getattr(self._e, k)
                if callable(a):
                    def w(*args, **kw):
                        r = a(*args, **kw)
                        if hasattr(r, "then_inc"):
                            self.last = r
                        return r
                    return w
                return a

        def step(eng, fn):
            cur = s_ch.n

            def _f(e, fn=fn, cur=cur):
                e.wait_ge(_s_out, cur)
                rec = _Rec(e)
                fn(rec)
                (rec.last if rec.last is not None else e.nop()).then_inc(_s_out, 1)
            sec(eng, _f)
            s_ch.n += 1

        def dstep(eng, fn, n=1):
            cur = s_ch.n

            def _f(e, fn=fn, cur=cur):
                e.wait_ge(_s_out, cur)
                fn(e, lambda ins: ins.then_inc(_s_out, 16))
            sec(eng, _f)
            s_ch.n += 16 * n

        step("gpsimd", lambda g: g.memset(identb[:], 1.0))
        step("gpsimd", lambda g: g.affine_select(
            out=identb[:], in_=identb[:], compare_op=ALU.is_equal, fill=0.0,
            base=0, pattern=[[-1, 128]], channel_multiplier=1))

        for p in range(4):
            def _l(s, mark, p=p):
                mark(s.dma_start(out=wstg[0:64, 0:1024], in_=sinp[p]))
                mark(s.dma_start(out=bcol[0:64, 1:2], in_=dsp[p]))
            dstep("sync", _l, 2)
            if p == 0:
                step("vector", lambda v: v.tensor_copy(sin[0:64, 0:C_],
                                                       wstg[0:64, 0:1024]))
                step("vector", lambda v: v.tensor_copy(bcol[0:64, 0:1],
                                                       bcol[0:64, 1:2]))
            else:
                step("vector", lambda v: v.tensor_add(sin[0:64, 0:C_],
                                                      sin[0:64, 0:C_],
                                                      wstg[0:64, 0:1024]))
                step("vector", lambda v: v.tensor_add(bcol[0:64, 0:1],
                                                      bcol[0:64, 0:1],
                                                      bcol[0:64, 1:2]))
        step("scalar", lambda sc: sc.activation(out=bcol[0:64, 0:1],
                                                in_=bcol[0:64, 0:1], func=AF.Ln))
        step("scalar", lambda sc: sc.activation(out=bcol[0:64, 0:1],
                                                in_=bcol[0:64, 0:1],
                                                func=AF.Exp, scale=-1.0))
        step("vector", lambda v: v.tensor_scalar(
            out=sin[0:64, C_:2 * C_], in0=sin[0:64, 0:C_],
            scalar1=bcol[0:64, 0:1], scalar2=None, op0=ALU.mult))
        step("vector", lambda v: v.tensor_copy(wb[0:64, 0:C_],
                                               sin[0:64, C_:2 * C_]))
        for dc in range(8):
            step("tensor", lambda t, dc=dc: t.transpose(
                psT[:, 0:64], wb[0:64, dc * 128:(dc + 1) * 128],
                identb[0:64, 0:64]))
            step("vector", lambda v, dc=dc: v.tensor_copy(
                sintb[:, dc * 64:(dc + 1) * 64], psT[:, 0:64]))

        for e in range(4):
            for hc in range(2):
                def _lb1(s, mark, e=e, hc=hc):
                    mark(s.dma_start(out=bcol[:, 0:1],
                                     in_=b1e[e, hc * 128:(hc + 1) * 128, :]))
                dstep("sync", _lb1, 1)
                for d in range(8):
                    def _lw(s, mark, e=e, hc=hc, d=d):
                        mark(s.dma_start(out=wstg[:, 0:128],
                                         in_=w1e[e, d * 128:(d + 1) * 128,
                                                 hc * 128:(hc + 1) * 128]))
                    dstep("sync", _lw, 1)
                    step("vector", lambda v, d=d: v.tensor_copy(
                        wb[:, d * 128:(d + 1) * 128], wstg[:, 0:128]))

                def _mm(t, e=e, hc=hc):
                    for d in range(8):
                        t.matmul(psM[:, 0:16], wb[:, d * 128:(d + 1) * 128],
                                 sintb[:, d * 64 + e * 16:d * 64 + (e + 1) * 16],
                                 start=(d == 0), stop=(d == 7))
                step("tensor", _mm)
                step("scalar", lambda sc, hc=hc: sc.activation(
                    out=hten[:, hc * 16:(hc + 1) * 16], in_=psM[:, 0:16],
                    func=AF.Gelu, bias=bcol[:, 0:1]))
            for dc in range(8):
                def _lb2(s, mark, e=e, dc=dc):
                    mark(s.dma_start(out=bcol[:, 0:1],
                                     in_=b2e[e, dc * 128:(dc + 1) * 128, :]))
                dstep("sync", _lb2, 1)
                for hc in range(2):
                    def _lw(s, mark, e=e, dc=dc, hc=hc):
                        mark(s.dma_start(out=wstg[:, 0:128],
                                         in_=w2e[e, hc * 128:(hc + 1) * 128,
                                                 dc * 128:(dc + 1) * 128]))
                    dstep("sync", _lw, 1)
                    step("vector", lambda v, hc=hc: v.tensor_copy(
                        wb[:, hc * 128:(hc + 1) * 128], wstg[:, 0:128]))

                def _mm(t, e=e, dc=dc):
                    for hc in range(2):
                        t.matmul(psM[:, 64:80], wb[:, hc * 128:(hc + 1) * 128],
                                 hten[:, hc * 16:(hc + 1) * 16],
                                 start=(hc == 0), stop=(hc == 1))
                step("tensor", _mm)
                step("vector", lambda v: v.tensor_scalar_add(
                    stg[:, 0:16], psM[:, 64:80], bcol[:, 0:1]))

                def _oso(g, mark, e=e, dc=dc):
                    mark(g.dma_start(out=souto[dc * 128:(dc + 1) * 128,
                                               e * 16:(e + 1) * 16],
                                     in_=stg[:, 0:16]))
                dstep("gpsimd", _oso, 1)

        fin = s_ch.n

        def _fin(g, fin=fin):
            g.wait_ge(_s_out, fin)
        sec("gpsimd", _fin)
    return nc


def build_G():
    nc = bass.Bass("TRN2", target_bir_lowering=False, debug=False)
    soutT = nc.dram_tensor("soutT", [ES, C_], F32, kind="ExternalInput").ap()
    edi = nc.dram_tensor("edi", [ES, TPC], F32, kind="ExternalInput").ap()
    csi = nc.dram_tensor("csi", [1, TPC], F32, kind="ExternalInput").ap()
    s1i = nc.dram_tensor("s1i", [C_, TPC], F32, kind="ExternalInput").ap()
    x1i = nc.dram_tensor("x1i", [C_, TPC], F32, kind="ExternalInput").ap()
    outo = nc.dram_tensor("outo", [C_, TPC], F32, kind="ExternalOutput").ap()
    NT = TPC // 512

    with ExitStack() as _st, nc.Block() as blk:
        edb = _st.enter_context(nc.sbuf_tensor("edb", [128, 2 * TPC], BF16))
        sob = _st.enter_context(nc.sbuf_tensor("sob", [128, 2 * C_], BF16))
        rcs = _st.enter_context(nc.sbuf_tensor("rcs", [1, TPC], F32))
        rcb = _st.enter_context(nc.sbuf_tensor("rcb", [1, TPC], BF16))
        stg = _st.enter_context(nc.sbuf_tensor("stg", [128, 6 * 512], F32))
        ones1b = _st.enter_context(nc.sbuf_tensor("ones1b", [1, 128], BF16))
        psM = _st.enter_context(nc.psum_tensor("psM", [128, 2 * 512], F32))
        _s_out = _st.enter_context(nc.semaphore("s_out"))

        def sec(name, f):
            getattr(blk, name)(f)

        s_ch = Sem(_s_out)

        class _Rec:
            def __init__(self, e):
                self._e = e
                self.last = None

            def __getattr__(self, k):
                a = getattr(self._e, k)
                if callable(a):
                    def w(*args, **kw):
                        r = a(*args, **kw)
                        if hasattr(r, "then_inc"):
                            self.last = r
                        return r
                    return w
                return a

        def step(eng, fn):
            cur = s_ch.n

            def _f(e, fn=fn, cur=cur):
                e.wait_ge(_s_out, cur)
                rec = _Rec(e)
                fn(rec)
                (rec.last if rec.last is not None else e.nop()).then_inc(_s_out, 1)
            sec(eng, _f)
            s_ch.n += 1

        def dstep(eng, fn, n=1):
            cur = s_ch.n

            def _f(e, fn=fn, cur=cur):
                e.wait_ge(_s_out, cur)
                fn(e, lambda ins: ins.then_inc(_s_out, 16))
            sec(eng, _f)
            s_ch.n += 16 * n

        step("gpsimd", lambda g: g.memset(ones1b[:], 1.0))

        def _l0(s, mark):
            for ec in range(2):
                mark(s.dma_start(out=stg[:, 0:TPC],
                                 in_=edi[ec * 128:(ec + 1) * 128, :]))
            mark(s.dma_start(out=rcs[:], in_=csi[:]))
        for ec in range(2):
            def _le(s, mark, ec=ec):
                mark(s.dma_start(out=stg[:, 0:TPC],
                                 in_=edi[ec * 128:(ec + 1) * 128, :]))
            dstep("sync", _le, 1)
            step("vector", lambda v, ec=ec: v.tensor_copy(
                edb[:, ec * TPC:(ec + 1) * TPC], stg[:, 0:TPC]))
        for ec in range(2):
            def _ls(s, mark, ec=ec):
                mark(s.dma_start(out=stg[:, 0:C_],
                                 in_=soutT[ec * 128:(ec + 1) * 128, :]))
            dstep("sync", _ls, 1)
            step("vector", lambda v, ec=ec: v.tensor_copy(
                sob[:, ec * C_:(ec + 1) * C_], stg[:, 0:C_]))

        def _lc(s, mark):
            mark(s.dma_start(out=rcs[:], in_=csi[:]))
        dstep("sync", _lc, 1)
        step("scalar", lambda sc: sc.activation(out=rcs[:], in_=rcs[:], func=AF.Ln))
        step("scalar", lambda sc: sc.activation(out=rcs[:], in_=rcs[:],
                                                func=AF.Exp, scale=-1.0))
        step("vector", lambda v: v.tensor_copy(rcb[:], rcs[:]))

        for dc in range(8):
            for j in range(NT):
                def _mm(t, dc=dc, j=j):
                    for ec in range(2):
                        t.matmul(psM[:, 0:512],
                                 sob[:, ec * C_ + dc * 128:ec * C_ + (dc + 1) * 128],
                                 edb[:, ec * TPC + j * 512:ec * TPC + (j + 1) * 512],
                                 start=(ec == 0), stop=(ec == 1))
                    t.matmul(psM[:, 512:1024], ones1b[:],
                             rcb[:, j * 512:(j + 1) * 512], start=True, stop=True,
                             skip_group_check=True)
                step("tensor", _mm)

                def _lsx(s, mark, dc=dc, j=j):
                    mark(s.dma_start(out=stg[:, 512:1024],
                                     in_=s1i[dc * 128:(dc + 1) * 128,
                                             j * 512:(j + 1) * 512]))
                    mark(s.dma_start(out=stg[:, 1024:1536],
                                     in_=x1i[dc * 128:(dc + 1) * 128,
                                             j * 512:(j + 1) * 512]))
                dstep("sync", _lsx, 2)
                step("vector", lambda v: v.tensor_copy(stg[:, 2048:2560],
                                                       psM[:, 0:512]))
                step("vector", lambda v: v.tensor_mul(stg[:, 0:512],
                                                      stg[:, 2048:2560],
                                                      psM[:, 512:1024]))
                step("scalar", lambda sc: sc.activation(out=stg[:, 1536:2048],
                                                        in_=stg[:, 0:512],
                                                        func=AF.Silu))
                step("vector", lambda v: v.tensor_mul(stg[:, 0:512],
                                                      stg[:, 512:1024],
                                                      stg[:, 1536:2048]))
                step("vector", lambda v: v.tensor_add(stg[:, 512:1024],
                                                      stg[:, 1024:1536],
                                                      stg[:, 0:512]))

                def _oo(g, mark, dc=dc, j=j):
                    mark(g.dma_start(out=outo[dc * 128:(dc + 1) * 128,
                                              j * 512:(j + 1) * 512],
                                     in_=stg[:, 512:1024]))
                dstep("gpsimd", _oo, 1)

        fin = s_ch.n

        def _fin(g, fin=fin):
            g.wait_ge(_s_out, fin)
        sec("gpsimd", _fin)
    return nc


def host_prep_E(inputs, C_out, D_out):
    maps = []
    for c in range(NCORE):
        b, t0 = c // 4, (c % 4) * TPC
        cores = range(b * 4, b * 4 + 4)
        ys = np.concatenate([D_out[b * 4 + i]["yssd"].reshape(512, T_)
                             for i in range(4)], axis=0)
        m = {
            "ysT": np.ascontiguousarray(ys[:, t0:t0 + TPC]),
            "zT": C_out[c]["zo"],
            "xn1T": C_out[c]["xn1o"],
            "woutT": np.ascontiguousarray(inputs["L1_attn_W_out"].T),
            "normw": inputs["L1_attn_norm_w"].reshape(DIN, 1),
            "muT": np.ascontiguousarray(inputs["L1_ffn_mu"].reshape(C_, ES)),
            "lng": inputs["L1_ffn_ln_gamma"].reshape(C_, 1),
            "lnb": inputs["L1_ffn_ln_beta"].reshape(C_, 1),
            "scl": inputs["L1_ffn_scale"].reshape(1, 1),
        }
        maps.append(m)
    return maps


def host_prep_F(inputs, E_out):
    w1 = inputs["L1_ffn_w1"]
    b1 = inputs["L1_ffn_b1"]
    w2 = inputs["L1_ffn_w2"]
    b2 = inputs["L1_ffn_b2"]
    maps = []
    for c in range(NCORE):
        b, e0 = c // 4, 4 * (c % 4)
        cores = range(b * 4, b * 4 + 4)
        r0 = e0 * 16
        m = {
            "sinp": np.stack([E_out[cc]["sino"][r0:r0 + 64] for cc in cores]),
            "dsp": np.stack([E_out[cc]["dso"][r0:r0 + 64] for cc in cores]),
            "w1e": np.ascontiguousarray(w1[e0:e0 + 4]),
            "b1e": np.ascontiguousarray(b1[e0:e0 + 4].reshape(4, 256, 1)),
            "w2e": np.ascontiguousarray(w2[e0:e0 + 4]),
            "b2e": np.ascontiguousarray(b2[e0:e0 + 4].reshape(4, C_, 1)),
        }
        maps.append(m)
    return maps


def host_prep_G(F_out, E_out, C_out):
    sout = {}
    for b in range(2):
        s = np.concatenate([F_out[b * 4 + i]["souto"] for i in range(4)],
                           axis=1)
        sout[b] = np.ascontiguousarray(s.T)
    maps = []
    for c in range(NCORE):
        b = c // 4
        m = {
            "soutT": sout[b],
            "edi": E_out[c]["edo"],
            "csi": E_out[c]["cso"],
            "s1i": E_out[c]["s1o"],
            "x1i": np.ascontiguousarray(C_out[c]["x1o"][:, 512:]),
        }
        maps.append(m)
    return maps


def kernel(**inputs):
    core_ids = list(range(NCORE))
    nA = build_A()
    A_out = run_bass_kernel_spmd(nA, host_prep_A(inputs), core_ids=core_ids).results
    nB = build_B()
    B_out = run_bass_kernel_spmd(nB, host_prep_B(inputs, A_out),
                                 core_ids=core_ids).results
    y = reassemble_B(B_out)
    nC = build_C()
    C_out = run_bass_kernel_spmd(nC, host_prep_C(inputs, A_out, y),
                                 core_ids=core_ids).results
    nD = build_D()
    D_out = run_bass_kernel_spmd(nD, host_prep_D(inputs, C_out),
                                 core_ids=core_ids).results
    nE = build_E()
    E_out = run_bass_kernel_spmd(nE, host_prep_E(inputs, C_out, D_out),
                                 core_ids=core_ids).results
    nF = build_F()
    F_out = run_bass_kernel_spmd(nF, host_prep_F(inputs, E_out),
                                 core_ids=core_ids).results
    nG = build_G()
    G_out = run_bass_kernel_spmd(nG, host_prep_G(F_out, E_out, C_out),
                                 core_ids=core_ids).results
    out = np.empty((B_, T_, C_), np.float32)
    for c in range(NCORE):
        b, t0 = c // 4, (c % 4) * TPC
        out[b, t0:t0 + TPC] = G_out[c]["outo"].T
    return out
